# version 25
# baseline (speedup 1.0000x reference)
import numpy as np
from contextlib import ExitStack

import concourse.bass as bass
import concourse.mybir as mybir
from concourse.bass_utils import run_bass_kernel_spmd

F32 = mybir.dt.float32
BF16 = mybir.dt.bfloat16
AF = mybir.ActivationFunctionType
ALU = mybir.AluOpType
AX = mybir.AxisListType

B, T, DIM, H = 16, 2048, 1024, 16
HD = DIM // H
FFI = 2816
NCORES = 8
NB = B // NCORES
TOK = NB * T
NT128 = TOK // 128
NCH = T // 128
NFC = DIM // 128
NIC = FFI // 128
LN_EPS = 1e-5

FP4_MID = [0.25, 0.75, 1.25, 1.75, 2.5, 3.5, 5.0]
FP4_W = [0.5, 0.5, 0.5, 0.5, 1.0, 1.0, 2.0]

SAME_ENGINE_WAITS = False
DETECT_RACES = False


class Res:
    __slots__ = ("name", "writer", "readers", "preds")

    def __init__(self, name=""):
        self.name = name
        self.writer = None
        self.readers = []
        self.preds = []


class Op:
    __slots__ = ("eng", "fn", "ticket", "deps", "need_inc", "tag")

    def __init__(self, eng, fn, tag):
        self.eng = eng
        self.fn = fn
        self.ticket = None
        self.deps = []
        self.need_inc = False
        self.tag = tag


class Prog:
    NLANES = 8

    def __init__(self):
        self.ops = {k: [] for k in ("T", "V", "S", "G", "Y")}
        self.dma_count = {"DY": 0, "DG": 0}
        self.lane_last = {}
        self.lane_uses = {}
        self.n_ops = 0

    def emit(self, eng, fn, reads=(), writes=(), tag=""):
        op = Op(eng, fn, tag)
        seen = set()

        def dep(x):
            if x is not None and id(x) not in seen and x is not op:
                seen.add(id(x))
                op.deps.append(x)

        for r in reads:
            dep(r.writer)
        for r in writes:
            dep(r.writer)
            for rd in r.readers:
                dep(rd)
            for p in r.preds:
                dep(p)
        issue_eng = {"DY": "Y", "DG": "G"}.get(eng, eng)
        if eng in ("DY", "DG"):
            lane = self.dma_count[eng] % self.NLANES
            self.dma_count[eng] += 1
            key = (eng, lane)
            dep(self.lane_last.get(key))
            self.lane_last[key] = op
            self.lane_uses[key] = self.lane_uses.get(key, 0) + 1
            op.ticket = (f"{eng}{lane}", 16 * self.lane_uses[key])
        self.ops[issue_eng].append(op)
        self.n_ops += 1
        for r in reads:
            r.readers.append(op)
        for r in writes:
            r.writer = op
            r.readers = []
            r.preds = []
        return op

    def T(self, fn, reads=(), writes=(), tag=""):
        return self.emit("T", fn, reads, writes, tag)

    def V(self, fn, reads=(), writes=(), tag=""):
        return self.emit("V", fn, reads, writes, tag)

    def S(self, fn, reads=(), writes=(), tag=""):
        return self.emit("S", fn, reads, writes, tag)

    def G(self, fn, reads=(), writes=(), tag=""):
        return self.emit("G", fn, reads, writes, tag)

    def load(self, dst_ap, src_ap, reads=(), writes=(), tag="ld"):
        return self.emit("DY", lambda e: e.dma_start(dst_ap, src_ap),
                         reads, writes, tag)

    def store(self, dst_ap, src_ap, reads=(), writes=(), tag="st"):
        return self.emit("DG", lambda e: e.dma_start(dst_ap, src_ap),
                         reads, writes, tag)

    COST = {"T": 0.25, "V": 0.9, "S": 0.9, "G": 0.7, "Y": 0.1}

    def schedule(self):
        import heapq
        eng_of = {}
        idx_of = {}
        nsucc_sched = {}
        for eng, lst in self.ops.items():
            for i, op in enumerate(lst):
                eng_of[id(op)] = eng
                idx_of[id(op)] = i
        npred = {}
        succs = {}
        for eng, lst in self.ops.items():
            for op in lst:
                npred[id(op)] = len(op.deps)
                for dp in op.deps:
                    succs.setdefault(id(dp), []).append(op)
        finish = {}
        clock = {e: 0.0 for e in self.ops}
        ready = {e: [] for e in self.ops}
        new_order = {e: [] for e in self.ops}
        n_total = sum(len(v) for v in self.ops.values())

        cp = {}
        order_all = []
        for eng, lst in self.ops.items():
            order_all.extend(lst)
        indeg = {id(op): len(succs.get(id(op), ())) for op in order_all}
        stackq = [op for op in order_all if indeg[id(op)] == 0]
        while stackq:
            op = stackq.pop()
            e = eng_of[id(op)]
            dma = op.ticket is not None and op.ticket[0][0] == "D"
            dur = 2.5 if dma else self.op_cost(e, op) \
                if hasattr(self, "op_cost") else self.COST[e]
            cp[id(op)] = dur + max((cp[id(su)] for su in succs.get(id(op), ())),
                                   default=0.0)
            for dp in op.deps:
                indeg[id(dp)] -= 1
                if indeg[id(dp)] == 0:
                    stackq.append(dp)

        def push_ready(op):
            av = max((finish[id(dp)] for dp in op.deps), default=0.0)
            e = eng_of[id(op)]
            heapq.heappush(ready[e], (av, -cp.get(id(op), 0.0),
                                      idx_of[id(op)], id(op), op))

        for eng, lst in self.ops.items():
            for op in lst:
                if npred[id(op)] == 0:
                    push_ready(op)
        done = 0
        while done < n_total:
            best, bh = None, None
            for e, h in ready.items():
                if not h:
                    continue
                av = h[0][0]
                start = max(clock[e], av)
                key = (start, av)
                if best is None or key < bh:
                    best, bh = e, key
            assert best is not None, "scheduler deadlock (cycle in deps)"
            av, _, _, _, op = heapq.heappop(ready[best])
            e = best
            dma = op.ticket is not None and op.ticket[0][0] == "D"
            dur = 2.5 if dma else self.COST[e]
            start = max(clock[e], av)
            fin = start + dur
            clock[e] = start + (0.15 if dma else dur)
            finish[id(op)] = fin
            new_order[e].append(op)
            done += 1
            for su in succs.get(id(op), ()):
                npred[id(su)] -= 1
                if npred[id(su)] == 0:
                    push_ready(su)
        self.ops = new_order

    def lower(self, nc, ec):
        self.schedule()
        for eng in self.ops:
            for op in self.ops[eng]:
                for d in op.deps:
                    d.need_inc = True
        for eng in ("T", "V", "S", "G"):
            c = 0
            for op in self.ops[eng]:
                if op.ticket is not None:
                    continue
                if op.need_inc:
                    c += 1
                    op.ticket = (eng, c)

        sem_names = ["T", "V", "S", "G"]
        for q in ("DY", "DG"):
            for ln in range(self.NLANES):
                sem_names.append(f"{q}{ln}")
        sems = {k: ec(nc.semaphore("sem" + k)) for k in sem_names}

        def runner(key):
            def run(h):
                waited = {}
                for op in self.ops[key]:
                    for d in op.deps:
                        if d.ticket is None:
                            continue
                        sname, val = d.ticket
                        if sname == key and not SAME_ENGINE_WAITS:
                            continue
                        if waited.get(sname, -1) >= val:
                            continue
                        h.wait_ge(sems[sname], val)
                        waited[sname] = val
                    inst = op.fn(h)
                    if op.ticket is not None:
                        sname, val = op.ticket
                        inst.then_inc(sems[sname],
                                      16 if sname[0] == "D" else 1)
                q = {"G": "DG", "Y": "DY"}.get(key)
                if q:
                    for ln in range(self.NLANES):
                        u = self.lane_uses.get((q, ln), 0)
                        if u:
                            h.wait_ge(sems[f"{q}{ln}"], 16 * u)
            return run

        block = ec(nc.Block())
        block.tensor(runner("T"))
        block.vector(runner("V"))
        block.scalar(runner("S"))
        block.gpsimd(runner("G"))
        block.sync(runner("Y"))


class Arena:

    def __init__(self, nc, name, per_part_elems, dtype):
        self.t = nc.alloc_sbuf_tensor(name, [128, per_part_elems], dtype)
        self.cap = per_part_elems
        self.cursor = 0
        self.live = []
        self.dead = []
        self.name_start = {}

    def alloc(self, name, shape):
        n = int(np.prod(shape))
        ivs = sorted((s, e) for (s, e, _) in self.live)
        start = None
        pos = 0
        for (s0, e0) in ivs:
            cand = (pos + 31) & ~31
            if cand + n <= s0:
                start = cand
                break
            pos = max(pos, e0)
        if start is None:
            cand = (pos + 31) & ~31
            assert cand + n <= self.cap, \
                f"arena {self.t.name} overflow: {name}: {cand + n} > {self.cap}"
            start = cand
        end = start + n
        prev = self.name_start.setdefault(name, start)
        assert prev == start, \
            f"region {name} realloc moved: {prev} -> {start} (closure hazard)"
        reg = Region(self, name, start, list(shape))
        for (s0, e0, r0) in self.dead:
            if s0 < end and start < e0:
                if r0.res.writer is not None:
                    reg.res.preds.append(r0.res.writer)
                reg.res.preds.extend(r0.res.readers)
        self.live.append((start, end, reg))
        return reg

    def free(self, *regs):
        ids = {id(r) for r in regs}
        keep = []
        for (s, e, r) in self.live:
            (self.dead if id(r) in ids else keep).append((s, e, r))
        self.live = keep


class Region:

    def __init__(self, arena, name, start, shape):
        self.arena = arena
        self.name = name
        self.start = start
        self.shape = shape
        self.res = Res(name)
        st = []
        acc = 1
        for s in reversed(shape):
            st.append(acc)
            acc *= s
        self.strides = list(reversed(st))

    def a(self, *idx, p0=0, pn=128):
        pat = [[self.arena.cap, 128]]
        base = 0
        for d, s in enumerate(self.shape):
            sel = idx[d] if d < len(idx) else None
            if sel is None:
                pat.append([self.strides[d], s])
            elif isinstance(sel, int):
                base += sel * self.strides[d]
            else:
                st, cnt = sel
                base += st * self.strides[d]
                pat.append([self.strides[d], cnt])
        ap = bass.AP(self.arena.t, self.start + base, pat)
        if p0 or pn != 128:
            ap = ap[p0:p0 + pn]
        return ap


def _quant_emit(p, nc, ctx, xf32_ap, out_bf_ap, nblk, scr, tag=""):
    N = nblk * 16
    bm, s, r, sbf = scr["bm"], scr["s"], scr["r"], scr["sbf"]
    an, aab, sgn = scr["an"], scr["aab"], scr["sgn"]
    c = scr["c"]; g = scr["g"]
    xres = scr["xres"]; outres = scr["outres"]
    blk = lambda ap: ap

    x3 = xf32_ap.rearrange("p (nb k) -> p nb k", k=16)
    p.V(lambda e: e.tensor_reduce(bm.a((0, nblk)), x3, AX.X, ALU.max,
                                  apply_absolute_value=True),
        reads=[xres], writes=[bm.res], tag=tag + "bm")
    p.V(lambda e: e.tensor_scalar(s.a((0, nblk)), bm.a((0, nblk)),
                                  1.0 / 6.0, 1e-12, ALU.mult, ALU.max),
        reads=[bm.res], writes=[s.res], tag=tag + "s")
    p.V(lambda e: e.reciprocal(r.a((0, nblk)), s.a((0, nblk))),
        reads=[s.res], writes=[r.res], tag=tag + "r")
    p.V(lambda e: e.tensor_copy(sbf.a((0, nblk)), s.a((0, nblk))),
        reads=[s.res], writes=[sbf.res], tag=tag + "sbf")
    rbc = r.a((0, nblk)).broadcast_to([128, nblk, 16])
    p.V(lambda e: e.tensor_tensor(an.a((0, N)).rearrange("p (nb k) -> p nb k", k=16),
                                  x3, rbc, ALU.mult),
        reads=[xres, r.res], writes=[an.res], tag=tag + "an")
    p.S(lambda e: e.activation(aab.a((0, N)), an.a((0, N)), AF.Abs),
        reads=[an.res], writes=[aab.res], tag=tag + "aab")
    p.S(lambda e: e.activation(sgn.a((0, N)), an.a((0, N)), AF.Sign),
        reads=[an.res], writes=[sgn.res], tag=tag + "sgn")
    def cmp(j, dst):
        p.V(lambda e, j=j, dst=dst: e.tensor_scalar(dst.a((0, N)), aab.a((0, N)),
                                                    FP4_MID[j], FP4_W[j],
                                                    ALU.is_gt, ALU.mult),
            reads=[aab.res], writes=[dst.res], tag=tag + f"c{j}")
    cmp(0, c[0]); cmp(1, c[1])
    p.V(lambda e: e.tensor_tensor(g[0].a((0, N)), c[0].a((0, N)), c[1].a((0, N)), ALU.add),
        reads=[c[0].res, c[1].res], writes=[g[0].res], tag=tag + "g0")
    cmp(2, c[0]); cmp(3, c[1])
    p.V(lambda e: e.tensor_tensor(g[1].a((0, N)), c[0].a((0, N)), c[1].a((0, N)), ALU.add),
        reads=[c[0].res, c[1].res], writes=[g[1].res], tag=tag + "g1")
    cmp(4, c[0]); cmp(5, c[1])
    p.V(lambda e: e.tensor_tensor(g[2].a((0, N)), c[0].a((0, N)), c[1].a((0, N)), ALU.add),
        reads=[c[0].res, c[1].res], writes=[g[2].res], tag=tag + "g2")
    cmp(6, c[0])
    p.V(lambda e: e.tensor_tensor(g[0].a((0, N)), g[0].a((0, N)), g[1].a((0, N)), ALU.add),
        reads=[g[0].res, g[1].res], writes=[g[0].res], tag=tag + "g3")
    p.V(lambda e: e.tensor_tensor(g[2].a((0, N)), g[2].a((0, N)), c[0].a((0, N)), ALU.add),
        reads=[g[2].res, c[0].res], writes=[g[2].res], tag=tag + "g4")
    p.V(lambda e: e.tensor_tensor(g[0].a((0, N)), g[0].a((0, N)), g[2].a((0, N)), ALU.add),
        reads=[g[0].res, g[2].res], writes=[g[0].res], tag=tag + "qsum")
    p.V(lambda e: e.tensor_tensor(g[1].a((0, N)), g[0].a((0, N)), sgn.a((0, N)), ALU.mult),
        reads=[g[0].res, sgn.res], writes=[g[1].res], tag=tag + "qsgn")
    sbc = sbf.a((0, nblk)).broadcast_to([128, nblk, 16])
    p.V(lambda e: e.tensor_tensor(out_bf_ap.rearrange("p (nb k) -> p nb k", k=16),
                                  g[1].a((0, N)).rearrange("p (nb k) -> p nb k", k=16),
                                  sbc, ALU.mult),
        reads=[g[1].res, sbf.res], writes=[outres], tag=tag + "qout")


def build_nc():
    nc = bass.Bass("TRN2", target_bir_lowering=False,
                   detect_race_conditions=DETECT_RACES)

    d = {}
    def din(name, shape, dt=F32):
        d[name] = nc.dram_tensor(name, shape, dt, kind="ExternalInput").ap()
        return d[name]

    x_in = din("x", [NB, T, DIM])
    for w in ("wsqT", "wskT", "wsvT", "wvT", "wqT", "wkT", "woT"):
        din(w, [DIM, DIM])
    din("wgT", [DIM, H])
    for bn in ("bsq", "bsk", "bsv", "bq", "bk", "bv", "bo"):
        din(bn, [1, DIM])
    din("bg", [1, H])
    din("temp16", [H, 1])
    din("wgate", [FFI, DIM])
    din("wup", [FFI, DIM])
    din("wdown", [DIM, FFI])
    din("identb", [128, 128], BF16)
    din("identf", [128, 128], F32)
    din("triLneg", [128, 128], F32)
    din("triUneg", [128, 128], F32)
    din("maskLowNeg", [128, 128], BF16)
    din("pfull", [H, DIM], BF16)
    din("pfullT", [128, NFC * H], BF16)
    din("onesb", [1, 512], BF16)
    din("onesf", [1, 128], F32)

    out_t = nc.dram_tensor("out", [NB, T, DIM], F32, kind="ExternalOutput").ap()

    ssq = nc.dram_tensor("ssq", [NB, NCH, 128, NFC, 128], BF16, kind="Internal").ap()
    ssk = nc.dram_tensor("ssk", [NB, NCH, 128, NFC, 128], BF16, kind="Internal").ap()
    ssv = nc.dram_tensor("ssv", [NB, NCH, 128, DIM], BF16, kind="Internal").ap()
    sV = nc.dram_tensor("sV", [NB, NCH, 128, H, HD], BF16, kind="Internal").ap()
    sxq = nc.dram_tensor("sxq", [NT128, 128, NFC, 128], BF16, kind="Internal").ap()
    shq = nc.dram_tensor("shq", [NT128, 128, NIC, 128], BF16, kind="Internal").ap()
    soutp = nc.dram_tensor("soutp", [NT128, 128, DIM], F32, kind="Internal").ap()
    rsq = [[Res() for _ in range(NCH)] for _ in range(NB)]
    rsk = [[Res() for _ in range(NCH)] for _ in range(NB)]
    rsv = [[Res() for _ in range(NCH)] for _ in range(NB)]
    rV = [[Res() for _ in range(NCH)] for _ in range(NB)]
    rxq = [Res() for _ in range(NT128)]
    rhq = [Res() for _ in range(NT128)]
    routp = [Res() for _ in range(NT128)]

    stack = ExitStack()
    ec = stack.enter_context
    p = Prog()
    AF32 = Arena(nc, "AF", 14080, F32)
    AB = Arena(nc, "ABF", 78240, BF16)
    psW = nc.alloc_psum_tensor("psW", [128, 1024], F32)
    psX = nc.alloc_psum_tensor("psX", [128, 1024], F32)
    psA = nc.alloc_psum_tensor("psA", [128, 512], F32)
    psB = nc.alloc_psum_tensor("psB", [128, 512], F32)
    psTa = nc.alloc_psum_tensor("psTa", [128, 1024], BF16)
    psTb = nc.alloc_psum_tensor("psTb", [128, 1024], BF16)
    rBig = [Res(f"psBig{i}") for i in range(4)]
    rA, rB_, rTa, rTb = Res("psA"), Res("psB"), Res("psTa"), Res("psTb")
    def bigap(i, w=512):
        t = psW if i < 2 else psX
        j = i % 2
        return t.ap()[:, j * 512:j * 512 + w]

    return (nc, d, out_t, stack, ec, p, AF32, AB,
            dict(psW=psW, psX=psX, psA=psA, psB=psB, psTa=psTa, psTb=psTb,
                 rBig=rBig, rA=rA, rB=rB_, rTa=rTa, rTb=rTb, bigap=bigap),
            dict(ssq=ssq, ssk=ssk, ssv=ssv, sV=sV, sxq=sxq, shq=shq,
                 soutp=soutp, rsq=rsq, rsk=rsk, rsv=rsv, rV=rV, rxq=rxq,
                 rhq=rhq, routp=routp))


_DBG = {}


def build_program(upto="all"):
    (nc, d, out_t, stack, ec, p, AF32, AB, PS, SC) = build_nc()
    psW, psX, psA, psB = PS["psW"], PS["psX"], PS["psA"], PS["psB"]
    psTa, psTb = PS["psTa"], PS["psTb"]
    rBig, rA, rB, rTa, rTb = PS["rBig"], PS["rA"], PS["rB"], PS["rTa"], PS["rTb"]
    bigap = PS["bigap"]

    cst = {}
    def c_load(nm, arena, shape, pn=128, src=None):
        reg = arena.alloc(nm, shape)
        p.load(reg.a(p0=0, pn=pn), src if src is not None else d[nm],
               writes=[reg.res], tag="c_" + nm)
        return reg
    identb = c_load("identb", AB, [128])
    identf = c_load("identf", AF32, [128])
    triLneg = c_load("triLneg", AF32, [128])
    triUneg = c_load("triUneg", AF32, [128])
    maskLN = c_load("maskLowNeg", AB, [128])
    pfull = c_load("pfull", AB, [DIM], pn=H)
    pfullT = c_load("pfullT", AB, [NFC, H])
    onesb = c_load("onesb", AB, [512], pn=1)
    onesf = c_load("onesf", AF32, [128], pn=1)
    bias_b = {}
    bf_tmp = AF32.alloc("btmp", [DIM])
    for bn in ("bsq", "bsk", "bsv", "bq", "bk", "bv", "bo", "bg"):
        n = H if bn == "bg" else DIM
        reg = AB.alloc(bn + "_b", [n])
        p.load(bf_tmp.a((0, n), p0=0, pn=1), d[bn], writes=[bf_tmp.res], tag=bn)
        p.V(lambda e, r=reg, n=n: e.tensor_copy(r.a(p0=0, pn=1),
                                                bf_tmp.a((0, n), p0=0, pn=1)),
            reads=[bf_tmp.res], writes=[reg.res], tag=bn + "c")
        bias_b[bn] = reg
    tmp16 = AF32.alloc("tmp16", [1])
    sp16 = AF32.alloc("sp16", [1])
    p.load(tmp16.a(p0=0, pn=H), d["temp16"], writes=[tmp16.res], tag="temp")
    p.S(lambda e: e.activation(tmp16.a(p0=0, pn=H), tmp16.a(p0=0, pn=H), AF.Exp),
        reads=[tmp16.res], writes=[tmp16.res], tag="spt1")
    p.S(lambda e: e.activation(sp16.a(p0=0, pn=H), tmp16.a(p0=0, pn=H),
                               AF.Ln, bias=1.0),
        reads=[tmp16.res], writes=[sp16.res], tag="spt2")

    logg = AF32.alloc("logg", [NT128, H])
    _DBG["logg"] = logg

    wA = {}
    ftmp = [AF32.alloc(f"ftmp{i}", [1024]) for i in range(2)]
    for nm, dn in (("wsq", "wsqT"), ("wsk", "wskT"), ("wsv", "wsvT"),
                   ("wv", "wvT")):
        reg = AB.alloc(nm, [NFC, 1024])
        for kc in range(NFC):
            ft = ftmp[kc % 2]
            p.load(ft.a(), d[dn][kc * 128:(kc + 1) * 128, :],
                   writes=[ft.res], tag=f"{nm}l{kc}")
            p.V(lambda e, r=reg, kc=kc, ft=ft: e.tensor_copy(r.a(kc), ft.a()),
                reads=[ft.res], writes=[reg.res], tag=f"{nm}c{kc}")
        wA[nm] = reg
    wg = AB.alloc("wg", [NFC, H])
    ftg = AF32.alloc("ftg", [NFC, H])
    p.load(ftg.a(), d["wgT"].rearrange("(kc p) h -> p kc h", p=128),
           writes=[ftg.res], tag="wgl")
    p.V(lambda e: e.tensor_copy(wg.a(), ftg.a()), reads=[ftg.res],
        writes=[wg.res], tag="wgc")

    xin = [AF32.alloc(f"xin{i}", [1024]) for i in range(2)]
    xnf = [AF32.alloc(f"xnf{i}", [1024]) for i in range(2)]
    x2s = AF32.alloc("x2s", [1024])
    sm = AF32.alloc("sm", [16])
    ezg = AF32.alloc("ezg", [H])
    xnbf = [AB.alloc(f"xnbf{i}", [1024]) for i in range(2)]
    xnfmr = [AB.alloc(f"xnfm{i}", [NFC, 512]) for i in range(2)]
    xqtm = [AB.alloc(f"xqtm{i}", [1024]) for i in range(2)]
    xqst = [AB.alloc("xqst0", [NFC, 128])]
    stg = {"sq": [AB.alloc("stsq0", [NFC, 512])],
           "sk": [AB.alloc("stsk0", [NFC, 512])]}
    svst = [AB.alloc("svst0", [1024])]
    Vst = [AB.alloc("Vst0", [1024])]
    def mk_qscr(QN, pfx, sgn_share=None):
        if sgn_share is not None:
            sgn_reg = sgn_share
        else:
            sgn_reg = None
        return dict(
            bm=AF32.alloc(pfx + "bm", [QN // 16]), s=AF32.alloc(pfx + "s", [QN // 16]),
            r=AF32.alloc(pfx + "r", [QN // 16]), sbf=AB.alloc(pfx + "sbf", [QN // 16]),
            an=AF32.alloc(pfx + "an", [QN]), aab=AF32.alloc(pfx + "aab", [QN]),
            sgn=(sgn_reg if sgn_reg is not None else AB.alloc(pfx + "sgn", [QN])),
            c=[AB.alloc(pfx + f"c{j}", [QN]) for j in range(2)],
            g=[AB.alloc(pfx + f"g{j}", [QN]) for j in range(3)])

    def free_qscr(q, skip_sgn=False):
        AF32.free(q["bm"], q["s"], q["r"], q["an"], q["aab"])
        AB.free(q["sbf"], *q["c"], *q["g"])
        if not skip_sgn:
            AB.free(q["sgn"])

    qscr2 = [mk_qscr(1024, "qa_"), mk_qscr(1024, "qb_")]

    ssq, ssk, ssv, sVt = SC["ssq"], SC["ssk"], SC["ssv"], SC["sV"]
    sxq, shq, soutp = SC["sxq"], SC["shq"], SC["soutp"]
    rsq, rsk, rsv, rV = SC["rsq"], SC["rsk"], SC["rsv"], SC["rV"]
    rxq, rhq, routp = SC["rxq"], SC["rhq"], SC["routp"]

    for tt in range(8):
        xnfm = xnfmr[tt % 2]
        b = tt // 4
        for j in range(4):
            ti = tt * 4 + j
            trow = (ti % 16) * 128
            xi = xin[ti % 2]
            p.load(xi.a(), d["x"][b, trow:trow + 128, :], writes=[xi.res],
                   tag=f"x{ti}")
            xn = xnf[ti % 2]
            p.V(lambda e, xi=xi: e.reduce_sum(sm.a((0, 1)), xi.a(), AX.X),
                reads=[xi.res], writes=[sm.res], tag=f"ln_s1_{ti}")
            p.S(lambda e, xi=xi: e.activation(x2s.a(), xi.a(), AF.Square,
                                              accum_out=sm.a((1, 1))),
                reads=[xi.res], writes=[x2s.res, sm.res], tag=f"ln_sq_{ti}")
            p.V(lambda e: e.tensor_scalar(sm.a((2, 1)), sm.a((0, 1)),
                                          1.0 / DIM, None, ALU.mult),
                reads=[sm.res], writes=[sm.res], tag=f"ln_m_{ti}")
            p.V(lambda e: e.tensor_scalar(sm.a((3, 1)), sm.a((1, 1)),
                                          1.0 / DIM, LN_EPS, ALU.mult, ALU.add),
                reads=[sm.res], writes=[sm.res], tag=f"ln_e2_{ti}")
            p.V(lambda e: e.tensor_tensor(sm.a((4, 1)), sm.a((2, 1)),
                                          sm.a((2, 1)), ALU.mult),
                reads=[sm.res], writes=[sm.res], tag=f"ln_m2_{ti}")
            p.V(lambda e: e.tensor_tensor(sm.a((5, 1)), sm.a((3, 1)),
                                          sm.a((4, 1)), ALU.subtract),
                reads=[sm.res], writes=[sm.res], tag=f"ln_var_{ti}")
            p.S(lambda e: e.activation(sm.a((6, 1)), sm.a((5, 1)), AF.Ln),
                reads=[sm.res], writes=[sm.res], tag=f"ln_ln_{ti}")
            p.S(lambda e: e.activation(sm.a((7, 1)), sm.a((6, 1)), AF.Exp,
                                       scale=-0.5),
                reads=[sm.res], writes=[sm.res], tag=f"ln_rs_{ti}")
            p.V(lambda e, xi=xi, xn=xn: e.tensor_scalar(
                    xn.a(), xi.a(), sm.a((2, 1)), sm.a((7, 1)),
                    ALU.subtract, ALU.mult),
                reads=[xi.res, sm.res], writes=[xn.res], tag=f"ln_ap_{ti}")
            xb_ = xnbf[ti % 2]
            p.V(lambda e, xn=xn, xb_=xb_: e.tensor_copy(xb_.a(), xn.a()),
                reads=[xn.res], writes=[xb_.res], tag=f"cst_{ti}")
            for fc in range(NFC):
                p.T(lambda e, fc=fc, xb_=xb_: e.transpose(
                        psTa.ap()[:, fc * 128:(fc + 1) * 128],
                        xb_.a((fc * 128, 128)), identb.a()),
                    reads=[xb_.res, identb.res], writes=[rTa],
                    tag=f"trx_{ti}_{fc}")
            p.S(lambda e, j=j, xnfm=xnfm: e.copy(
                    xnfm.a(None, (j * 128, 128)),
                    psTa.ap().rearrange("p (fc t) -> p fc t", t=128)),
                reads=[rTa], writes=[xnfm.res], tag=f"evx_{ti}")
            xq = xqtm[ti % 2]
            qscr = qscr2[ti % 2]
            qscr["xres"] = xn.res
            qscr["outres"] = xq.res
            _quant_emit(p, nc, None, xn.a(), xq.a(), DIM // 16, qscr,
                        tag=f"qx{ti}_")
            for fc in range(NFC):
                p.T(lambda e, fc=fc, xq=xq: e.transpose(
                        psTb.ap()[:, fc * 128:(fc + 1) * 128],
                        xq.a((fc * 128, 128)), identb.a()),
                    reads=[xq.res, identb.res], writes=[rTb],
                    tag=f"trq_{ti}_{fc}")
            xs = xqst[0]
            p.S(lambda e, xs=xs: e.copy(
                    xs.a(), psTb.ap().rearrange("p (fc t) -> p fc t", t=128)),
                reads=[rTb], writes=[xs.res], tag=f"evq_{ti}")
            p.store(sxq[ti], xs.a(), reads=[xs.res], writes=[rxq[ti]],
                    tag=f"sxq{ti}")
            for kc in range(NFC):
                p.T(lambda e, kc=kc, j=j, xnfm=xnfm: e.matmul(
                        psA.ap()[:, 0:H], xnfm.a(kc, (j * 128, 128)),
                        wg.a(kc), start=(kc == 0), stop=False),
                    reads=[xnfm.res, wg.res], writes=[rA], tag=f"zg_{ti}_{kc}")
            p.T(lambda e: e.matmul(psA.ap()[:, 0:H],
                                   onesb.a((0, 128), p0=0, pn=1),
                                   bias_b["bg"].a(p0=0, pn=1),
                                   start=False, stop=True),
                reads=[onesb.res, bias_b["bg"].res], writes=[rA],
                tag=f"zg_b_{ti}")
            p.S(lambda e: e.activation(ezg.a(), psA.ap()[:, 0:H], AF.Exp),
                reads=[rA], writes=[ezg.res], tag=f"zg_e_{ti}")
            p.S(lambda e, ti=ti: e.activation(logg.a(ti), ezg.a(), AF.Ln,
                                              bias=1.0),
                reads=[ezg.res], writes=[logg.res], tag=f"zg_l_{ti}")
        c0 = (tt % 4) * 4
        for prn, wn, bn, sres, sdst in (("sq", "wsq", "bsq", rsq, ssq),
                                        ("sk", "wsk", "bsk", rsk, ssk)):
            st = stg[prn][0]
            for ncx in range(NFC):
                bank = ncx % 2
                for kc in range(NFC):
                    p.T(lambda e, kc=kc, ncx=ncx, bank=bank, wn=wn, xnfm=xnfm: e.matmul(
                            bigap(bank), wA[wn].a(kc, (ncx * 128, 128)),
                            xnfm.a(kc), start=(kc == 0), stop=False),
                        reads=[wA[wn].res, xnfm.res], writes=[rBig[bank]],
                        tag=f"{prn}_{tt}_{ncx}_{kc}")
                p.T(lambda e, ncx=ncx, bank=bank, bn=bn: e.matmul(
                        bigap(bank), bias_b[bn].a((ncx * 128, 128), p0=0, pn=1),
                        onesb.a(p0=0, pn=1), start=False, stop=True),
                    reads=[bias_b[bn].res, onesb.res], writes=[rBig[bank]],
                    tag=f"{prn}b_{tt}_{ncx}")
                p.S(lambda e, ncx=ncx, bank=bank, st=st: e.copy(
                        st.a(ncx), bigap(bank)),
                    reads=[rBig[bank]], writes=[st.res], tag=f"{prn}e_{tt}_{ncx}")
            for q in range(4):
                cix = (tt % 4) * 4 + q
                p.store(sdst[tt // 4, cix],
                        st.a(None, (q * 128, 128)),
                        reads=[st.res], writes=[sres[tt // 4][cix]],
                        tag=f"{prn}s_{tt}_{q}")
        for j in range(4):
            ti = tt * 4 + j
            cix = ti % 16
            for prn, wn, bn, stb in (("sv", "wsv", "bsv", svst),
                                     ("V", "wv", "bv", Vst)):
                for kc in range(NFC):
                    for nh in range(2):
                        p.T(lambda e, kc=kc, nh=nh, j=j, wn=wn, xnfm=xnfm: e.matmul(
                                bigap(2 + nh), xnfm.a(kc, (j * 128, 128)),
                                wA[wn].a(kc, (nh * 512, 512)),
                                start=(kc == 0), stop=False),
                            reads=[xnfm.res, wA[wn].res], writes=[rBig[2 + nh]],
                            tag=f"{prn}_{ti}_{kc}_{nh}")
                for nh in range(2):
                    p.T(lambda e, nh=nh, bn=bn: e.matmul(
                            bigap(2 + nh), onesb.a((0, 128), p0=0, pn=1),
                            bias_b[bn].a((nh * 512, 512), p0=0, pn=1),
                            start=False, stop=True),
                        reads=[onesb.res, bias_b[bn].res], writes=[rBig[2 + nh]],
                        tag=f"{prn}b_{ti}_{nh}")
                sb_ = stb[0]
                p.V(lambda e, sb_=sb_: e.tensor_copy(
                        sb_.a(), psX.ap()),
                    reads=[rBig[2], rBig[3]], writes=[sb_.res],
                    tag=f"{prn}e_{ti}")
                if prn == "sv":
                    p.store(ssv[b, cix], sb_.a(), reads=[sb_.res],
                            writes=[rsv[b][cix]], tag=f"svs_{ti}")
                else:
                    p.store(sVt[b, cix], sb_.a().rearrange(
                                "p (h e) -> p h e", e=HD),
                            reads=[sb_.res], writes=[rV[b][cix]],
                            tag=f"Vs_{ti}")
    AF32.free(xin[0], xin[1], x2s, ftmp[0], ftmp[1], ftg, bf_tmp,
              ezg, xnf[0], xnf[1], tmp16, sm)
    AB.free(wA["wsq"], wA["wsk"], wA["wsv"], wA["wv"], wg,
            xnbf[0], xnbf[1], xnfmr[0], xnfmr[1], xqtm[0], xqtm[1],
            xqst[0],
            stg["sq"][0], stg["sk"][0], svst[0], Vst[0],
            bias_b["bsq"], bias_b["bsk"], bias_b["bsv"], bias_b["bv"],
            bias_b["bg"])
    free_qscr(qscr2[0])
    free_qscr(qscr2[1])
    if upto == "a1":
        p.lower(nc, ec)
        return nc, stack
    _phases_bcd(nc, d, out_t, p, AF32, AB, PS, SC, locals(), upto)
    p.lower(nc, ec)
    return nc, stack


def _phases_bcd(nc, d, out_t, p, AF32, AB, PS, SC, env, upto):
    psW, psX, psA, psB = PS["psW"], PS["psX"], PS["psA"], PS["psB"]
    psTa, psTb = PS["psTa"], PS["psTb"]
    rBig, rA, rB, rTa, rTb = PS["rBig"], PS["rA"], PS["rB"], PS["rTa"], PS["rTb"]
    bigap = PS["bigap"]
    ssq, ssk, ssv, sVt = SC["ssq"], SC["ssk"], SC["ssv"], SC["sV"]
    sxq, shq, soutp = SC["sxq"], SC["shq"], SC["soutp"]
    rsq, rsk, rsv, rV = SC["rsq"], SC["rsk"], SC["rsv"], SC["rV"]
    rxq, rhq, routp = SC["rxq"], SC["rhq"], SC["routp"]
    logg = env["logg"]
    identb, identf = env["identb"], env["identf"]
    triLneg, triUneg, maskLN = env["triLneg"], env["triUneg"], env["maskLN"]
    pfull, pfullT = env["pfull"], env["pfullT"]
    onesb, onesf = env["onesb"], env["onesf"]
    bias_b, sp16 = env["bias_b"], env["sp16"]

    ftc = [AF32.alloc(f"ftc{i}", [1024]) for i in range(2)]

    def load_w(nm, dn, tagp):
        reg = AB.alloc(nm, [NFC, 1024])
        for kc in range(NFC):
            ft = ftc[kc % 2]
            p.load(ft.a(), d[dn][kc * 128:(kc + 1) * 128, :], writes=[ft.res],
                   tag=f"{tagp}{nm}l{kc}")
            p.V(lambda e, r=reg, kc=kc, ft=ft: e.tensor_copy(r.a(kc), ft.a()),
                reads=[ft.res], writes=[reg.res], tag=f"{tagp}{nm}c{kc}")
        return reg

    Mf = AF32.alloc("Mf", [H, HD])
    lld = AF32.alloc("lld", [32])
    nl = AF32.alloc("nl", [16])
    ek2 = AF32.alloc("ek2", [16])
    eend = AF32.alloc("eend", [NFC])
    lTr = AF32.alloc("lTr", [H, 128])
    lTsb = AF32.alloc("lTsb", [128])

    for b in range(NB):
        Hseq = AB.alloc("Hseq", [NFC, T])
        _DBG["Hseq"] = Hseq
        sqg = [AB.alloc(f"sqg{i}", [NFC, 128]) for i in range(2)]
        skg = [AB.alloc(f"skg{i}", [NFC, 128]) for i in range(2)]
        svc = [AB.alloc(f"svc{i}", [DIM]) for i in range(2)]
        Mbf = AB.alloc("Mbf", [H, HD])
        qp = AB.alloc("qp", [NFC, 128])
        Dm = [AB.alloc(f"Dm{i}", [128]) for i in range(16)]
        SDm = [AB.alloc(f"SDm{i}", [128]) for i in range(2)]
        elT = AB.alloc("elT", [128])
        ktil = AB.alloc("ktil", [H, HD])
        p.V(lambda e: e.memset(Mf.a(), 0.0), writes=[Mf.res], tag=f"M0_{b}")
        for c in range(NCH):
            sq_, sk_ = sqg[c % 2], skg[c % 2]
            p.load(sq_.a(), ssq[b, c], reads=[rsq[b][c]],
                   writes=[sq_.res], tag=f"Bq{b}_{c}")
            p.load(sk_.a(), ssk[b, c], reads=[rsk[b][c]],
                   writes=[sk_.res], tag=f"Bk{b}_{c}")
            sv_ = svc[c % 2]
            p.load(sv_.a(), ssv[b, c], reads=[rsv[b][c]], writes=[sv_.res],
                   tag=f"Bv{b}_{c}")
            ti = b * NCH + c
            p.T(lambda e, ti=ti: e.matmul(psA.ap()[:, 0:H], triLneg.a(),
                                          logg.a(ti), start=True, stop=True),
                reads=[triLneg.res, logg.res], writes=[rA], tag=f"cums{ti}")
            p.T(lambda e, ti=ti: e.matmul(psA.ap()[:, H:2 * H], triUneg.a(),
                                          logg.a(ti), start=True, stop=True),
                reads=[triUneg.res, logg.res], writes=[rA], tag=f"ldif{ti}")
            p.V(lambda e: e.tensor_copy(lld.a((0, 32)), psA.ap()[:, 0:32]),
                reads=[rA], writes=[lld.res], tag=f"lev{ti}")
            p.V(lambda e: e.tensor_scalar(nl.a(), lld.a((0, 16)), -1.0, None,
                                          ALU.mult),
                reads=[lld.res], writes=[nl.res], tag=f"nl{ti}")
            p.S(lambda e: e.activation(ek2.a(), lld.a((16, 16)), AF.Exp),
                reads=[lld.res], writes=[ek2.res], tag=f"ek2{ti}")
            p.T(lambda e: e.transpose(psB.ap()[0:16, 0:128], lld.a((0, 16)),
                                      identf.a()),
                reads=[lld.res, identf.res], writes=[rB], tag=f"lT{ti}")
            p.S(lambda e: e.activation(elT.a(p0=0, pn=16),
                                       psB.ap()[0:16, 0:128], AF.Exp),
                reads=[rB], writes=[elT.res], tag=f"elT{ti}")
            for h in range(H):
                p.T(lambda e, h=h: e.transpose(psB.ap()[0:1, 128:256],
                                               lld.a((h, 1)), identf.a()),
                    reads=[lld.res, identf.res], writes=[rB], tag=f"lr{ti}_{h}")
                p.V(lambda e, h=h: e.tensor_copy(lTr.a(h, p0=0, pn=1),
                                                 psB.ap()[0:1, 128:256]),
                    reads=[rB], writes=[lTr.res], tag=f"lrc{ti}_{h}")
            for fc in range(NFC):
                p.T(lambda e, fc=fc: e.matmul(
                        bigap(fc // 4)[:, (fc % 4) * 128:(fc % 4) * 128 + 128],
                        pfull.a((fc * 128, 128), p0=0, pn=16),
                        elT.a(p0=0, pn=16), start=True, stop=True),
                    reads=[pfull.res, elT.res], writes=[rBig[fc // 4]],
                    tag=f"eg{ti}_{fc}")
            for fc in range(NFC):
                p.V(lambda e, fc=fc, sq_=sq_: e.tensor_tensor(
                        qp.a(fc), sq_.a(fc),
                        bigap(fc // 4)[:, (fc % 4) * 128:(fc % 4) * 128 + 128],
                        ALU.mult),
                    reads=[sq_.res, rBig[fc // 4]], writes=[qp.res],
                    tag=f"qp{ti}_{fc}")
            p.V(lambda e: e.tensor_copy(
                    eend.a(), bass.AP(psW, 127, [[1024, 128], [128, 8]])),
                reads=[rBig[0], rBig[1]], writes=[eend.res], tag=f"ee{ti}")
            p.V(lambda e: e.tensor_copy(Mbf.a(), Mf.a()),
                reads=[Mf.res], writes=[Mbf.res], tag=f"mc{ti}")
            for h in range(H):
                pban = psA if h % 2 == 0 else psB
                rban = rA if h % 2 == 0 else rB
                p.T(lambda e, h=h, pban=pban: e.matmul(
                        pban.ap()[:, 256:384], onesf.a((0, 128), p0=0, pn=1),
                        lTr.a(h, p0=0, pn=1), start=True, stop=False),
                    reads=[onesf.res, lTr.res], writes=[rban],
                    tag=f"lbc{ti}_{h}")
                p.T(lambda e, pban=pban: e.matmul(
                        pban.ap()[:, 256:384], identb.a(), maskLN.a(),
                        start=False, stop=True),
                    reads=[identb.res, maskLN.res], writes=[rban],
                    tag=f"lbm{ti}_{h}")
                p.S(lambda e, h=h, pban=pban: e.activation(
                        Dm[h].a(), pban.ap()[:, 256:384], AF.Exp,
                        bias=nl.a((h, 1))),
                    reads=[rban, nl.res], writes=[Dm[h].res],
                    tag=f"dex{ti}_{h}")
            for fc in range(NFC):
                p.T(lambda e, fc=fc, sk_=sk_: e.transpose(
                        psTa.ap()[:, fc * 128:(fc + 1) * 128],
                        sk_.a(fc), identb.a()),
                    reads=[sk_.res, identb.res], writes=[rTa],
                    tag=f"ktr{ti}_{fc}")
            for h in range(H):
                fc, off = h // 2, (h % 2) * 64
                p.V(lambda e, h=h, fc=fc, off=off: e.tensor_scalar(
                        ktil.a(h),
                        psTa.ap()[:, fc * 128 + off:fc * 128 + off + 64],
                        ek2.a((h, 1)), None, ALU.mult),
                    reads=[rTa, ek2.res], writes=[ktil.res],
                    tag=f"kts{ti}_{h}")
            for h in range(H):
                fc, off = h // 2, (h % 2) * 64
                sb_ = 2 + h % 2
                hb = h % 2
                if h % 2 == 0:
                    def qkfn(e, fc=fc, sk_=sk_, sq_=sq_):
                        e.matmul(bigap(2)[:, 0:128],
                                 sk_.a(fc, p0=0, pn=64),
                                 sq_.a(fc, p0=0, pn=64),
                                 start=True, stop=True)
                        return e.matmul(bigap(3)[:, 0:128],
                                        sk_.a(fc, p0=64, pn=64),
                                        sq_.a(fc, p0=64, pn=64),
                                        start=True, stop=True)
                    p.T(qkfn, reads=[sk_.res, sq_.res],
                        writes=[rBig[2], rBig[3]], tag=f"qk{ti}_{h}")
                p.V(lambda e, h=h, sb_=sb_: e.scalar_tensor_tensor(
                        SDm[h % 2].a(), bigap(sb_)[:, 0:128], 1.0,
                        Dm[h].a(), ALU.mult, ALU.mult),
                    reads=[rBig[sb_], Dm[h].res], writes=[SDm[h % 2].res],
                    tag=f"sdm{ti}_{h}")
                p.T(lambda e, h=h, hb=hb: e.matmul(
                        bigap(hb)[0:64, 128:256], Mbf.a(h), qp.a(h // 2),
                        start=True, stop=False),
                    reads=[Mbf.res, qp.res], writes=[rBig[hb]],
                    tag=f"hin{ti}_{h}")
                p.T(lambda e, h=h, hb=hb, sv_=sv_: e.matmul(
                        bigap(hb)[0:64, 128:256], sv_.a((h * HD, HD)),
                        SDm[h % 2].a(), start=False, stop=True),
                    reads=[sv_.res, SDm[h % 2].res], writes=[rBig[hb]],
                    tag=f"hsv{ti}_{h}")
                p.V(lambda e, h=h, hb=hb, fc=fc, off=off, c=c: e.tensor_copy(
                        Hseq.a(fc, (c * 128, 128), p0=off, pn=64),
                        bigap(hb)[0:64, 128:256]),
                    reads=[rBig[hb]], writes=[Hseq.res], tag=f"hev{ti}_{h}")
                db = psA if h % 2 == 0 else psB
                rdb = rA if h % 2 == 0 else rB
                p.T(lambda e, h=h, db=db, off=off, sv_=sv_: e.matmul(
                        db.ap()[off:off + 64, 384:448], ktil.a(h),
                        sv_.a((h * HD, HD)), start=True, stop=True,
                        tile_position=(0, off)),
                    reads=[ktil.res, sv_.res], writes=[rdb], tag=f"dlt{ti}_{h}")
                p.V(lambda e, h=h, db=db, off=off, fc=fc:
                    e.scalar_tensor_tensor(
                        Mf.a(h, p0=off, pn=64), Mf.a(h, p0=off, pn=64),
                        eend.a((fc, 1), p0=off, pn=64),
                        db.ap()[off:off + 64, 384:448], ALU.mult, ALU.add),
                    reads=[Mf.res, eend.res, rdb], writes=[Mf.res],
                    tag=f"mup{ti}_{h}")
        AB.free(sqg[0], sqg[1], skg[0], skg[1], svc[0], svc[1], Mbf, qp,
                *Dm, SDm[0], SDm[1], elT, ktil)
        if upto == "b":
            if b == 0:
                AB.free(Hseq)
            continue
        Qhat = AB.alloc("Qhat", [NFC, T])
        Khat = AB.alloc("Khat", [NFC, T])
        _DBG["Qhat"], _DBG["Khat"] = Qhat, Khat
        Rraw = AF32.alloc("Rraw", [NFC, 512])
        sqb = AB.alloc("sqb", [NFC, 512])
        rq = AF32.alloc("rq", [512])
        rqb = AB.alloc("rqb", [512])
        for prn, wn, bn, dst in (("q", "wqT", "bq", Qhat),
                                 ("k", "wkT", "bk", Khat)):
            wqk = load_w("wqk", wn, f"c{b}{prn}")
            for qg in range(4):
                for ncx in range(NFC):
                    bank = ncx % 2
                    for kc in range(NFC):
                        p.T(lambda e, kc=kc, ncx=ncx, bank=bank, qg=qg,
                                   wqk=wqk: e.matmul(
                                bigap(bank), wqk.a(kc, (ncx * 128, 128)),
                                Hseq.a(kc, (qg * 512, 512)),
                                start=(kc == 0), stop=False),
                            reads=[wqk.res, Hseq.res], writes=[rBig[bank]],
                            tag=f"{prn}p{b}_{qg}_{ncx}_{kc}")
                    p.T(lambda e, ncx=ncx, bank=bank, bn=bn: e.matmul(
                            bigap(bank),
                            bias_b[bn].a((ncx * 128, 128), p0=0, pn=1),
                            onesb.a(p0=0, pn=1), start=False, stop=True),
                        reads=[bias_b[bn].res, onesb.res], writes=[rBig[bank]],
                        tag=f"{prn}pb{b}_{qg}_{ncx}")
                    p.V(lambda e, ncx=ncx, bank=bank: e.tensor_copy(
                            Rraw.a(ncx), bigap(bank)),
                        reads=[rBig[bank]], writes=[Rraw.res],
                        tag=f"{prn}ev{b}_{qg}_{ncx}")
                    p.S(lambda e, ncx=ncx: e.activation(
                            sqb.a(ncx), Rraw.a(ncx), AF.Square),
                        reads=[Rraw.res], writes=[sqb.res],
                        tag=f"{prn}sq{b}_{qg}_{ncx}")
                for fc in range(NFC):
                    p.T(lambda e, fc=fc: e.matmul(
                            psA.ap()[0:16, 0:512], pfullT.a(fc), sqb.a(fc),
                            start=(fc == 0), stop=(fc == NFC - 1)),
                        reads=[pfullT.res, sqb.res], writes=[rA],
                        tag=f"{prn}ss{b}_{qg}_{fc}")
                p.V(lambda e: e.tensor_scalar(rq.a(p0=0, pn=16),
                                              psA.ap()[0:16, 0:512],
                                              1e-24, None, ALU.max),
                    reads=[rA], writes=[rq.res], tag=f"{prn}mx{b}_{qg}")
                p.S(lambda e: e.activation(rq.a(p0=0, pn=16),
                                           rq.a(p0=0, pn=16), AF.Ln),
                    reads=[rq.res], writes=[rq.res], tag=f"{prn}ln{b}_{qg}")
                p.S(lambda e: e.activation(rq.a(p0=0, pn=16),
                                           rq.a(p0=0, pn=16), AF.Exp,
                                           scale=-0.5),
                    reads=[rq.res], writes=[rq.res], tag=f"{prn}ex{b}_{qg}")
                if prn == "k":
                    p.V(lambda e: e.tensor_scalar(rq.a(p0=0, pn=16),
                                                  rq.a(p0=0, pn=16),
                                                  sp16.a(p0=0, pn=16), None,
                                                  ALU.mult),
                        reads=[rq.res, sp16.res], writes=[rq.res],
                        tag=f"ksp{b}_{qg}")
                p.V(lambda e: e.tensor_copy(rqb.a(p0=0, pn=16),
                                            rq.a(p0=0, pn=16)),
                    reads=[rq.res], writes=[rqb.res], tag=f"{prn}cb{b}_{qg}")
                for fc in range(NFC):
                    pban = psA if fc % 2 == 0 else psB
                    rban = rA if fc % 2 == 0 else rB
                    p.T(lambda e, fc=fc, pban=pban: e.matmul(
                            pban.ap()[:, 0:512],
                            pfull.a((fc * 128, 128), p0=0, pn=16),
                            rqb.a(p0=0, pn=16), start=True, stop=True),
                        reads=[pfull.res, rqb.res], writes=[rban],
                        tag=f"{prn}rg{b}_{qg}_{fc}")
                    p.V(lambda e, fc=fc, pban=pban, rban=rban, dst=dst, qg=qg:
                        e.tensor_tensor(dst.a(fc, (qg * 512, 512)),
                                        Rraw.a(fc), pban.ap()[:, 0:512],
                                        ALU.mult),
                        reads=[Rraw.res, rban], writes=[dst.res],
                        tag=f"{prn}hm{b}_{qg}_{fc}")
            AB.free(wqk)
        AB.free(sqb, rqb)
        AF32.free(Rraw, rq)
        AB.free(Hseq)
        if upto == "c":
            if b == 0:
                AB.free(Qhat, Khat)
            continue
        wo = load_w("wo", "woT", f"d{b}")
        Vaug = AB.alloc("Vaug", [H, NCH, 65])
        Esb = [AB.alloc(f"Esb{i}", [2048]) for i in range(2)]
        attnfm = AB.alloc("attnfm", [NFC, 512])
        r64 = [AB.alloc(f"r64{i}", [512]) for i in range(2)]
        rden = AF32.alloc("rden", [512])
        rden2 = AF32.alloc("rden2", [512])
        r64b = [AB.alloc(f"r64b{i}", [512]) for i in range(2)]
        xres = [AF32.alloc(f"xres{i}", [1024]) for i in range(2)]
        ofin = [AF32.alloc(f"ofin{i}", [1024]) for i in range(2)]
        p.V(lambda e: e.memset(Vaug.a(None, None, (64, 1)), 1.0),
            writes=[Vaug.res], tag=f"vaug1_{b}")
        for c in range(NCH):
            p.load(Vaug.a(None, c, (0, 64)), sVt[b, c],
                   reads=[rV[b][c]], writes=[Vaug.res], tag=f"VL{b}_{c}")
        for qg in range(4):
            for hp in range(NFC):
                h0, h1 = 2 * hp, 2 * hp + 1
                cfg = [(h0, 0, psW, (rBig[0], rBig[1]), psA, rA, Esb[0]),
                       (h1, 64, psX, (rBig[2], rBig[3]), psB, rB, Esb[1])]

                def sc_pair(g):
                    for j in range(2):
                        kc = g * 2 + j
                        def fn(e, kc=kc, j=j, qg=qg, hp=hp):
                            e.matmul(
                                psW.ap()[:, j * 512:(j + 1) * 512],
                                Khat.a(hp, (kc * 128, 128), p0=0, pn=64),
                                Qhat.a(hp, (qg * 512, 512), p0=0, pn=64),
                                start=True, stop=True)
                            return e.matmul(
                                psX.ap()[:, j * 512:(j + 1) * 512],
                                Khat.a(hp, (kc * 128, 128), p0=64, pn=64),
                                Qhat.a(hp, (qg * 512, 512), p0=64, pn=64),
                                start=True, stop=True)
                        p.T(fn, reads=[Khat.res, Qhat.res],
                            writes=[rBig[j], rBig[2 + j]],
                            tag=f"sc{b}_{qg}_{hp}_{g}_{j}")

                def sc_group(ci, g):
                    if ci == 0:
                        sc_pair(g)

                def exp_group(ci, g):
                    h, off, pt, prs, _, _, eb = cfg[ci]
                    p.S(lambda e, eb=eb, pt=pt: e.activation(
                            eb.a((0, 1024)), pt.ap(), AF.Exp),
                        reads=[prs[0], prs[1]], writes=[eb.res],
                        tag=f"ex{b}_{qg}_{h}_{g}")

                def ev_group(ci, g):
                    h, off, pt, prs, pacc, racc, eb = cfg[ci]
                    for j in range(2):
                        kc = g * 2 + j
                        p.T(lambda e, h=h, kc=kc, j=j, eb=eb, pacc=pacc:
                            e.matmul(
                                pacc.ap()[0:65, 0:512], Vaug.a(h, kc),
                                eb.a((j * 512, 512)),
                                start=(kc == 0), stop=(kc == 15)),
                            reads=[Vaug.res, eb.res], writes=[racc],
                            tag=f"ev{b}_{qg}_{h}_{kc}")

                for g in range(8):
                    for ci in (0, 1):
                        sc_group(ci, g)
                        exp_group(ci, g)
                        ev_group(ci, g)
                for ci in (0, 1):
                    h, off, pt, prs, pacc, racc, eb = cfg[ci]
                    rd = rden if ci == 0 else rden2
                    rr0 = r64[0] if ci == 0 else r64b[0]
                    rr1 = r64[1] if ci == 0 else r64b[1]
                    p.V(lambda e, pacc=pacc, rd=rd: e.tensor_copy(
                            rd.a((0, 512), p0=0, pn=1),
                            pacc.ap()[64:65, 0:512]),
                        reads=[racc], writes=[rd.res], tag=f"rcp{b}_{qg}_{h}")
                    p.S(lambda e, rd=rd: e.activation(
                            rd.a((0, 512), p0=0, pn=1),
                            rd.a((0, 512), p0=0, pn=1), AF.Ln),
                        reads=[rd.res], writes=[rd.res], tag=f"rc{b}_{qg}_{h}")
                    p.S(lambda e, rd=rd, rr0=rr0: e.activation(
                            rr0.a(p0=0, pn=1), rd.a((0, 512), p0=0, pn=1),
                            AF.Exp, scale=-1.0),
                        reads=[rd.res], writes=[rr0.res], tag=f"rcb{b}_{qg}_{h}")
                    p.T(lambda e, pt=pt, rr0=rr0: e.matmul(
                            pt.ap()[0:64, 0:512], onesb.a((0, 64), p0=0, pn=1),
                            rr0.a(p0=0, pn=1), start=True, stop=True),
                        reads=[onesb.res, rr0.res], writes=[prs[0]],
                        tag=f"rbc{b}_{qg}_{h}")
                    p.S(lambda e, pt=pt, rr1=rr1: e.activation(
                            rr1.a(p0=0, pn=64), pt.ap()[0:64, 0:512], AF.Copy),
                        reads=[prs[0]], writes=[rr1.res], tag=f"rev{b}_{qg}_{h}")
                    p.V(lambda e, h=h, off=off, pacc=pacc, rr1=rr1, hp=hp:
                        e.tensor_tensor(
                            attnfm.a(hp, p0=off, pn=64),
                            pacc.ap()[0:64, 0:512], rr1.a(p0=0, pn=64),
                            ALU.mult),
                        reads=[racc, rr1.res], writes=[attnfm.res],
                        tag=f"at{b}_{qg}_{h}")
            for t4 in range(4):
                ti = b * NCH + qg * 4 + t4
                trow = (qg * 4 + t4) * 128
                xr = xres[t4 % 2]
                p.load(xr.a(), d["x"][b, trow:trow + 128, :], writes=[xr.res],
                       tag=f"xl{b}_{qg}_{t4}")
                for nh in range(2):
                    for kc in range(NFC):
                        p.T(lambda e, kc=kc, nh=nh, t4=t4: e.matmul(
                                psX.ap()[:, nh * 512:(nh + 1) * 512],
                                attnfm.a(kc, (t4 * 128, 128)),
                                wo.a(kc, (nh * 512, 512)),
                                start=(kc == 0), stop=False),
                            reads=[attnfm.res, wo.res],
                            writes=[rBig[2 + nh]],
                            tag=f"wo{b}_{qg}_{t4}_{kc}_{nh}")
                    p.T(lambda e, nh=nh: e.matmul(
                            psX.ap()[:, nh * 512:(nh + 1) * 512],
                            onesb.a((0, 128), p0=0, pn=1),
                            bias_b["bo"].a((nh * 512, 512), p0=0, pn=1),
                            start=False, stop=True),
                        reads=[onesb.res, bias_b["bo"].res],
                        writes=[rBig[2 + nh]], tag=f"wob{b}_{qg}_{t4}_{nh}")
                of = ofin[t4 % 2]
                p.V(lambda e, xr=xr, of=of: e.tensor_tensor(
                        of.a(), psX.ap(), xr.a(), ALU.add),
                    reads=[rBig[2], rBig[3], xr.res], writes=[of.res],
                    tag=f"oadd{b}_{qg}_{t4}")
                p.store(soutp[ti], of.a(), reads=[of.res],
                        writes=[routp[ti]], tag=f"ost{b}_{qg}_{t4}")
        AB.free(Vaug, Esb[0], Esb[1], attnfm, r64[0], r64[1], r64b[0],
                r64b[1], Qhat, Khat, wo)
        AF32.free(rden, rden2, xres[0], xres[1], ofin[0], ofin[1])
    AF32.free(Mf, lld, nl, ek2, eend, lTr, lTsb, ftc[0], ftc[1])
    if upto in ("b", "c", "d"):
        return
    _phase_a2(nc, d, out_t, p, AF32, AB, PS, SC, env)


def _phase_a2(nc, d, out_t, p, AF32, AB, PS, SC, env):
    psW, psX, psA, psB = PS["psW"], PS["psX"], PS["psA"], PS["psB"]
    psTa, psTb = PS["psTa"], PS["psTb"]
    rBig, rA, rB, rTa, rTb = PS["rBig"], PS["rA"], PS["rB"], PS["rTa"], PS["rTb"]
    bigap = PS["bigap"]
    sxq, shq, soutp = SC["sxq"], SC["shq"], SC["soutp"]
    rxq, rhq, routp = SC["rxq"], SC["rhq"], SC["routp"]
    identb = env["identb"]
    mk_qscr, free_qscr = env["mk_qscr"], env["free_qscr"]

    IG = [(0, 512), (512, 512), (1024, 512), (1536, 512), (2048, 512),
          (2560, 256)]
    AF32.free(env["logg"])

    wqg = AB.alloc("wqg", [NFC, FFI])
    wqu = AB.alloc("wqu", [NFC, FFI])
    xq = [AB.alloc(f"a2xq{i}", [NFC, 128]) for i in range(2)]
    hqr = [AB.alloc(f"hq{i}", [FFI]) for i in range(2)]
    qtile = hqr[0]
    hqs = [AB.alloc(f"hqs{i}", [4, 128]) for i in range(2)]
    qs2 = [mk_qscr(1408, "qw_"), mk_qscr(1408, "qx_")]
    qs = qs2[0]
    ftw = [AF32.alloc(f"ftw{i}", [1024]) for i in range(2)]

    def prep_w_rows(dsrc, wdst, nrt, rowlen, tag):
        for rt in range(nrt):
            ft = ftw[rt % 2]
            p.load(ft.a((0, rowlen)), dsrc[rt * 128:(rt + 1) * 128, :],
                   writes=[ft.res], tag=f"{tag}l{rt}")
            for qi in range((rowlen + 1407) // 1408):
                off = qi * 1408
                ln = min(1408, rowlen - off)
                qv = qs2[rt % 2]
                qv["xres"] = ft.res
                qv["outres"] = qtile.res
                _quant_emit(p, nc, None, ft.a((off, ln)),
                            qtile.a((off, ln)), ln // 16, qv,
                            tag=f"{tag}q{rt}_{qi}_")
            nch = rowlen // 128
            for ch in range(nch):
                pt = psTa if ch % 2 == 0 else psTb
                rt_ = rTa if ch % 2 == 0 else rTb
                p.T(lambda e, ch=ch, pt=pt: e.transpose(
                        pt.ap()[:, 0:128], qtile.a((ch * 128, 128)), identb.a()),
                    reads=[qtile.res, identb.res], writes=[rt_],
                    tag=f"{tag}t{rt}_{ch}")
                p.S(lambda e, ch=ch, rt=rt, pt=pt, wdst=wdst: e.copy(
                        wdst.a(ch, (rt * 128, 128)), pt.ap()[:, 0:128]),
                    reads=[rt_], writes=[wdst.res], tag=f"{tag}e{rt}_{ch}")
    prep_w_rows(d["wgate"], wqg, NIC, 1024, "wg_")
    prep_w_rows(d["wup"], wqu, NIC, 1024, "wu_")
    AF32.free(ftw[0], ftw[1])

    hbr = [AF32.alloc(f"hbuf{i}", [FFI]) for i in range(2)]
    sgr = [AF32.alloc("sg0", [512])]
    qh = qs
    for ti in range(NT128):
        xt = xq[ti % 2]
        hb, sg, hq = hbr[ti % 2], sgr[0], hqr[ti % 2]
        p.load(xt.a(), sxq[ti], reads=[rxq[ti]], writes=[xt.res],
               tag=f"a2x{ti}")
        for gi, (i0, ilen) in enumerate(IG):
            for kc in range(NFC):
                p.T(lambda e, kc=kc, i0=i0, ilen=ilen, xt=xt: e.matmul(
                        bigap(0)[:, 0:ilen], xt.a(kc), wqg.a(kc, (i0, ilen)),
                        start=(kc == 0), stop=(kc == NFC - 1)),
                    reads=[xt.res, wqg.res], writes=[rBig[0]],
                    tag=f"gt{ti}_{gi}_{kc}")
                p.T(lambda e, kc=kc, i0=i0, ilen=ilen, xt=xt: e.matmul(
                        bigap(1)[:, 0:ilen], xt.a(kc), wqu.a(kc, (i0, ilen)),
                        start=(kc == 0), stop=(kc == NFC - 1)),
                    reads=[xt.res, wqu.res], writes=[rBig[1]],
                    tag=f"up{ti}_{gi}_{kc}")
            p.S(lambda e, ilen=ilen, sg=sg: e.activation(sg.a((0, ilen)),
                                                  bigap(0)[:, 0:ilen], AF.Silu),
                reads=[rBig[0]], writes=[sg.res], tag=f"si{ti}_{gi}")
            p.V(lambda e, i0=i0, ilen=ilen, hb=hb, sg=sg: e.tensor_tensor(
                    hb.a((i0, ilen)), sg.a((0, ilen)), bigap(1)[:, 0:ilen],
                    ALU.mult),
                reads=[sg.res, rBig[1]], writes=[hb.res], tag=f"hm{ti}_{gi}")
        for qi in range(2):
            off, ln = qi * 1408, 1408
            qv = qs2[qi]
            qv["xres"] = hb.res
            qv["outres"] = hq.res
            _quant_emit(p, nc, None, hb.a((off, ln)), hq.a((off, ln)),
                        ln // 16, qv, tag=f"qh{ti}_{qi}_")
        ch = 0
        grp = 0
        while ch < NIC:
            n4 = min(4, NIC - ch)
            hs = hqs[grp % 2]
            for q2 in range(n4):
                pt = psTa if q2 % 2 == 0 else psTb
                rt_ = rTa if q2 % 2 == 0 else rTb
                p.T(lambda e, ch=ch, q2=q2, pt=pt, hq=hq: e.transpose(
                        pt.ap()[:, 0:128], hq.a(((ch + q2) * 128, 128)),
                        identb.a()),
                    reads=[hq.res, identb.res], writes=[rt_],
                    tag=f"ht{ti}_{ch + q2}")
                p.S(lambda e, q2=q2, pt=pt, hs=hs: e.copy(
                        hs.a(q2), pt.ap()[:, 0:128]),
                    reads=[rt_], writes=[hs.res], tag=f"he{ti}_{ch + q2}")
            p.store(shq[ti][:, ch:ch + n4, :], hs.a((0, n4)), reads=[hs.res],
                    writes=[rhq[ti]], tag=f"hs{ti}_{grp}")
            ch += n4
            grp += 1
    AB.free(wqg, wqu, xq[0], xq[1], hqr[0], hqr[1], hqs[0], hqs[1])
    AF32.free(hbr[0], hbr[1], sgr[0])

    wqd = AB.alloc("wqd", [NIC, 1024])
    ftd = AF32.alloc("ftd", [FFI])
    for rt in range(NFC):
        p.load(ftd.a(), d["wdown"][rt * 128:(rt + 1) * 128, :],
               writes=[ftd.res], tag=f"wd_l{rt}")
        qdt2 = AB.alloc(f"qdt2_{rt}", [FFI]) if rt == 0 else env["qdt2"]
        env["qdt2"] = qdt2
        for qi in range(2):
            off, ln = qi * 1408, 1408
            qv = qs2[qi]
            qv["xres"] = ftd.res
            qv["outres"] = qdt2.res
            _quant_emit(p, nc, None, ftd.a((off, ln)), qdt2.a((off, ln)),
                        ln // 16, qv, tag=f"wd_q{rt}_{qi}_")
        for ch in range(NIC):
            pt = psTa if ch % 2 == 0 else psTb
            rt_ = rTa if ch % 2 == 0 else rTb
            p.T(lambda e, ch=ch, pt=pt, qdt2=qdt2: e.transpose(
                    pt.ap()[:, 0:128], qdt2.a((ch * 128, 128)), identb.a()),
                reads=[qdt2.res, identb.res], writes=[rt_],
                tag=f"wd_t{rt}_{ch}")
            p.S(lambda e, ch=ch, rt=rt, pt=pt: e.copy(
                    wqd.a(ch, (rt * 128, 128)), pt.ap()[:, 0:128]),
                reads=[rt_], writes=[wqd.res], tag=f"wd_e{rt}_{ch}")
    free_qscr(qs2[0])
    free_qscr(qs2[1])
    AF32.free(ftd)

    hqt = [AB.alloc(f"hqt{i}", [NIC, 128]) for i in range(2)]
    opt = [AF32.alloc(f"opt{i}", [1024]) for i in range(2)]
    fin = [AF32.alloc(f"fin{i}", [1024]) for i in range(2)]
    for ti in range(NT128):
        b, trow = ti // NCH, (ti % NCH) * 128
        ht = hqt[ti % 2]
        ot = opt[ti % 2]
        p.load(ht.a(), shq[ti], reads=[rhq[ti]], writes=[ht.res],
               tag=f"p2h{ti}")
        p.load(ot.a(), soutp[ti], reads=[routp[ti]], writes=[ot.res],
               tag=f"p2o{ti}")
        for nh in range(2):
            for ic in range(NIC):
                p.T(lambda e, ic=ic, nh=nh, ht=ht: e.matmul(
                        bigap(nh), ht.a(ic), wqd.a(ic, (nh * 512, 512)),
                        start=(ic == 0), stop=(ic == NIC - 1)),
                    reads=[ht.res, wqd.res], writes=[rBig[nh]],
                    tag=f"dn{ti}_{ic}_{nh}")
        fi = fin[ti % 2]
        p.V(lambda e, ot=ot, fi=fi: e.tensor_tensor(
                fi.a(), psW.ap(), ot.a(), ALU.add),
            reads=[rBig[0], rBig[1], ot.res], writes=[fi.res],
            tag=f"fadd{ti}")
        p.store(out_t[b, trow:trow + 128, :], fi.a(), reads=[fi.res],
                tag=f"out{ti}")


_NC_CACHE = {}


def _get_nc():
    if "nc" not in _NC_CACHE:
        nc, stack = build_program(upto="all")
        _NC_CACHE["nc"] = nc
        _NC_CACHE["stack"] = stack
    return _NC_CACHE["nc"]


def _consts():
    i128 = np.eye(128, dtype=np.float32)
    k = np.arange(128)
    import ml_dtypes
    BF = ml_dtypes.bfloat16
    pf = np.zeros((16, 1024), np.float32)
    for h in range(16):
        pf[h, h * 64:(h + 1) * 64] = 1.0
    pfT = np.zeros((128, NFC * 16), np.float32)
    for fc in range(8):
        for h2 in range(2):
            pfT[h2 * 64:(h2 + 1) * 64, fc * 16 + 2 * fc + h2] = 1.0
    return {
        "identb": i128.astype(BF), "identf": i128,
        "triLneg": -((k[:, None] <= k[None, :]).astype(np.float32)),
        "triUneg": -((k[:, None] > k[None, :]).astype(np.float32)),
        "maskLowNeg": ((k[:, None] > k[None, :]).astype(np.float32)
                       * np.float32(-1e30)).astype(BF),
        "pfull": pf.astype(BF), "pfullT": pfT.astype(BF),
        "onesb": np.ones((1, 512), BF), "onesf": np.ones((1, 128), np.float32),
    }


def kernel(**inputs):
    nc = _get_nc()
    consts = _consts()
    xfull = np.ascontiguousarray(np.asarray(inputs["L0_attn_x"], np.float32))
    shared = {
        "wsqT": np.ascontiguousarray(inputs["L0_attn_Wsq"].T),
        "wskT": np.ascontiguousarray(inputs["L0_attn_Wsk"].T),
        "wsvT": np.ascontiguousarray(inputs["L0_attn_Wsv"].T),
        "wvT": np.ascontiguousarray(inputs["L0_attn_Wv"].T),
        "wqT": np.ascontiguousarray(inputs["L0_attn_Wq"].T),
        "wkT": np.ascontiguousarray(inputs["L0_attn_Wk"].T),
        "woT": np.ascontiguousarray(inputs["L0_attn_Wo"].T),
        "wgT": np.ascontiguousarray(inputs["L0_attn_Wg"].T),
        "bsq": np.ascontiguousarray(inputs["L0_attn_bsq"]).reshape(1, -1),
        "bsk": np.ascontiguousarray(inputs["L0_attn_bsk"]).reshape(1, -1),
        "bsv": np.ascontiguousarray(inputs["L0_attn_bsv"]).reshape(1, -1),
        "bq": np.ascontiguousarray(inputs["L0_attn_bq"]).reshape(1, -1),
        "bk": np.ascontiguousarray(inputs["L0_attn_bk"]).reshape(1, -1),
        "bv": np.ascontiguousarray(inputs["L0_attn_bv"]).reshape(1, -1),
        "bo": np.ascontiguousarray(inputs["L0_attn_bo"]).reshape(1, -1),
        "bg": np.ascontiguousarray(inputs["L0_attn_bg"]).reshape(1, -1),
        "temp16": np.ascontiguousarray(
            np.asarray(inputs["L0_attn_temp"], np.float32).reshape(16, 1)),
        "wgate": np.ascontiguousarray(inputs["L0_ffn_w_gate"]),
        "wup": np.ascontiguousarray(inputs["L0_ffn_w_up"]),
        "wdown": np.ascontiguousarray(inputs["L0_ffn_w_down"]),
    }
    shared = {k: np.ascontiguousarray(np.asarray(v, np.float32))
              for k, v in shared.items()}
    shared.update(consts)
    in_maps = []
    for core in range(NCORES):
        m = dict(shared)
        m["x"] = np.ascontiguousarray(xfull[core * NB:(core + 1) * NB])
        in_maps.append(m)
    res = run_bass_kernel_spmd(nc, in_maps, core_ids=list(range(NCORES)))
    out = np.concatenate([r["out"] for r in res.results], axis=0)
    return out.astype(np.float32)


# version 26
# speedup vs baseline: 1.0022x; 1.0022x over previous
import numpy as np
from contextlib import ExitStack

import concourse.bass as bass
import concourse.mybir as mybir
from concourse.bass_utils import run_bass_kernel_spmd

F32 = mybir.dt.float32
BF16 = mybir.dt.bfloat16
AF = mybir.ActivationFunctionType
ALU = mybir.AluOpType
AX = mybir.AxisListType

B, T, DIM, H = 16, 2048, 1024, 16
HD = DIM // H
FFI = 2816
NCORES = 8
NB = B // NCORES
TOK = NB * T
NT128 = TOK // 128
NCH = T // 128
NFC = DIM // 128
NIC = FFI // 128
LN_EPS = 1e-5

FP4_MID = [0.25, 0.75, 1.25, 1.75, 2.5, 3.5, 5.0]
FP4_W = [0.5, 0.5, 0.5, 0.5, 1.0, 1.0, 2.0]

SAME_ENGINE_WAITS = False
DETECT_RACES = False


class Res:
    __slots__ = ("name", "writer", "readers", "preds")

    def __init__(self, name=""):
        self.name = name
        self.writer = None
        self.readers = []
        self.preds = []


class Op:
    __slots__ = ("eng", "fn", "ticket", "deps", "need_inc", "tag")

    def __init__(self, eng, fn, tag):
        self.eng = eng
        self.fn = fn
        self.ticket = None
        self.deps = []
        self.need_inc = False
        self.tag = tag


class Prog:
    NLANES = 8

    def __init__(self):
        self.warm_ap = None
        self.op_start = {}
        self.op_finish = {}
        self.ops = {k: [] for k in ("T", "V", "S", "G", "Y")}
        self.dma_count = {"DY": 0, "DG": 0}
        self.lane_last = {}
        self.lane_uses = {}
        self.n_ops = 0

    def emit(self, eng, fn, reads=(), writes=(), tag=""):
        op = Op(eng, fn, tag)
        seen = set()

        def dep(x):
            if x is not None and id(x) not in seen and x is not op:
                seen.add(id(x))
                op.deps.append(x)

        for r in reads:
            dep(r.writer)
        for r in writes:
            dep(r.writer)
            for rd in r.readers:
                dep(rd)
            for p in r.preds:
                dep(p)
        issue_eng = {"DY": "Y", "DG": "G"}.get(eng, eng)
        if eng in ("DY", "DG"):
            lane = self.dma_count[eng] % self.NLANES
            self.dma_count[eng] += 1
            key = (eng, lane)
            dep(self.lane_last.get(key))
            self.lane_last[key] = op
            self.lane_uses[key] = self.lane_uses.get(key, 0) + 1
            op.ticket = (f"{eng}{lane}", 16 * self.lane_uses[key])
        self.ops[issue_eng].append(op)
        self.n_ops += 1
        for r in reads:
            r.readers.append(op)
        for r in writes:
            r.writer = op
            r.readers = []
            r.preds = []
        return op

    def T(self, fn, reads=(), writes=(), tag=""):
        return self.emit("T", fn, reads, writes, tag)

    def V(self, fn, reads=(), writes=(), tag=""):
        return self.emit("V", fn, reads, writes, tag)

    def S(self, fn, reads=(), writes=(), tag=""):
        return self.emit("S", fn, reads, writes, tag)

    def G(self, fn, reads=(), writes=(), tag=""):
        return self.emit("G", fn, reads, writes, tag)

    def load(self, dst_ap, src_ap, reads=(), writes=(), tag="ld"):
        return self.emit("DY", lambda e: e.dma_start(dst_ap, src_ap),
                         reads, writes, tag)

    def store(self, dst_ap, src_ap, reads=(), writes=(), tag="st"):
        return self.emit("DG", lambda e: e.dma_start(dst_ap, src_ap),
                         reads, writes, tag)

    COST = {"T": 0.25, "V": 0.9, "S": 0.9, "G": 0.7, "Y": 0.1}

    def schedule(self):
        import heapq
        eng_of = {}
        idx_of = {}
        nsucc_sched = {}
        for eng, lst in self.ops.items():
            for i, op in enumerate(lst):
                eng_of[id(op)] = eng
                idx_of[id(op)] = i
        npred = {}
        succs = {}
        for eng, lst in self.ops.items():
            for op in lst:
                npred[id(op)] = len(op.deps)
                for dp in op.deps:
                    succs.setdefault(id(dp), []).append(op)
        finish = {}
        clock = {e: 0.0 for e in self.ops}
        ready = {e: [] for e in self.ops}
        new_order = {e: [] for e in self.ops}
        n_total = sum(len(v) for v in self.ops.values())

        def push_ready(op):
            av = max((finish[id(dp)] for dp in op.deps), default=0.0)
            e = eng_of[id(op)]
            heapq.heappush(ready[e], (av, idx_of[id(op)], id(op), op))

        for eng, lst in self.ops.items():
            for op in lst:
                if npred[id(op)] == 0:
                    push_ready(op)
        done = 0
        while done < n_total:
            best, bh = None, None
            for e, h in ready.items():
                if not h:
                    continue
                av = h[0][0]
                start = max(clock[e], av)
                key = (start, av)
                if best is None or key < bh:
                    best, bh = e, key
            assert best is not None, "scheduler deadlock (cycle in deps)"
            av, _, _, op = heapq.heappop(ready[best])
            e = best
            dma = op.ticket is not None and op.ticket[0][0] == "D"
            dur = 2.5 if dma else self.COST[e]
            start = max(clock[e], av)
            fin = start + dur
            clock[e] = start + (0.15 if dma else dur)
            finish[id(op)] = fin
            self.op_start[id(op)] = start
            self.op_finish[id(op)] = fin
            new_order[e].append(op)
            done += 1
            for su in succs.get(id(op), ()):
                npred[id(su)] -= 1
                if npred[id(su)] == 0:
                    push_ready(su)
        self.ops = new_order

    def _insert_warmers(self):
        if self.warm_ap is None:
            return
        wap = self.warm_ap
        new_T = []
        prev_end = 0.0
        nfill = 0
        for op in self.ops["T"]:
            st = self.op_start.get(id(op), prev_end)
            if st - prev_end > 2.0 and nfill < 4000:
                seen = set()
                anc = []
                frontier = list(op.deps)
                depth = 0
                while frontier and depth < 16:
                    nxt = []
                    for a in frontier:
                        if id(a) in seen:
                            continue
                        seen.add(id(a))
                        f = self.op_finish.get(id(a))
                        if f is not None and prev_end + 0.8 < f < st - 0.2:
                            anc.append((f, a))
                        nxt.extend(a.deps)
                    frontier = nxt
                    depth += 1
                anc.sort(key=lambda x: x[0])
                last_t = prev_end
                for f, a in anc:
                    if f >= last_t + 1.2:
                        fo = Op("T", lambda e: e.ldweights(wap), "warm")
                        fo.deps = [a]
                        new_T.append(fo)
                        nfill += 1
                        last_t = f
            new_T.append(op)
            prev_end = max(prev_end, self.op_finish.get(id(op), prev_end))
        self.ops["T"] = new_T

    def lower(self, nc, ec):
        self.schedule()
        self._insert_warmers()
        for eng in self.ops:
            for op in self.ops[eng]:
                for d in op.deps:
                    d.need_inc = True
        for eng in ("T", "V", "S", "G"):
            c = 0
            for op in self.ops[eng]:
                if op.ticket is not None:
                    continue
                if op.need_inc:
                    c += 1
                    op.ticket = (eng, c)

        sem_names = ["T", "V", "S", "G"]
        for q in ("DY", "DG"):
            for ln in range(self.NLANES):
                sem_names.append(f"{q}{ln}")
        sems = {k: ec(nc.semaphore("sem" + k)) for k in sem_names}

        def runner(key):
            def run(h):
                waited = {}
                for op in self.ops[key]:
                    for d in op.deps:
                        if d.ticket is None:
                            continue
                        sname, val = d.ticket
                        if sname == key and not SAME_ENGINE_WAITS:
                            continue
                        if waited.get(sname, -1) >= val:
                            continue
                        h.wait_ge(sems[sname], val)
                        waited[sname] = val
                    inst = op.fn(h)
                    if op.ticket is not None:
                        sname, val = op.ticket
                        inst.then_inc(sems[sname],
                                      16 if sname[0] == "D" else 1)
                q = {"G": "DG", "Y": "DY"}.get(key)
                if q:
                    for ln in range(self.NLANES):
                        u = self.lane_uses.get((q, ln), 0)
                        if u:
                            h.wait_ge(sems[f"{q}{ln}"], 16 * u)
            return run

        block = ec(nc.Block())
        block.tensor(runner("T"))
        block.vector(runner("V"))
        block.scalar(runner("S"))
        block.gpsimd(runner("G"))
        block.sync(runner("Y"))


class Arena:

    def __init__(self, nc, name, per_part_elems, dtype):
        self.t = nc.alloc_sbuf_tensor(name, [128, per_part_elems], dtype)
        self.cap = per_part_elems
        self.cursor = 0
        self.live = []
        self.dead = []
        self.name_start = {}

    def alloc(self, name, shape):
        n = int(np.prod(shape))
        ivs = sorted((s, e) for (s, e, _) in self.live)
        start = None
        pos = 0
        for (s0, e0) in ivs:
            cand = (pos + 31) & ~31
            if cand + n <= s0:
                start = cand
                break
            pos = max(pos, e0)
        if start is None:
            cand = (pos + 31) & ~31
            assert cand + n <= self.cap, \
                f"arena {self.t.name} overflow: {name}: {cand + n} > {self.cap}"
            start = cand
        end = start + n
        prev = self.name_start.setdefault(name, start)
        assert prev == start, \
            f"region {name} realloc moved: {prev} -> {start} (closure hazard)"
        reg = Region(self, name, start, list(shape))
        for (s0, e0, r0) in self.dead:
            if s0 < end and start < e0:
                if r0.res.writer is not None:
                    reg.res.preds.append(r0.res.writer)
                reg.res.preds.extend(r0.res.readers)
        self.live.append((start, end, reg))
        return reg

    def free(self, *regs):
        ids = {id(r) for r in regs}
        keep = []
        for (s, e, r) in self.live:
            (self.dead if id(r) in ids else keep).append((s, e, r))
        self.live = keep


class Region:

    def __init__(self, arena, name, start, shape):
        self.arena = arena
        self.name = name
        self.start = start
        self.shape = shape
        self.res = Res(name)
        st = []
        acc = 1
        for s in reversed(shape):
            st.append(acc)
            acc *= s
        self.strides = list(reversed(st))

    def a(self, *idx, p0=0, pn=128):
        pat = [[self.arena.cap, 128]]
        base = 0
        for d, s in enumerate(self.shape):
            sel = idx[d] if d < len(idx) else None
            if sel is None:
                pat.append([self.strides[d], s])
            elif isinstance(sel, int):
                base += sel * self.strides[d]
            else:
                st, cnt = sel
                base += st * self.strides[d]
                pat.append([self.strides[d], cnt])
        ap = bass.AP(self.arena.t, self.start + base, pat)
        if p0 or pn != 128:
            ap = ap[p0:p0 + pn]
        return ap


def _quant_emit(p, nc, ctx, xf32_ap, out_bf_ap, nblk, scr, tag=""):
    N = nblk * 16
    bm, s, r, sbf = scr["bm"], scr["s"], scr["r"], scr["sbf"]
    an, aab, sgn = scr["an"], scr["aab"], scr["sgn"]
    c = scr["c"]; g = scr["g"]
    xres = scr["xres"]; outres = scr["outres"]
    blk = lambda ap: ap

    x3 = xf32_ap.rearrange("p (nb k) -> p nb k", k=16)
    p.V(lambda e: e.tensor_reduce(bm.a((0, nblk)), x3, AX.X, ALU.max,
                                  apply_absolute_value=True),
        reads=[xres], writes=[bm.res], tag=tag + "bm")
    p.V(lambda e: e.tensor_scalar(s.a((0, nblk)), bm.a((0, nblk)),
                                  1.0 / 6.0, 1e-12, ALU.mult, ALU.max),
        reads=[bm.res], writes=[s.res], tag=tag + "s")
    p.V(lambda e: e.reciprocal(r.a((0, nblk)), s.a((0, nblk))),
        reads=[s.res], writes=[r.res], tag=tag + "r")
    p.V(lambda e: e.tensor_copy(sbf.a((0, nblk)), s.a((0, nblk))),
        reads=[s.res], writes=[sbf.res], tag=tag + "sbf")
    rbc = r.a((0, nblk)).broadcast_to([128, nblk, 16])
    p.V(lambda e: e.tensor_tensor(an.a((0, N)).rearrange("p (nb k) -> p nb k", k=16),
                                  x3, rbc, ALU.mult),
        reads=[xres, r.res], writes=[an.res], tag=tag + "an")
    p.S(lambda e: e.activation(aab.a((0, N)), an.a((0, N)), AF.Abs),
        reads=[an.res], writes=[aab.res], tag=tag + "aab")
    p.S(lambda e: e.activation(sgn.a((0, N)), an.a((0, N)), AF.Sign),
        reads=[an.res], writes=[sgn.res], tag=tag + "sgn")
    def cmp(j, dst):
        p.V(lambda e, j=j, dst=dst: e.tensor_scalar(dst.a((0, N)), aab.a((0, N)),
                                                    FP4_MID[j], FP4_W[j],
                                                    ALU.is_gt, ALU.mult),
            reads=[aab.res], writes=[dst.res], tag=tag + f"c{j}")
    cmp(0, c[0]); cmp(1, c[1])
    p.V(lambda e: e.tensor_tensor(g[0].a((0, N)), c[0].a((0, N)), c[1].a((0, N)), ALU.add),
        reads=[c[0].res, c[1].res], writes=[g[0].res], tag=tag + "g0")
    cmp(2, c[0]); cmp(3, c[1])
    p.V(lambda e: e.tensor_tensor(g[1].a((0, N)), c[0].a((0, N)), c[1].a((0, N)), ALU.add),
        reads=[c[0].res, c[1].res], writes=[g[1].res], tag=tag + "g1")
    cmp(4, c[0]); cmp(5, c[1])
    p.V(lambda e: e.tensor_tensor(g[2].a((0, N)), c[0].a((0, N)), c[1].a((0, N)), ALU.add),
        reads=[c[0].res, c[1].res], writes=[g[2].res], tag=tag + "g2")
    cmp(6, c[0])
    p.V(lambda e: e.tensor_tensor(g[0].a((0, N)), g[0].a((0, N)), g[1].a((0, N)), ALU.add),
        reads=[g[0].res, g[1].res], writes=[g[0].res], tag=tag + "g3")
    p.V(lambda e: e.tensor_tensor(g[2].a((0, N)), g[2].a((0, N)), c[0].a((0, N)), ALU.add),
        reads=[g[2].res, c[0].res], writes=[g[2].res], tag=tag + "g4")
    p.V(lambda e: e.tensor_tensor(g[0].a((0, N)), g[0].a((0, N)), g[2].a((0, N)), ALU.add),
        reads=[g[0].res, g[2].res], writes=[g[0].res], tag=tag + "qsum")
    p.V(lambda e: e.tensor_tensor(g[1].a((0, N)), g[0].a((0, N)), sgn.a((0, N)), ALU.mult),
        reads=[g[0].res, sgn.res], writes=[g[1].res], tag=tag + "qsgn")
    sbc = sbf.a((0, nblk)).broadcast_to([128, nblk, 16])
    p.V(lambda e: e.tensor_tensor(out_bf_ap.rearrange("p (nb k) -> p nb k", k=16),
                                  g[1].a((0, N)).rearrange("p (nb k) -> p nb k", k=16),
                                  sbc, ALU.mult),
        reads=[g[1].res, sbf.res], writes=[outres], tag=tag + "qout")


def build_nc():
    nc = bass.Bass("TRN2", target_bir_lowering=False,
                   detect_race_conditions=DETECT_RACES)

    d = {}
    def din(name, shape, dt=F32):
        d[name] = nc.dram_tensor(name, shape, dt, kind="ExternalInput").ap()
        return d[name]

    x_in = din("x", [NB, T, DIM])
    for w in ("wsqT", "wskT", "wsvT", "wvT", "wqT", "wkT", "woT"):
        din(w, [DIM, DIM])
    din("wgT", [DIM, H])
    for bn in ("bsq", "bsk", "bsv", "bq", "bk", "bv", "bo"):
        din(bn, [1, DIM])
    din("bg", [1, H])
    din("temp16", [H, 1])
    din("wgate", [FFI, DIM])
    din("wup", [FFI, DIM])
    din("wdown", [DIM, FFI])
    din("identb", [128, 128], BF16)
    din("identf", [128, 128], F32)
    din("triLneg", [128, 128], F32)
    din("triUneg", [128, 128], F32)
    din("maskLowNeg", [128, 128], BF16)
    din("pfull", [H, DIM], BF16)
    din("pfullT", [128, NFC * H], BF16)
    din("onesb", [1, 512], BF16)
    din("onesf", [1, 128], F32)

    out_t = nc.dram_tensor("out", [NB, T, DIM], F32, kind="ExternalOutput").ap()

    ssq = nc.dram_tensor("ssq", [NB, NCH, 128, NFC, 128], BF16, kind="Internal").ap()
    ssk = nc.dram_tensor("ssk", [NB, NCH, 128, NFC, 128], BF16, kind="Internal").ap()
    ssv = nc.dram_tensor("ssv", [NB, NCH, 128, DIM], BF16, kind="Internal").ap()
    sV = nc.dram_tensor("sV", [NB, NCH, 128, H, HD], BF16, kind="Internal").ap()
    sxq = nc.dram_tensor("sxq", [NT128, 128, NFC, 128], BF16, kind="Internal").ap()
    shq = nc.dram_tensor("shq", [NT128, 128, NIC, 128], BF16, kind="Internal").ap()
    soutp = nc.dram_tensor("soutp", [NT128, 128, DIM], F32, kind="Internal").ap()
    rsq = [[Res() for _ in range(NCH)] for _ in range(NB)]
    rsk = [[Res() for _ in range(NCH)] for _ in range(NB)]
    rsv = [[Res() for _ in range(NCH)] for _ in range(NB)]
    rV = [[Res() for _ in range(NCH)] for _ in range(NB)]
    rxq = [Res() for _ in range(NT128)]
    rhq = [Res() for _ in range(NT128)]
    routp = [Res() for _ in range(NT128)]

    stack = ExitStack()
    ec = stack.enter_context
    p = Prog()
    AF32 = Arena(nc, "AF", 13568, F32)
    AB = Arena(nc, "ABF", 79264, BF16)
    psW = nc.alloc_psum_tensor("psW", [128, 1024], F32)
    psX = nc.alloc_psum_tensor("psX", [128, 1024], F32)
    psA = nc.alloc_psum_tensor("psA", [128, 512], F32)
    psB = nc.alloc_psum_tensor("psB", [128, 512], F32)
    psTa = nc.alloc_psum_tensor("psTa", [128, 1024], BF16)
    psTb = nc.alloc_psum_tensor("psTb", [128, 1024], BF16)
    rBig = [Res(f"psBig{i}") for i in range(4)]
    rA, rB_, rTa, rTb = Res("psA"), Res("psB"), Res("psTa"), Res("psTb")
    def bigap(i, w=512):
        t = psW if i < 2 else psX
        j = i % 2
        return t.ap()[:, j * 512:j * 512 + w]

    return (nc, d, out_t, stack, ec, p, AF32, AB,
            dict(psW=psW, psX=psX, psA=psA, psB=psB, psTa=psTa, psTb=psTb,
                 rBig=rBig, rA=rA, rB=rB_, rTa=rTa, rTb=rTb, bigap=bigap),
            dict(ssq=ssq, ssk=ssk, ssv=ssv, sV=sV, sxq=sxq, shq=shq,
                 soutp=soutp, rsq=rsq, rsk=rsk, rsv=rsv, rV=rV, rxq=rxq,
                 rhq=rhq, routp=routp))


_DBG = {}


def build_program(upto="all"):
    (nc, d, out_t, stack, ec, p, AF32, AB, PS, SC) = build_nc()
    psW, psX, psA, psB = PS["psW"], PS["psX"], PS["psA"], PS["psB"]
    psTa, psTb = PS["psTa"], PS["psTb"]
    rBig, rA, rB, rTa, rTb = PS["rBig"], PS["rA"], PS["rB"], PS["rTa"], PS["rTb"]
    bigap = PS["bigap"]

    cst = {}
    def c_load(nm, arena, shape, pn=128, src=None):
        reg = arena.alloc(nm, shape)
        p.load(reg.a(p0=0, pn=pn), src if src is not None else d[nm],
               writes=[reg.res], tag="c_" + nm)
        return reg
    identb = c_load("identb", AB, [128])
    p.warm_ap = identb.a()
    identf = c_load("identf", AF32, [128])
    triLneg = c_load("triLneg", AF32, [128])
    triUneg = c_load("triUneg", AF32, [128])
    maskLN = c_load("maskLowNeg", AB, [128])
    pfull = c_load("pfull", AB, [DIM], pn=H)
    pfullT = c_load("pfullT", AB, [NFC, H])
    onesb = c_load("onesb", AB, [512], pn=1)
    onesf = c_load("onesf", AF32, [128], pn=1)
    bias_b = {}
    bf_tmp = AF32.alloc("btmp", [DIM])
    for bn in ("bsq", "bsk", "bsv", "bq", "bk", "bv", "bo", "bg"):
        n = H if bn == "bg" else DIM
        reg = AB.alloc(bn + "_b", [n])
        p.load(bf_tmp.a((0, n), p0=0, pn=1), d[bn], writes=[bf_tmp.res], tag=bn)
        p.V(lambda e, r=reg, n=n: e.tensor_copy(r.a(p0=0, pn=1),
                                                bf_tmp.a((0, n), p0=0, pn=1)),
            reads=[bf_tmp.res], writes=[reg.res], tag=bn + "c")
        bias_b[bn] = reg
    tmp16 = AF32.alloc("tmp16", [1])
    sp16 = AF32.alloc("sp16", [1])
    p.load(tmp16.a(p0=0, pn=H), d["temp16"], writes=[tmp16.res], tag="temp")
    p.S(lambda e: e.activation(tmp16.a(p0=0, pn=H), tmp16.a(p0=0, pn=H), AF.Exp),
        reads=[tmp16.res], writes=[tmp16.res], tag="spt1")
    p.S(lambda e: e.activation(sp16.a(p0=0, pn=H), tmp16.a(p0=0, pn=H),
                               AF.Ln, bias=1.0),
        reads=[tmp16.res], writes=[sp16.res], tag="spt2")

    logg = AF32.alloc("logg", [NT128, H])
    _DBG["logg"] = logg

    wA = {}
    ftmp = [AF32.alloc(f"ftmp{i}", [1024]) for i in range(2)]
    for nm, dn in (("wsq", "wsqT"), ("wsk", "wskT"), ("wsv", "wsvT"),
                   ("wv", "wvT")):
        reg = AB.alloc(nm, [NFC, 1024])
        for kc in range(NFC):
            ft = ftmp[kc % 2]
            p.load(ft.a(), d[dn][kc * 128:(kc + 1) * 128, :],
                   writes=[ft.res], tag=f"{nm}l{kc}")
            p.V(lambda e, r=reg, kc=kc, ft=ft: e.tensor_copy(r.a(kc), ft.a()),
                reads=[ft.res], writes=[reg.res], tag=f"{nm}c{kc}")
        wA[nm] = reg
    wg = AB.alloc("wg", [NFC, H])
    ftg = AF32.alloc("ftg", [NFC, H])
    p.load(ftg.a(), d["wgT"].rearrange("(kc p) h -> p kc h", p=128),
           writes=[ftg.res], tag="wgl")
    p.V(lambda e: e.tensor_copy(wg.a(), ftg.a()), reads=[ftg.res],
        writes=[wg.res], tag="wgc")

    xin = [AF32.alloc(f"xin{i}", [1024]) for i in range(2)]
    xnf = [AF32.alloc(f"xnf{i}", [1024]) for i in range(2)]
    x2s = AF32.alloc("x2s", [1024])
    sm = AF32.alloc("sm", [16])
    ezg = AF32.alloc("ezg", [H])
    xnbf = [AB.alloc(f"xnbf{i}", [1024]) for i in range(2)]
    xnfmr = [AB.alloc(f"xnfm{i}", [NFC, 512]) for i in range(2)]
    xqtm = [AB.alloc(f"xqtm{i}", [1024]) for i in range(2)]
    xqst = [AB.alloc(f"xqst{i}", [NFC, 128]) for i in range(2)]
    stg = {"sq": [AB.alloc("stsq0", [NFC, 512])],
           "sk": [AB.alloc("stsk0", [NFC, 512])]}
    svst = [AB.alloc(f"svst{i}", [1024]) for i in range(2)]
    Vst = [AB.alloc(f"Vst{i}", [1024]) for i in range(2)]
    def mk_qscr(QN, pfx, sgn_share=None):
        if sgn_share is not None:
            sgn_reg = sgn_share
        else:
            sgn_reg = None
        return dict(
            bm=AF32.alloc(pfx + "bm", [QN // 16]), s=AF32.alloc(pfx + "s", [QN // 16]),
            r=AF32.alloc(pfx + "r", [QN // 16]), sbf=AB.alloc(pfx + "sbf", [QN // 16]),
            an=AF32.alloc(pfx + "an", [QN]), aab=AF32.alloc(pfx + "aab", [QN]),
            sgn=(sgn_reg if sgn_reg is not None else AB.alloc(pfx + "sgn", [QN])),
            c=[AB.alloc(pfx + f"c{j}", [QN]) for j in range(2)],
            g=[AB.alloc(pfx + f"g{j}", [QN]) for j in range(3)])

    def free_qscr(q, skip_sgn=False):
        AF32.free(q["bm"], q["s"], q["r"], q["an"], q["aab"])
        AB.free(q["sbf"], *q["c"], *q["g"])
        if not skip_sgn:
            AB.free(q["sgn"])

    _q1 = mk_qscr(1024, "qa_")
    qscr2 = [_q1, _q1]

    ssq, ssk, ssv, sVt = SC["ssq"], SC["ssk"], SC["ssv"], SC["sV"]
    sxq, shq, soutp = SC["sxq"], SC["shq"], SC["soutp"]
    rsq, rsk, rsv, rV = SC["rsq"], SC["rsk"], SC["rsv"], SC["rV"]
    rxq, rhq, routp = SC["rxq"], SC["rhq"], SC["routp"]

    for tt in range(8):
        xnfm = xnfmr[tt % 2]
        b = tt // 4
        for j in range(4):
            ti = tt * 4 + j
            trow = (ti % 16) * 128
            xi = xin[ti % 2]
            p.load(xi.a(), d["x"][b, trow:trow + 128, :], writes=[xi.res],
                   tag=f"x{ti}")
            xn = xnf[ti % 2]
            p.V(lambda e, xi=xi: e.reduce_sum(sm.a((0, 1)), xi.a(), AX.X),
                reads=[xi.res], writes=[sm.res], tag=f"ln_s1_{ti}")
            p.S(lambda e, xi=xi: e.activation(x2s.a(), xi.a(), AF.Square,
                                              accum_out=sm.a((1, 1))),
                reads=[xi.res], writes=[x2s.res, sm.res], tag=f"ln_sq_{ti}")
            p.V(lambda e: e.tensor_scalar(sm.a((2, 1)), sm.a((0, 1)),
                                          1.0 / DIM, None, ALU.mult),
                reads=[sm.res], writes=[sm.res], tag=f"ln_m_{ti}")
            p.V(lambda e: e.tensor_scalar(sm.a((3, 1)), sm.a((1, 1)),
                                          1.0 / DIM, LN_EPS, ALU.mult, ALU.add),
                reads=[sm.res], writes=[sm.res], tag=f"ln_e2_{ti}")
            p.V(lambda e: e.tensor_tensor(sm.a((4, 1)), sm.a((2, 1)),
                                          sm.a((2, 1)), ALU.mult),
                reads=[sm.res], writes=[sm.res], tag=f"ln_m2_{ti}")
            p.V(lambda e: e.tensor_tensor(sm.a((5, 1)), sm.a((3, 1)),
                                          sm.a((4, 1)), ALU.subtract),
                reads=[sm.res], writes=[sm.res], tag=f"ln_var_{ti}")
            p.S(lambda e: e.activation(sm.a((6, 1)), sm.a((5, 1)), AF.Ln),
                reads=[sm.res], writes=[sm.res], tag=f"ln_ln_{ti}")
            p.S(lambda e: e.activation(sm.a((7, 1)), sm.a((6, 1)), AF.Exp,
                                       scale=-0.5),
                reads=[sm.res], writes=[sm.res], tag=f"ln_rs_{ti}")
            p.V(lambda e, xi=xi, xn=xn: e.tensor_scalar(
                    xn.a(), xi.a(), sm.a((2, 1)), sm.a((7, 1)),
                    ALU.subtract, ALU.mult),
                reads=[xi.res, sm.res], writes=[xn.res], tag=f"ln_ap_{ti}")
            xb_ = xnbf[ti % 2]
            p.V(lambda e, xn=xn, xb_=xb_: e.tensor_copy(xb_.a(), xn.a()),
                reads=[xn.res], writes=[xb_.res], tag=f"cst_{ti}")
            for fc in range(NFC):
                p.T(lambda e, fc=fc, xb_=xb_: e.transpose(
                        psTa.ap()[:, fc * 128:(fc + 1) * 128],
                        xb_.a((fc * 128, 128)), identb.a()),
                    reads=[xb_.res, identb.res], writes=[rTa],
                    tag=f"trx_{ti}_{fc}")
            p.S(lambda e, j=j, xnfm=xnfm: e.copy(
                    xnfm.a(None, (j * 128, 128)),
                    psTa.ap().rearrange("p (fc t) -> p fc t", t=128)),
                reads=[rTa], writes=[xnfm.res], tag=f"evx_{ti}")
            xq = xqtm[ti % 2]
            qscr = qscr2[ti % 2]
            qscr["xres"] = xn.res
            qscr["outres"] = xq.res
            _quant_emit(p, nc, None, xn.a(), xq.a(), DIM // 16, qscr,
                        tag=f"qx{ti}_")
            for fc in range(NFC):
                p.T(lambda e, fc=fc, xq=xq: e.transpose(
                        psTb.ap()[:, fc * 128:(fc + 1) * 128],
                        xq.a((fc * 128, 128)), identb.a()),
                    reads=[xq.res, identb.res], writes=[rTb],
                    tag=f"trq_{ti}_{fc}")
            xs = xqst[ti % 2]
            p.S(lambda e, xs=xs: e.copy(
                    xs.a(), psTb.ap().rearrange("p (fc t) -> p fc t", t=128)),
                reads=[rTb], writes=[xs.res], tag=f"evq_{ti}")
            p.store(sxq[ti], xs.a(), reads=[xs.res], writes=[rxq[ti]],
                    tag=f"sxq{ti}")
            for kc in range(NFC):
                p.T(lambda e, kc=kc, j=j, xnfm=xnfm: e.matmul(
                        psA.ap()[:, 0:H], xnfm.a(kc, (j * 128, 128)),
                        wg.a(kc), start=(kc == 0), stop=False),
                    reads=[xnfm.res, wg.res], writes=[rA], tag=f"zg_{ti}_{kc}")
            p.T(lambda e: e.matmul(psA.ap()[:, 0:H],
                                   onesb.a((0, 128), p0=0, pn=1),
                                   bias_b["bg"].a(p0=0, pn=1),
                                   start=False, stop=True),
                reads=[onesb.res, bias_b["bg"].res], writes=[rA],
                tag=f"zg_b_{ti}")
            p.S(lambda e: e.activation(ezg.a(), psA.ap()[:, 0:H], AF.Exp),
                reads=[rA], writes=[ezg.res], tag=f"zg_e_{ti}")
            p.S(lambda e, ti=ti: e.activation(logg.a(ti), ezg.a(), AF.Ln,
                                              bias=1.0),
                reads=[ezg.res], writes=[logg.res], tag=f"zg_l_{ti}")
        c0 = (tt % 4) * 4
        for prn, wn, bn, sres, sdst in (("sq", "wsq", "bsq", rsq, ssq),
                                        ("sk", "wsk", "bsk", rsk, ssk)):
            st = stg[prn][0]
            for ncx in range(NFC):
                bank = ncx % 2
                for kc in range(NFC):
                    p.T(lambda e, kc=kc, ncx=ncx, bank=bank, wn=wn, xnfm=xnfm: e.matmul(
                            bigap(bank), wA[wn].a(kc, (ncx * 128, 128)),
                            xnfm.a(kc), start=(kc == 0), stop=False),
                        reads=[wA[wn].res, xnfm.res], writes=[rBig[bank]],
                        tag=f"{prn}_{tt}_{ncx}_{kc}")
                p.T(lambda e, ncx=ncx, bank=bank, bn=bn: e.matmul(
                        bigap(bank), bias_b[bn].a((ncx * 128, 128), p0=0, pn=1),
                        onesb.a(p0=0, pn=1), start=False, stop=True),
                    reads=[bias_b[bn].res, onesb.res], writes=[rBig[bank]],
                    tag=f"{prn}b_{tt}_{ncx}")
                p.S(lambda e, ncx=ncx, bank=bank, st=st: e.copy(
                        st.a(ncx), bigap(bank)),
                    reads=[rBig[bank]], writes=[st.res], tag=f"{prn}e_{tt}_{ncx}")
            for q in range(4):
                cix = (tt % 4) * 4 + q
                p.store(sdst[tt // 4, cix],
                        st.a(None, (q * 128, 128)),
                        reads=[st.res], writes=[sres[tt // 4][cix]],
                        tag=f"{prn}s_{tt}_{q}")
        for j in range(4):
            ti = tt * 4 + j
            cix = ti % 16
            for prn, wn, bn, stb in (("sv", "wsv", "bsv", svst),
                                     ("V", "wv", "bv", Vst)):
                for kc in range(NFC):
                    for nh in range(2):
                        p.T(lambda e, kc=kc, nh=nh, j=j, wn=wn, xnfm=xnfm: e.matmul(
                                bigap(2 + nh), xnfm.a(kc, (j * 128, 128)),
                                wA[wn].a(kc, (nh * 512, 512)),
                                start=(kc == 0), stop=False),
                            reads=[xnfm.res, wA[wn].res], writes=[rBig[2 + nh]],
                            tag=f"{prn}_{ti}_{kc}_{nh}")
                for nh in range(2):
                    p.T(lambda e, nh=nh, bn=bn: e.matmul(
                            bigap(2 + nh), onesb.a((0, 128), p0=0, pn=1),
                            bias_b[bn].a((nh * 512, 512), p0=0, pn=1),
                            start=False, stop=True),
                        reads=[onesb.res, bias_b[bn].res], writes=[rBig[2 + nh]],
                        tag=f"{prn}b_{ti}_{nh}")
                sb_ = stb[ti % 2]
                p.V(lambda e, sb_=sb_: e.tensor_copy(
                        sb_.a(), psX.ap()),
                    reads=[rBig[2], rBig[3]], writes=[sb_.res],
                    tag=f"{prn}e_{ti}")
                if prn == "sv":
                    p.store(ssv[b, cix], sb_.a(), reads=[sb_.res],
                            writes=[rsv[b][cix]], tag=f"svs_{ti}")
                else:
                    p.store(sVt[b, cix], sb_.a().rearrange(
                                "p (h e) -> p h e", e=HD),
                            reads=[sb_.res], writes=[rV[b][cix]],
                            tag=f"Vs_{ti}")
    AF32.free(xin[0], xin[1], x2s, ftmp[0], ftmp[1], ftg, bf_tmp,
              ezg, xnf[0], xnf[1], tmp16, sm)
    AB.free(wA["wsq"], wA["wsk"], wA["wsv"], wA["wv"], wg,
            xnbf[0], xnbf[1], xnfmr[0], xnfmr[1], xqtm[0], xqtm[1],
            xqst[0], xqst[1],
            stg["sq"][0], stg["sk"][0], svst[0], svst[1], Vst[0], Vst[1],
            bias_b["bsq"], bias_b["bsk"], bias_b["bsv"], bias_b["bv"],
            bias_b["bg"])
    free_qscr(qscr2[0])
    if upto == "a1":
        p.lower(nc, ec)
        return nc, stack
    _phases_bcd(nc, d, out_t, p, AF32, AB, PS, SC, locals(), upto)
    p.lower(nc, ec)
    return nc, stack


def _phases_bcd(nc, d, out_t, p, AF32, AB, PS, SC, env, upto):
    psW, psX, psA, psB = PS["psW"], PS["psX"], PS["psA"], PS["psB"]
    psTa, psTb = PS["psTa"], PS["psTb"]
    rBig, rA, rB, rTa, rTb = PS["rBig"], PS["rA"], PS["rB"], PS["rTa"], PS["rTb"]
    bigap = PS["bigap"]
    ssq, ssk, ssv, sVt = SC["ssq"], SC["ssk"], SC["ssv"], SC["sV"]
    sxq, shq, soutp = SC["sxq"], SC["shq"], SC["soutp"]
    rsq, rsk, rsv, rV = SC["rsq"], SC["rsk"], SC["rsv"], SC["rV"]
    rxq, rhq, routp = SC["rxq"], SC["rhq"], SC["routp"]
    logg = env["logg"]
    identb, identf = env["identb"], env["identf"]
    triLneg, triUneg, maskLN = env["triLneg"], env["triUneg"], env["maskLN"]
    pfull, pfullT = env["pfull"], env["pfullT"]
    onesb, onesf = env["onesb"], env["onesf"]
    bias_b, sp16 = env["bias_b"], env["sp16"]

    ftc = [AF32.alloc(f"ftc{i}", [1024]) for i in range(2)]

    def load_w(nm, dn, tagp):
        reg = AB.alloc(nm, [NFC, 1024])
        for kc in range(NFC):
            ft = ftc[kc % 2]
            p.load(ft.a(), d[dn][kc * 128:(kc + 1) * 128, :], writes=[ft.res],
                   tag=f"{tagp}{nm}l{kc}")
            p.V(lambda e, r=reg, kc=kc, ft=ft: e.tensor_copy(r.a(kc), ft.a()),
                reads=[ft.res], writes=[reg.res], tag=f"{tagp}{nm}c{kc}")
        return reg

    Mf = AF32.alloc("Mf", [H, HD])
    lld = AF32.alloc("lld", [32])
    nl = AF32.alloc("nl", [16])
    ek2 = AF32.alloc("ek2", [16])
    eend = AF32.alloc("eend", [NFC])
    lTr = AF32.alloc("lTr", [H, 128])
    lTsb = AF32.alloc("lTsb", [128])

    for b in range(NB):
        Hseq = AB.alloc("Hseq", [NFC, T])
        _DBG["Hseq"] = Hseq
        sqg = [AB.alloc(f"sqg{i}", [NFC, 128]) for i in range(2)]
        skg = [AB.alloc(f"skg{i}", [NFC, 128]) for i in range(2)]
        svc = [AB.alloc(f"svc{i}", [DIM]) for i in range(2)]
        Mbf = AB.alloc("Mbf", [H, HD])
        qp = AB.alloc("qp", [NFC, 128])
        Dm = [AB.alloc(f"Dm{i}", [128]) for i in range(16)]
        SDm = [AB.alloc(f"SDm{i}", [128]) for i in range(2)]
        elT = AB.alloc("elT", [128])
        ktil = AB.alloc("ktil", [H, HD])
        p.V(lambda e: e.memset(Mf.a(), 0.0), writes=[Mf.res], tag=f"M0_{b}")
        for c in range(NCH):
            sq_, sk_ = sqg[c % 2], skg[c % 2]
            p.load(sq_.a(), ssq[b, c], reads=[rsq[b][c]],
                   writes=[sq_.res], tag=f"Bq{b}_{c}")
            p.load(sk_.a(), ssk[b, c], reads=[rsk[b][c]],
                   writes=[sk_.res], tag=f"Bk{b}_{c}")
            sv_ = svc[c % 2]
            p.load(sv_.a(), ssv[b, c], reads=[rsv[b][c]], writes=[sv_.res],
                   tag=f"Bv{b}_{c}")
            ti = b * NCH + c
            p.T(lambda e, ti=ti: e.matmul(psA.ap()[:, 0:H], triLneg.a(),
                                          logg.a(ti), start=True, stop=True),
                reads=[triLneg.res, logg.res], writes=[rA], tag=f"cums{ti}")
            p.T(lambda e, ti=ti: e.matmul(psA.ap()[:, H:2 * H], triUneg.a(),
                                          logg.a(ti), start=True, stop=True),
                reads=[triUneg.res, logg.res], writes=[rA], tag=f"ldif{ti}")
            p.V(lambda e: e.tensor_copy(lld.a((0, 32)), psA.ap()[:, 0:32]),
                reads=[rA], writes=[lld.res], tag=f"lev{ti}")
            p.V(lambda e: e.tensor_scalar(nl.a(), lld.a((0, 16)), -1.0, None,
                                          ALU.mult),
                reads=[lld.res], writes=[nl.res], tag=f"nl{ti}")
            p.S(lambda e: e.activation(ek2.a(), lld.a((16, 16)), AF.Exp),
                reads=[lld.res], writes=[ek2.res], tag=f"ek2{ti}")
            p.T(lambda e: e.transpose(psB.ap()[0:16, 0:128], lld.a((0, 16)),
                                      identf.a()),
                reads=[lld.res, identf.res], writes=[rB], tag=f"lT{ti}")
            p.S(lambda e: e.activation(elT.a(p0=0, pn=16),
                                       psB.ap()[0:16, 0:128], AF.Exp),
                reads=[rB], writes=[elT.res], tag=f"elT{ti}")
            for h in range(H):
                p.T(lambda e, h=h: e.transpose(psB.ap()[0:1, 128:256],
                                               lld.a((h, 1)), identf.a()),
                    reads=[lld.res, identf.res], writes=[rB], tag=f"lr{ti}_{h}")
                p.V(lambda e, h=h: e.tensor_copy(lTr.a(h, p0=0, pn=1),
                                                 psB.ap()[0:1, 128:256]),
                    reads=[rB], writes=[lTr.res], tag=f"lrc{ti}_{h}")
            for fc in range(NFC):
                p.T(lambda e, fc=fc: e.matmul(
                        bigap(fc // 4)[:, (fc % 4) * 128:(fc % 4) * 128 + 128],
                        pfull.a((fc * 128, 128), p0=0, pn=16),
                        elT.a(p0=0, pn=16), start=True, stop=True),
                    reads=[pfull.res, elT.res], writes=[rBig[fc // 4]],
                    tag=f"eg{ti}_{fc}")
            for fc in range(NFC):
                p.V(lambda e, fc=fc, sq_=sq_: e.tensor_tensor(
                        qp.a(fc), sq_.a(fc),
                        bigap(fc // 4)[:, (fc % 4) * 128:(fc % 4) * 128 + 128],
                        ALU.mult),
                    reads=[sq_.res, rBig[fc // 4]], writes=[qp.res],
                    tag=f"qp{ti}_{fc}")
            p.V(lambda e: e.tensor_copy(
                    eend.a(), bass.AP(psW, 127, [[1024, 128], [128, 8]])),
                reads=[rBig[0], rBig[1]], writes=[eend.res], tag=f"ee{ti}")
            p.V(lambda e: e.tensor_copy(Mbf.a(), Mf.a()),
                reads=[Mf.res], writes=[Mbf.res], tag=f"mc{ti}")
            for h in range(H):
                pban = psA if h % 2 == 0 else psB
                rban = rA if h % 2 == 0 else rB
                p.T(lambda e, h=h, pban=pban: e.matmul(
                        pban.ap()[:, 256:384], onesf.a((0, 128), p0=0, pn=1),
                        lTr.a(h, p0=0, pn=1), start=True, stop=False),
                    reads=[onesf.res, lTr.res], writes=[rban],
                    tag=f"lbc{ti}_{h}")
                p.T(lambda e, pban=pban: e.matmul(
                        pban.ap()[:, 256:384], identb.a(), maskLN.a(),
                        start=False, stop=True),
                    reads=[identb.res, maskLN.res], writes=[rban],
                    tag=f"lbm{ti}_{h}")
                p.S(lambda e, h=h, pban=pban: e.activation(
                        Dm[h].a(), pban.ap()[:, 256:384], AF.Exp,
                        bias=nl.a((h, 1))),
                    reads=[rban, nl.res], writes=[Dm[h].res],
                    tag=f"dex{ti}_{h}")
            for fc in range(NFC):
                p.T(lambda e, fc=fc, sk_=sk_: e.transpose(
                        psTa.ap()[:, fc * 128:(fc + 1) * 128],
                        sk_.a(fc), identb.a()),
                    reads=[sk_.res, identb.res], writes=[rTa],
                    tag=f"ktr{ti}_{fc}")
            for h in range(H):
                fc, off = h // 2, (h % 2) * 64
                p.V(lambda e, h=h, fc=fc, off=off: e.tensor_scalar(
                        ktil.a(h),
                        psTa.ap()[:, fc * 128 + off:fc * 128 + off + 64],
                        ek2.a((h, 1)), None, ALU.mult),
                    reads=[rTa, ek2.res], writes=[ktil.res],
                    tag=f"kts{ti}_{h}")
            for h in range(H):
                fc, off = h // 2, (h % 2) * 64
                sb_ = 2 + h % 2
                hb = h % 2
                if h % 2 == 0:
                    def qkfn(e, fc=fc, sk_=sk_, sq_=sq_):
                        e.matmul(bigap(2)[:, 0:128],
                                 sk_.a(fc, p0=0, pn=64),
                                 sq_.a(fc, p0=0, pn=64),
                                 start=True, stop=True)
                        return e.matmul(bigap(3)[:, 0:128],
                                        sk_.a(fc, p0=64, pn=64),
                                        sq_.a(fc, p0=64, pn=64),
                                        start=True, stop=True)
                    p.T(qkfn, reads=[sk_.res, sq_.res],
                        writes=[rBig[2], rBig[3]], tag=f"qk{ti}_{h}")
                p.V(lambda e, h=h, sb_=sb_: e.scalar_tensor_tensor(
                        SDm[h % 2].a(), bigap(sb_)[:, 0:128], 1.0,
                        Dm[h].a(), ALU.mult, ALU.mult),
                    reads=[rBig[sb_], Dm[h].res], writes=[SDm[h % 2].res],
                    tag=f"sdm{ti}_{h}")
                p.T(lambda e, h=h, hb=hb: e.matmul(
                        bigap(hb)[0:64, 128:256], Mbf.a(h), qp.a(h // 2),
                        start=True, stop=False),
                    reads=[Mbf.res, qp.res], writes=[rBig[hb]],
                    tag=f"hin{ti}_{h}")
                p.T(lambda e, h=h, hb=hb, sv_=sv_: e.matmul(
                        bigap(hb)[0:64, 128:256], sv_.a((h * HD, HD)),
                        SDm[h % 2].a(), start=False, stop=True),
                    reads=[sv_.res, SDm[h % 2].res], writes=[rBig[hb]],
                    tag=f"hsv{ti}_{h}")
                p.V(lambda e, h=h, hb=hb, fc=fc, off=off, c=c: e.tensor_copy(
                        Hseq.a(fc, (c * 128, 128), p0=off, pn=64),
                        bigap(hb)[0:64, 128:256]),
                    reads=[rBig[hb]], writes=[Hseq.res], tag=f"hev{ti}_{h}")
                db = psA if h % 2 == 0 else psB
                rdb = rA if h % 2 == 0 else rB
                p.T(lambda e, h=h, db=db, off=off, sv_=sv_: e.matmul(
                        db.ap()[off:off + 64, 384:448], ktil.a(h),
                        sv_.a((h * HD, HD)), start=True, stop=True,
                        tile_position=(0, off)),
                    reads=[ktil.res, sv_.res], writes=[rdb], tag=f"dlt{ti}_{h}")
                p.V(lambda e, h=h, db=db, off=off, fc=fc:
                    e.scalar_tensor_tensor(
                        Mf.a(h, p0=off, pn=64), Mf.a(h, p0=off, pn=64),
                        eend.a((fc, 1), p0=off, pn=64),
                        db.ap()[off:off + 64, 384:448], ALU.mult, ALU.add),
                    reads=[Mf.res, eend.res, rdb], writes=[Mf.res],
                    tag=f"mup{ti}_{h}")
        AB.free(sqg[0], sqg[1], skg[0], skg[1], svc[0], svc[1], Mbf, qp,
                *Dm, SDm[0], SDm[1], elT, ktil)
        if upto == "b":
            if b == 0:
                AB.free(Hseq)
            continue
        Qhat = AB.alloc("Qhat", [NFC, T])
        Khat = AB.alloc("Khat", [NFC, T])
        _DBG["Qhat"], _DBG["Khat"] = Qhat, Khat
        Rraw = AF32.alloc("Rraw", [NFC, 512])
        sqb = AB.alloc("sqb", [NFC, 512])
        rq = AF32.alloc("rq", [512])
        rqb = AB.alloc("rqb", [512])
        for prn, wn, bn, dst in (("q", "wqT", "bq", Qhat),
                                 ("k", "wkT", "bk", Khat)):
            wqk = load_w("wqk", wn, f"c{b}{prn}")
            for qg in range(4):
                for ncx in range(NFC):
                    bank = ncx % 2
                    for kc in range(NFC):
                        p.T(lambda e, kc=kc, ncx=ncx, bank=bank, qg=qg,
                                   wqk=wqk: e.matmul(
                                bigap(bank), wqk.a(kc, (ncx * 128, 128)),
                                Hseq.a(kc, (qg * 512, 512)),
                                start=(kc == 0), stop=False),
                            reads=[wqk.res, Hseq.res], writes=[rBig[bank]],
                            tag=f"{prn}p{b}_{qg}_{ncx}_{kc}")
                    p.T(lambda e, ncx=ncx, bank=bank, bn=bn: e.matmul(
                            bigap(bank),
                            bias_b[bn].a((ncx * 128, 128), p0=0, pn=1),
                            onesb.a(p0=0, pn=1), start=False, stop=True),
                        reads=[bias_b[bn].res, onesb.res], writes=[rBig[bank]],
                        tag=f"{prn}pb{b}_{qg}_{ncx}")
                    p.V(lambda e, ncx=ncx, bank=bank: e.tensor_copy(
                            Rraw.a(ncx), bigap(bank)),
                        reads=[rBig[bank]], writes=[Rraw.res],
                        tag=f"{prn}ev{b}_{qg}_{ncx}")
                    p.S(lambda e, ncx=ncx: e.activation(
                            sqb.a(ncx), Rraw.a(ncx), AF.Square),
                        reads=[Rraw.res], writes=[sqb.res],
                        tag=f"{prn}sq{b}_{qg}_{ncx}")
                for fc in range(NFC):
                    p.T(lambda e, fc=fc: e.matmul(
                            psA.ap()[0:16, 0:512], pfullT.a(fc), sqb.a(fc),
                            start=(fc == 0), stop=(fc == NFC - 1)),
                        reads=[pfullT.res, sqb.res], writes=[rA],
                        tag=f"{prn}ss{b}_{qg}_{fc}")
                p.V(lambda e: e.tensor_scalar(rq.a(p0=0, pn=16),
                                              psA.ap()[0:16, 0:512],
                                              1e-24, None, ALU.max),
                    reads=[rA], writes=[rq.res], tag=f"{prn}mx{b}_{qg}")
                p.S(lambda e: e.activation(rq.a(p0=0, pn=16),
                                           rq.a(p0=0, pn=16), AF.Ln),
                    reads=[rq.res], writes=[rq.res], tag=f"{prn}ln{b}_{qg}")
                p.S(lambda e: e.activation(rq.a(p0=0, pn=16),
                                           rq.a(p0=0, pn=16), AF.Exp,
                                           scale=-0.5),
                    reads=[rq.res], writes=[rq.res], tag=f"{prn}ex{b}_{qg}")
                if prn == "k":
                    p.V(lambda e: e.tensor_scalar(rq.a(p0=0, pn=16),
                                                  rq.a(p0=0, pn=16),
                                                  sp16.a(p0=0, pn=16), None,
                                                  ALU.mult),
                        reads=[rq.res, sp16.res], writes=[rq.res],
                        tag=f"ksp{b}_{qg}")
                p.V(lambda e: e.tensor_copy(rqb.a(p0=0, pn=16),
                                            rq.a(p0=0, pn=16)),
                    reads=[rq.res], writes=[rqb.res], tag=f"{prn}cb{b}_{qg}")
                for fc in range(NFC):
                    pban = psA if fc % 2 == 0 else psB
                    rban = rA if fc % 2 == 0 else rB
                    p.T(lambda e, fc=fc, pban=pban: e.matmul(
                            pban.ap()[:, 0:512],
                            pfull.a((fc * 128, 128), p0=0, pn=16),
                            rqb.a(p0=0, pn=16), start=True, stop=True),
                        reads=[pfull.res, rqb.res], writes=[rban],
                        tag=f"{prn}rg{b}_{qg}_{fc}")
                    p.V(lambda e, fc=fc, pban=pban, rban=rban, dst=dst, qg=qg:
                        e.tensor_tensor(dst.a(fc, (qg * 512, 512)),
                                        Rraw.a(fc), pban.ap()[:, 0:512],
                                        ALU.mult),
                        reads=[Rraw.res, rban], writes=[dst.res],
                        tag=f"{prn}hm{b}_{qg}_{fc}")
            AB.free(wqk)
        AB.free(sqb, rqb)
        AF32.free(Rraw, rq)
        AB.free(Hseq)
        if upto == "c":
            if b == 0:
                AB.free(Qhat, Khat)
            continue
        wo = load_w("wo", "woT", f"d{b}")
        Vaug = AB.alloc("Vaug", [H, NCH, 65])
        Esb = [AB.alloc(f"Esb{i}", [2048]) for i in range(2)]
        attnfm = AB.alloc("attnfm", [NFC, 512])
        r64 = [AB.alloc(f"r64{i}", [512]) for i in range(2)]
        rden = AF32.alloc("rden", [512])
        rden2 = AF32.alloc("rden2", [512])
        r64b = [AB.alloc(f"r64b{i}", [512]) for i in range(2)]
        xres = [AF32.alloc(f"xres{i}", [1024]) for i in range(2)]
        ofin = [AF32.alloc(f"ofin{i}", [1024]) for i in range(2)]
        p.V(lambda e: e.memset(Vaug.a(None, None, (64, 1)), 1.0),
            writes=[Vaug.res], tag=f"vaug1_{b}")
        for c in range(NCH):
            p.load(Vaug.a(None, c, (0, 64)), sVt[b, c],
                   reads=[rV[b][c]], writes=[Vaug.res], tag=f"VL{b}_{c}")
        for qg in range(4):
            for hp in range(NFC):
                h0, h1 = 2 * hp, 2 * hp + 1
                cfg = [(h0, 0, psW, (rBig[0], rBig[1]), psA, rA, Esb[0]),
                       (h1, 64, psX, (rBig[2], rBig[3]), psB, rB, Esb[1])]

                def sc_pair(g):
                    for j in range(2):
                        kc = g * 2 + j
                        def fn(e, kc=kc, j=j, qg=qg, hp=hp):
                            e.matmul(
                                psW.ap()[:, j * 512:(j + 1) * 512],
                                Khat.a(hp, (kc * 128, 128), p0=0, pn=64),
                                Qhat.a(hp, (qg * 512, 512), p0=0, pn=64),
                                start=True, stop=True)
                            return e.matmul(
                                psX.ap()[:, j * 512:(j + 1) * 512],
                                Khat.a(hp, (kc * 128, 128), p0=64, pn=64),
                                Qhat.a(hp, (qg * 512, 512), p0=64, pn=64),
                                start=True, stop=True)
                        p.T(fn, reads=[Khat.res, Qhat.res],
                            writes=[rBig[j], rBig[2 + j]],
                            tag=f"sc{b}_{qg}_{hp}_{g}_{j}")

                def sc_group(ci, g):
                    if ci == 0:
                        sc_pair(g)

                def exp_group(ci, g):
                    h, off, pt, prs, _, _, eb = cfg[ci]
                    p.S(lambda e, eb=eb, pt=pt: e.activation(
                            eb.a((0, 1024)), pt.ap(), AF.Exp),
                        reads=[prs[0], prs[1]], writes=[eb.res],
                        tag=f"ex{b}_{qg}_{h}_{g}")

                def ev_group(ci, g):
                    h, off, pt, prs, pacc, racc, eb = cfg[ci]
                    for j in range(2):
                        kc = g * 2 + j
                        p.T(lambda e, h=h, kc=kc, j=j, eb=eb, pacc=pacc:
                            e.matmul(
                                pacc.ap()[0:65, 0:512], Vaug.a(h, kc),
                                eb.a((j * 512, 512)),
                                start=(kc == 0), stop=(kc == 15)),
                            reads=[Vaug.res, eb.res], writes=[racc],
                            tag=f"ev{b}_{qg}_{h}_{kc}")

                for g in range(8):
                    for ci in (0, 1):
                        sc_group(ci, g)
                        exp_group(ci, g)
                        ev_group(ci, g)
                for ci in (0, 1):
                    h, off, pt, prs, pacc, racc, eb = cfg[ci]
                    rd = rden if ci == 0 else rden2
                    rr0 = r64[0] if ci == 0 else r64b[0]
                    rr1 = r64[1] if ci == 0 else r64b[1]
                    p.V(lambda e, pacc=pacc, rd=rd: e.tensor_copy(
                            rd.a((0, 512), p0=0, pn=1),
                            pacc.ap()[64:65, 0:512]),
                        reads=[racc], writes=[rd.res], tag=f"rcp{b}_{qg}_{h}")
                    p.S(lambda e, rd=rd: e.activation(
                            rd.a((0, 512), p0=0, pn=1),
                            rd.a((0, 512), p0=0, pn=1), AF.Ln),
                        reads=[rd.res], writes=[rd.res], tag=f"rc{b}_{qg}_{h}")
                    p.S(lambda e, rd=rd, rr0=rr0: e.activation(
                            rr0.a(p0=0, pn=1), rd.a((0, 512), p0=0, pn=1),
                            AF.Exp, scale=-1.0),
                        reads=[rd.res], writes=[rr0.res], tag=f"rcb{b}_{qg}_{h}")
                    p.T(lambda e, pt=pt, rr0=rr0: e.matmul(
                            pt.ap()[0:64, 0:512], onesb.a((0, 64), p0=0, pn=1),
                            rr0.a(p0=0, pn=1), start=True, stop=True),
                        reads=[onesb.res, rr0.res], writes=[prs[0]],
                        tag=f"rbc{b}_{qg}_{h}")
                    p.S(lambda e, pt=pt, rr1=rr1: e.activation(
                            rr1.a(p0=0, pn=64), pt.ap()[0:64, 0:512], AF.Copy),
                        reads=[prs[0]], writes=[rr1.res], tag=f"rev{b}_{qg}_{h}")
                    p.V(lambda e, h=h, off=off, pacc=pacc, rr1=rr1, hp=hp:
                        e.tensor_tensor(
                            attnfm.a(hp, p0=off, pn=64),
                            pacc.ap()[0:64, 0:512], rr1.a(p0=0, pn=64),
                            ALU.mult),
                        reads=[racc, rr1.res], writes=[attnfm.res],
                        tag=f"at{b}_{qg}_{h}")
            for t4 in range(4):
                ti = b * NCH + qg * 4 + t4
                trow = (qg * 4 + t4) * 128
                xr = xres[t4 % 2]
                p.load(xr.a(), d["x"][b, trow:trow + 128, :], writes=[xr.res],
                       tag=f"xl{b}_{qg}_{t4}")
                for nh in range(2):
                    for kc in range(NFC):
                        p.T(lambda e, kc=kc, nh=nh, t4=t4: e.matmul(
                                psX.ap()[:, nh * 512:(nh + 1) * 512],
                                attnfm.a(kc, (t4 * 128, 128)),
                                wo.a(kc, (nh * 512, 512)),
                                start=(kc == 0), stop=False),
                            reads=[attnfm.res, wo.res],
                            writes=[rBig[2 + nh]],
                            tag=f"wo{b}_{qg}_{t4}_{kc}_{nh}")
                    p.T(lambda e, nh=nh: e.matmul(
                            psX.ap()[:, nh * 512:(nh + 1) * 512],
                            onesb.a((0, 128), p0=0, pn=1),
                            bias_b["bo"].a((nh * 512, 512), p0=0, pn=1),
                            start=False, stop=True),
                        reads=[onesb.res, bias_b["bo"].res],
                        writes=[rBig[2 + nh]], tag=f"wob{b}_{qg}_{t4}_{nh}")
                of = ofin[t4 % 2]
                p.V(lambda e, xr=xr, of=of: e.tensor_tensor(
                        of.a(), psX.ap(), xr.a(), ALU.add),
                    reads=[rBig[2], rBig[3], xr.res], writes=[of.res],
                    tag=f"oadd{b}_{qg}_{t4}")
                p.store(soutp[ti], of.a(), reads=[of.res],
                        writes=[routp[ti]], tag=f"ost{b}_{qg}_{t4}")
        AB.free(Vaug, Esb[0], Esb[1], attnfm, r64[0], r64[1], r64b[0],
                r64b[1], Qhat, Khat, wo)
        AF32.free(rden, rden2, xres[0], xres[1], ofin[0], ofin[1])
    AF32.free(Mf, lld, nl, ek2, eend, lTr, lTsb, ftc[0], ftc[1])
    if upto in ("b", "c", "d"):
        return
    _phase_a2(nc, d, out_t, p, AF32, AB, PS, SC, env)


def _phase_a2(nc, d, out_t, p, AF32, AB, PS, SC, env):
    psW, psX, psA, psB = PS["psW"], PS["psX"], PS["psA"], PS["psB"]
    psTa, psTb = PS["psTa"], PS["psTb"]
    rBig, rA, rB, rTa, rTb = PS["rBig"], PS["rA"], PS["rB"], PS["rTa"], PS["rTb"]
    bigap = PS["bigap"]
    sxq, shq, soutp = SC["sxq"], SC["shq"], SC["soutp"]
    rxq, rhq, routp = SC["rxq"], SC["rhq"], SC["routp"]
    identb = env["identb"]
    mk_qscr, free_qscr = env["mk_qscr"], env["free_qscr"]

    IG = [(0, 512), (512, 512), (1024, 512), (1536, 512), (2048, 512),
          (2560, 256)]
    AF32.free(env["logg"])

    wqg = AB.alloc("wqg", [NFC, FFI])
    wqu = AB.alloc("wqu", [NFC, FFI])
    xq = [AB.alloc(f"a2xq{i}", [NFC, 128]) for i in range(2)]
    hqr = [AB.alloc(f"hq{i}", [FFI]) for i in range(2)]
    qtile = hqr[0]
    hqs = [AB.alloc(f"hqs{i}", [4, 128]) for i in range(2)]
    qs2 = [mk_qscr(1408, "qw_"), mk_qscr(1408, "qx_")]
    qs = qs2[0]
    ftw = [AF32.alloc(f"ftw{i}", [1024]) for i in range(2)]

    def prep_w_rows(dsrc, wdst, nrt, rowlen, tag):
        for rt in range(nrt):
            ft = ftw[rt % 2]
            p.load(ft.a((0, rowlen)), dsrc[rt * 128:(rt + 1) * 128, :],
                   writes=[ft.res], tag=f"{tag}l{rt}")
            for qi in range((rowlen + 1407) // 1408):
                off = qi * 1408
                ln = min(1408, rowlen - off)
                qv = qs2[rt % 2]
                qv["xres"] = ft.res
                qv["outres"] = qtile.res
                _quant_emit(p, nc, None, ft.a((off, ln)),
                            qtile.a((off, ln)), ln // 16, qv,
                            tag=f"{tag}q{rt}_{qi}_")
            nch = rowlen // 128
            for ch in range(nch):
                pt = psTa if ch % 2 == 0 else psTb
                rt_ = rTa if ch % 2 == 0 else rTb
                p.T(lambda e, ch=ch, pt=pt: e.transpose(
                        pt.ap()[:, 0:128], qtile.a((ch * 128, 128)), identb.a()),
                    reads=[qtile.res, identb.res], writes=[rt_],
                    tag=f"{tag}t{rt}_{ch}")
                p.S(lambda e, ch=ch, rt=rt, pt=pt, wdst=wdst: e.copy(
                        wdst.a(ch, (rt * 128, 128)), pt.ap()[:, 0:128]),
                    reads=[rt_], writes=[wdst.res], tag=f"{tag}e{rt}_{ch}")
    prep_w_rows(d["wgate"], wqg, NIC, 1024, "wg_")
    prep_w_rows(d["wup"], wqu, NIC, 1024, "wu_")
    AF32.free(ftw[0], ftw[1])

    hbr = [AF32.alloc(f"hbuf{i}", [FFI]) for i in range(2)]
    sgr = [AF32.alloc("sg0", [512])]
    qh = qs
    for ti in range(NT128):
        xt = xq[ti % 2]
        hb, sg, hq = hbr[ti % 2], sgr[0], hqr[ti % 2]
        p.load(xt.a(), sxq[ti], reads=[rxq[ti]], writes=[xt.res],
               tag=f"a2x{ti}")
        for gi, (i0, ilen) in enumerate(IG):
            for kc in range(NFC):
                p.T(lambda e, kc=kc, i0=i0, ilen=ilen, xt=xt: e.matmul(
                        bigap(0)[:, 0:ilen], xt.a(kc), wqg.a(kc, (i0, ilen)),
                        start=(kc == 0), stop=(kc == NFC - 1)),
                    reads=[xt.res, wqg.res], writes=[rBig[0]],
                    tag=f"gt{ti}_{gi}_{kc}")
                p.T(lambda e, kc=kc, i0=i0, ilen=ilen, xt=xt: e.matmul(
                        bigap(1)[:, 0:ilen], xt.a(kc), wqu.a(kc, (i0, ilen)),
                        start=(kc == 0), stop=(kc == NFC - 1)),
                    reads=[xt.res, wqu.res], writes=[rBig[1]],
                    tag=f"up{ti}_{gi}_{kc}")
            p.S(lambda e, ilen=ilen, sg=sg: e.activation(sg.a((0, ilen)),
                                                  bigap(0)[:, 0:ilen], AF.Silu),
                reads=[rBig[0]], writes=[sg.res], tag=f"si{ti}_{gi}")
            p.V(lambda e, i0=i0, ilen=ilen, hb=hb, sg=sg: e.tensor_tensor(
                    hb.a((i0, ilen)), sg.a((0, ilen)), bigap(1)[:, 0:ilen],
                    ALU.mult),
                reads=[sg.res, rBig[1]], writes=[hb.res], tag=f"hm{ti}_{gi}")
        for qi in range(2):
            off, ln = qi * 1408, 1408
            qv = qs2[qi]
            qv["xres"] = hb.res
            qv["outres"] = hq.res
            _quant_emit(p, nc, None, hb.a((off, ln)), hq.a((off, ln)),
                        ln // 16, qv, tag=f"qh{ti}_{qi}_")
        ch = 0
        grp = 0
        while ch < NIC:
            n4 = min(4, NIC - ch)
            hs = hqs[grp % 2]
            for q2 in range(n4):
                pt = psTa if q2 % 2 == 0 else psTb
                rt_ = rTa if q2 % 2 == 0 else rTb
                p.T(lambda e, ch=ch, q2=q2, pt=pt, hq=hq: e.transpose(
                        pt.ap()[:, 0:128], hq.a(((ch + q2) * 128, 128)),
                        identb.a()),
                    reads=[hq.res, identb.res], writes=[rt_],
                    tag=f"ht{ti}_{ch + q2}")
                p.S(lambda e, q2=q2, pt=pt, hs=hs: e.copy(
                        hs.a(q2), pt.ap()[:, 0:128]),
                    reads=[rt_], writes=[hs.res], tag=f"he{ti}_{ch + q2}")
            p.store(shq[ti][:, ch:ch + n4, :], hs.a((0, n4)), reads=[hs.res],
                    writes=[rhq[ti]], tag=f"hs{ti}_{grp}")
            ch += n4
            grp += 1
    AB.free(wqg, wqu, xq[0], xq[1], hqr[0], hqr[1], hqs[0], hqs[1])
    AF32.free(hbr[0], hbr[1], sgr[0])

    wqd = AB.alloc("wqd", [NIC, 1024])
    ftd = AF32.alloc("ftd", [FFI])
    for rt in range(NFC):
        p.load(ftd.a(), d["wdown"][rt * 128:(rt + 1) * 128, :],
               writes=[ftd.res], tag=f"wd_l{rt}")
        qdt2 = AB.alloc(f"qdt2_{rt}", [FFI]) if rt == 0 else env["qdt2"]
        env["qdt2"] = qdt2
        for qi in range(2):
            off, ln = qi * 1408, 1408
            qv = qs2[qi]
            qv["xres"] = ftd.res
            qv["outres"] = qdt2.res
            _quant_emit(p, nc, None, ftd.a((off, ln)), qdt2.a((off, ln)),
                        ln // 16, qv, tag=f"wd_q{rt}_{qi}_")
        for ch in range(NIC):
            pt = psTa if ch % 2 == 0 else psTb
            rt_ = rTa if ch % 2 == 0 else rTb
            p.T(lambda e, ch=ch, pt=pt, qdt2=qdt2: e.transpose(
                    pt.ap()[:, 0:128], qdt2.a((ch * 128, 128)), identb.a()),
                reads=[qdt2.res, identb.res], writes=[rt_],
                tag=f"wd_t{rt}_{ch}")
            p.S(lambda e, ch=ch, rt=rt, pt=pt: e.copy(
                    wqd.a(ch, (rt * 128, 128)), pt.ap()[:, 0:128]),
                reads=[rt_], writes=[wqd.res], tag=f"wd_e{rt}_{ch}")
    free_qscr(qs2[0])
    free_qscr(qs2[1])
    AF32.free(ftd)

    hqt = [AB.alloc(f"hqt{i}", [NIC, 128]) for i in range(2)]
    opt = [AF32.alloc(f"opt{i}", [1024]) for i in range(2)]
    fin = [AF32.alloc(f"fin{i}", [1024]) for i in range(2)]
    for ti in range(NT128):
        b, trow = ti // NCH, (ti % NCH) * 128
        ht = hqt[ti % 2]
        ot = opt[ti % 2]
        p.load(ht.a(), shq[ti], reads=[rhq[ti]], writes=[ht.res],
               tag=f"p2h{ti}")
        p.load(ot.a(), soutp[ti], reads=[routp[ti]], writes=[ot.res],
               tag=f"p2o{ti}")
        for nh in range(2):
            for ic in range(NIC):
                p.T(lambda e, ic=ic, nh=nh, ht=ht: e.matmul(
                        bigap(nh), ht.a(ic), wqd.a(ic, (nh * 512, 512)),
                        start=(ic == 0), stop=(ic == NIC - 1)),
                    reads=[ht.res, wqd.res], writes=[rBig[nh]],
                    tag=f"dn{ti}_{ic}_{nh}")
        fi = fin[ti % 2]
        p.V(lambda e, ot=ot, fi=fi: e.tensor_tensor(
                fi.a(), psW.ap(), ot.a(), ALU.add),
            reads=[rBig[0], rBig[1], ot.res], writes=[fi.res],
            tag=f"fadd{ti}")
        p.store(out_t[b, trow:trow + 128, :], fi.a(), reads=[fi.res],
                tag=f"out{ti}")


_NC_CACHE = {}


def _get_nc():
    if "nc" not in _NC_CACHE:
        nc, stack = build_program(upto="all")
        _NC_CACHE["nc"] = nc
        _NC_CACHE["stack"] = stack
    return _NC_CACHE["nc"]


def _consts():
    i128 = np.eye(128, dtype=np.float32)
    k = np.arange(128)
    import ml_dtypes
    BF = ml_dtypes.bfloat16
    pf = np.zeros((16, 1024), np.float32)
    for h in range(16):
        pf[h, h * 64:(h + 1) * 64] = 1.0
    pfT = np.zeros((128, NFC * 16), np.float32)
    for fc in range(8):
        for h2 in range(2):
            pfT[h2 * 64:(h2 + 1) * 64, fc * 16 + 2 * fc + h2] = 1.0
    return {
        "identb": i128.astype(BF), "identf": i128,
        "triLneg": -((k[:, None] <= k[None, :]).astype(np.float32)),
        "triUneg": -((k[:, None] > k[None, :]).astype(np.float32)),
        "maskLowNeg": ((k[:, None] > k[None, :]).astype(np.float32)
                       * np.float32(-1e30)).astype(BF),
        "pfull": pf.astype(BF), "pfullT": pfT.astype(BF),
        "onesb": np.ones((1, 512), BF), "onesf": np.ones((1, 128), np.float32),
    }


def kernel(**inputs):
    nc = _get_nc()
    consts = _consts()
    xfull = np.ascontiguousarray(np.asarray(inputs["L0_attn_x"], np.float32))
    shared = {
        "wsqT": np.ascontiguousarray(inputs["L0_attn_Wsq"].T),
        "wskT": np.ascontiguousarray(inputs["L0_attn_Wsk"].T),
        "wsvT": np.ascontiguousarray(inputs["L0_attn_Wsv"].T),
        "wvT": np.ascontiguousarray(inputs["L0_attn_Wv"].T),
        "wqT": np.ascontiguousarray(inputs["L0_attn_Wq"].T),
        "wkT": np.ascontiguousarray(inputs["L0_attn_Wk"].T),
        "woT": np.ascontiguousarray(inputs["L0_attn_Wo"].T),
        "wgT": np.ascontiguousarray(inputs["L0_attn_Wg"].T),
        "bsq": np.ascontiguousarray(inputs["L0_attn_bsq"]).reshape(1, -1),
        "bsk": np.ascontiguousarray(inputs["L0_attn_bsk"]).reshape(1, -1),
        "bsv": np.ascontiguousarray(inputs["L0_attn_bsv"]).reshape(1, -1),
        "bq": np.ascontiguousarray(inputs["L0_attn_bq"]).reshape(1, -1),
        "bk": np.ascontiguousarray(inputs["L0_attn_bk"]).reshape(1, -1),
        "bv": np.ascontiguousarray(inputs["L0_attn_bv"]).reshape(1, -1),
        "bo": np.ascontiguousarray(inputs["L0_attn_bo"]).reshape(1, -1),
        "bg": np.ascontiguousarray(inputs["L0_attn_bg"]).reshape(1, -1),
        "temp16": np.ascontiguousarray(
            np.asarray(inputs["L0_attn_temp"], np.float32).reshape(16, 1)),
        "wgate": np.ascontiguousarray(inputs["L0_ffn_w_gate"]),
        "wup": np.ascontiguousarray(inputs["L0_ffn_w_up"]),
        "wdown": np.ascontiguousarray(inputs["L0_ffn_w_down"]),
    }
    shared = {k: np.ascontiguousarray(np.asarray(v, np.float32))
              for k, v in shared.items()}
    shared.update(consts)
    in_maps = []
    for core in range(NCORES):
        m = dict(shared)
        m["x"] = np.ascontiguousarray(xfull[core * NB:(core + 1) * NB])
        in_maps.append(m)
    res = run_bass_kernel_spmd(nc, in_maps, core_ids=list(range(NCORES)))
    out = np.concatenate([r["out"] for r in res.results], axis=0)
    return out.astype(np.float32)


# version 27
# speedup vs baseline: 1.0048x; 1.0026x over previous
import numpy as np
from contextlib import ExitStack

import concourse.bass as bass
import concourse.mybir as mybir
from concourse.bass_utils import run_bass_kernel_spmd

F32 = mybir.dt.float32
BF16 = mybir.dt.bfloat16
AF = mybir.ActivationFunctionType
ALU = mybir.AluOpType
AX = mybir.AxisListType

B, T, DIM, H = 16, 2048, 1024, 16
HD = DIM // H
FFI = 2816
NCORES = 8
NB = B // NCORES
TOK = NB * T
NT128 = TOK // 128
NCH = T // 128
NFC = DIM // 128
NIC = FFI // 128
LN_EPS = 1e-5

FP4_MID = [0.25, 0.75, 1.25, 1.75, 2.5, 3.5, 5.0]
FP4_W = [0.5, 0.5, 0.5, 0.5, 1.0, 1.0, 2.0]

SAME_ENGINE_WAITS = False
DETECT_RACES = False


class Res:
    __slots__ = ("name", "writer", "readers", "preds")

    def __init__(self, name=""):
        self.name = name
        self.writer = None
        self.readers = []
        self.preds = []


class Op:
    __slots__ = ("eng", "fn", "ticket", "deps", "need_inc", "tag")

    def __init__(self, eng, fn, tag):
        self.eng = eng
        self.fn = fn
        self.ticket = None
        self.deps = []
        self.need_inc = False
        self.tag = tag


class Prog:
    NLANES = 8

    def __init__(self):
        self.warm_ap = None
        self.op_start = {}
        self.op_finish = {}
        self.ops = {k: [] for k in ("T", "V", "S", "G", "Y")}
        self.dma_count = {"DY": 0, "DG": 0}
        self.lane_last = {}
        self.lane_uses = {}
        self.n_ops = 0

    def emit(self, eng, fn, reads=(), writes=(), tag=""):
        op = Op(eng, fn, tag)
        seen = set()

        def dep(x):
            if x is not None and id(x) not in seen and x is not op:
                seen.add(id(x))
                op.deps.append(x)

        for r in reads:
            dep(r.writer)
        for r in writes:
            dep(r.writer)
            for rd in r.readers:
                dep(rd)
            for p in r.preds:
                dep(p)
        issue_eng = {"DY": "Y", "DG": "G"}.get(eng, eng)
        if eng in ("DY", "DG"):
            lane = self.dma_count[eng] % self.NLANES
            self.dma_count[eng] += 1
            key = (eng, lane)
            dep(self.lane_last.get(key))
            self.lane_last[key] = op
            self.lane_uses[key] = self.lane_uses.get(key, 0) + 1
            op.ticket = (f"{eng}{lane}", 16 * self.lane_uses[key])
        self.ops[issue_eng].append(op)
        self.n_ops += 1
        for r in reads:
            r.readers.append(op)
        for r in writes:
            r.writer = op
            r.readers = []
            r.preds = []
        return op

    def T(self, fn, reads=(), writes=(), tag=""):
        return self.emit("T", fn, reads, writes, tag)

    def V(self, fn, reads=(), writes=(), tag=""):
        return self.emit("V", fn, reads, writes, tag)

    def S(self, fn, reads=(), writes=(), tag=""):
        return self.emit("S", fn, reads, writes, tag)

    def G(self, fn, reads=(), writes=(), tag=""):
        return self.emit("G", fn, reads, writes, tag)

    def load(self, dst_ap, src_ap, reads=(), writes=(), tag="ld"):
        return self.emit("DY", lambda e: e.dma_start(dst_ap, src_ap),
                         reads, writes, tag)

    def store(self, dst_ap, src_ap, reads=(), writes=(), tag="st"):
        return self.emit("DG", lambda e: e.dma_start(dst_ap, src_ap),
                         reads, writes, tag)

    COST = {"T": 0.25, "V": 0.9, "S": 0.9, "G": 0.7, "Y": 0.1}

    def schedule(self):
        import heapq
        eng_of = {}
        idx_of = {}
        nsucc_sched = {}
        for eng, lst in self.ops.items():
            for i, op in enumerate(lst):
                eng_of[id(op)] = eng
                idx_of[id(op)] = i
        npred = {}
        succs = {}
        for eng, lst in self.ops.items():
            for op in lst:
                npred[id(op)] = len(op.deps)
                for dp in op.deps:
                    succs.setdefault(id(dp), []).append(op)
        finish = {}
        clock = {e: 0.0 for e in self.ops}
        ready = {e: [] for e in self.ops}
        new_order = {e: [] for e in self.ops}
        n_total = sum(len(v) for v in self.ops.values())

        def push_ready(op):
            av = max((finish[id(dp)] for dp in op.deps), default=0.0)
            e = eng_of[id(op)]
            heapq.heappush(ready[e], (av, idx_of[id(op)], id(op), op))

        for eng, lst in self.ops.items():
            for op in lst:
                if npred[id(op)] == 0:
                    push_ready(op)
        done = 0
        while done < n_total:
            best, bh = None, None
            for e, h in ready.items():
                if not h:
                    continue
                av = h[0][0]
                start = max(clock[e], av)
                key = (start, av)
                if best is None or key < bh:
                    best, bh = e, key
            assert best is not None, "scheduler deadlock (cycle in deps)"
            av, _, _, op = heapq.heappop(ready[best])
            e = best
            dma = op.ticket is not None and op.ticket[0][0] == "D"
            dur = 2.5 if dma else self.COST[e]
            start = max(clock[e], av)
            fin = start + dur
            clock[e] = start + (0.15 if dma else dur)
            finish[id(op)] = fin
            self.op_start[id(op)] = start
            self.op_finish[id(op)] = fin
            new_order[e].append(op)
            done += 1
            for su in succs.get(id(op), ()):
                npred[id(su)] -= 1
                if npred[id(su)] == 0:
                    push_ready(su)
        self.ops = new_order

    def _insert_warmers(self):
        if self.warm_ap is None:
            return
        wap = self.warm_ap
        new_T = []
        prev_end = 0.0
        nfill = 0
        for op in self.ops["T"]:
            st = self.op_start.get(id(op), prev_end)
            if st - prev_end > 1.0 and nfill < 9000:
                seen = set()
                anc = []
                frontier = list(op.deps)
                depth = 0
                while frontier and depth < 24:
                    nxt = []
                    for a in frontier:
                        if id(a) in seen:
                            continue
                        seen.add(id(a))
                        f = self.op_finish.get(id(a))
                        if f is not None and prev_end + 0.4 < f < st - 0.2:
                            anc.append((f, a))
                        nxt.extend(a.deps)
                    frontier = nxt
                    depth += 1
                anc.sort(key=lambda x: x[0])
                last_t = prev_end
                for f, a in anc:
                    if f >= last_t + 0.6:
                        fo = Op("T", lambda e: e.ldweights(wap), "warm")
                        fo.deps = [a]
                        new_T.append(fo)
                        nfill += 1
                        last_t = f
            new_T.append(op)
            prev_end = max(prev_end, self.op_finish.get(id(op), prev_end))
        self.ops["T"] = new_T

    def lower(self, nc, ec):
        self.schedule()
        self._insert_warmers()
        for eng in self.ops:
            for op in self.ops[eng]:
                for d in op.deps:
                    d.need_inc = True
        for eng in ("T", "V", "S", "G"):
            c = 0
            for op in self.ops[eng]:
                if op.ticket is not None:
                    continue
                if op.need_inc:
                    c += 1
                    op.ticket = (eng, c)

        sem_names = ["T", "V", "S", "G"]
        for q in ("DY", "DG"):
            for ln in range(self.NLANES):
                sem_names.append(f"{q}{ln}")
        sems = {k: ec(nc.semaphore("sem" + k)) for k in sem_names}

        def runner(key):
            def run(h):
                waited = {}
                for op in self.ops[key]:
                    for d in op.deps:
                        if d.ticket is None:
                            continue
                        sname, val = d.ticket
                        if sname == key and not SAME_ENGINE_WAITS:
                            continue
                        if waited.get(sname, -1) >= val:
                            continue
                        h.wait_ge(sems[sname], val)
                        waited[sname] = val
                    inst = op.fn(h)
                    if op.ticket is not None:
                        sname, val = op.ticket
                        inst.then_inc(sems[sname],
                                      16 if sname[0] == "D" else 1)
                q = {"G": "DG", "Y": "DY"}.get(key)
                if q:
                    for ln in range(self.NLANES):
                        u = self.lane_uses.get((q, ln), 0)
                        if u:
                            h.wait_ge(sems[f"{q}{ln}"], 16 * u)
            return run

        block = ec(nc.Block())
        block.tensor(runner("T"))
        block.vector(runner("V"))
        block.scalar(runner("S"))
        block.gpsimd(runner("G"))
        block.sync(runner("Y"))


class Arena:

    def __init__(self, nc, name, per_part_elems, dtype):
        self.t = nc.alloc_sbuf_tensor(name, [128, per_part_elems], dtype)
        self.cap = per_part_elems
        self.cursor = 0
        self.live = []
        self.dead = []
        self.name_start = {}

    def alloc(self, name, shape):
        n = int(np.prod(shape))
        ivs = sorted((s, e) for (s, e, _) in self.live)
        start = None
        pos = 0
        for (s0, e0) in ivs:
            cand = (pos + 31) & ~31
            if cand + n <= s0:
                start = cand
                break
            pos = max(pos, e0)
        if start is None:
            cand = (pos + 31) & ~31
            assert cand + n <= self.cap, \
                f"arena {self.t.name} overflow: {name}: {cand + n} > {self.cap}"
            start = cand
        end = start + n
        prev = self.name_start.setdefault(name, start)
        assert prev == start, \
            f"region {name} realloc moved: {prev} -> {start} (closure hazard)"
        reg = Region(self, name, start, list(shape))
        for (s0, e0, r0) in self.dead:
            if s0 < end and start < e0:
                if r0.res.writer is not None:
                    reg.res.preds.append(r0.res.writer)
                reg.res.preds.extend(r0.res.readers)
        self.live.append((start, end, reg))
        return reg

    def free(self, *regs):
        ids = {id(r) for r in regs}
        keep = []
        for (s, e, r) in self.live:
            (self.dead if id(r) in ids else keep).append((s, e, r))
        self.live = keep


class Region:

    def __init__(self, arena, name, start, shape):
        self.arena = arena
        self.name = name
        self.start = start
        self.shape = shape
        self.res = Res(name)
        st = []
        acc = 1
        for s in reversed(shape):
            st.append(acc)
            acc *= s
        self.strides = list(reversed(st))

    def a(self, *idx, p0=0, pn=128):
        pat = [[self.arena.cap, 128]]
        base = 0
        for d, s in enumerate(self.shape):
            sel = idx[d] if d < len(idx) else None
            if sel is None:
                pat.append([self.strides[d], s])
            elif isinstance(sel, int):
                base += sel * self.strides[d]
            else:
                st, cnt = sel
                base += st * self.strides[d]
                pat.append([self.strides[d], cnt])
        ap = bass.AP(self.arena.t, self.start + base, pat)
        if p0 or pn != 128:
            ap = ap[p0:p0 + pn]
        return ap


def _quant_emit(p, nc, ctx, xf32_ap, out_bf_ap, nblk, scr, tag=""):
    N = nblk * 16
    bm, s, r, sbf = scr["bm"], scr["s"], scr["r"], scr["sbf"]
    an, aab, sgn = scr["an"], scr["aab"], scr["sgn"]
    c = scr["c"]; g = scr["g"]
    xres = scr["xres"]; outres = scr["outres"]
    blk = lambda ap: ap

    x3 = xf32_ap.rearrange("p (nb k) -> p nb k", k=16)
    p.V(lambda e: e.tensor_reduce(bm.a((0, nblk)), x3, AX.X, ALU.max,
                                  apply_absolute_value=True),
        reads=[xres], writes=[bm.res], tag=tag + "bm")
    p.V(lambda e: e.tensor_scalar(s.a((0, nblk)), bm.a((0, nblk)),
                                  1.0 / 6.0, 1e-12, ALU.mult, ALU.max),
        reads=[bm.res], writes=[s.res], tag=tag + "s")
    p.V(lambda e: e.reciprocal(r.a((0, nblk)), s.a((0, nblk))),
        reads=[s.res], writes=[r.res], tag=tag + "r")
    p.V(lambda e: e.tensor_copy(sbf.a((0, nblk)), s.a((0, nblk))),
        reads=[s.res], writes=[sbf.res], tag=tag + "sbf")
    rbc = r.a((0, nblk)).broadcast_to([128, nblk, 16])
    p.V(lambda e: e.tensor_tensor(an.a((0, N)).rearrange("p (nb k) -> p nb k", k=16),
                                  x3, rbc, ALU.mult),
        reads=[xres, r.res], writes=[an.res], tag=tag + "an")
    p.S(lambda e: e.activation(aab.a((0, N)), an.a((0, N)), AF.Abs),
        reads=[an.res], writes=[aab.res], tag=tag + "aab")
    p.S(lambda e: e.activation(sgn.a((0, N)), an.a((0, N)), AF.Sign),
        reads=[an.res], writes=[sgn.res], tag=tag + "sgn")
    def cmp(j, dst):
        p.V(lambda e, j=j, dst=dst: e.tensor_scalar(dst.a((0, N)), aab.a((0, N)),
                                                    FP4_MID[j], FP4_W[j],
                                                    ALU.is_gt, ALU.mult),
            reads=[aab.res], writes=[dst.res], tag=tag + f"c{j}")
    cmp(0, c[0]); cmp(1, c[1])
    p.V(lambda e: e.tensor_tensor(g[0].a((0, N)), c[0].a((0, N)), c[1].a((0, N)), ALU.add),
        reads=[c[0].res, c[1].res], writes=[g[0].res], tag=tag + "g0")
    cmp(2, c[0]); cmp(3, c[1])
    p.V(lambda e: e.tensor_tensor(g[1].a((0, N)), c[0].a((0, N)), c[1].a((0, N)), ALU.add),
        reads=[c[0].res, c[1].res], writes=[g[1].res], tag=tag + "g1")
    cmp(4, c[0]); cmp(5, c[1])
    p.V(lambda e: e.tensor_tensor(g[2].a((0, N)), c[0].a((0, N)), c[1].a((0, N)), ALU.add),
        reads=[c[0].res, c[1].res], writes=[g[2].res], tag=tag + "g2")
    cmp(6, c[0])
    p.V(lambda e: e.tensor_tensor(g[0].a((0, N)), g[0].a((0, N)), g[1].a((0, N)), ALU.add),
        reads=[g[0].res, g[1].res], writes=[g[0].res], tag=tag + "g3")
    p.V(lambda e: e.tensor_tensor(g[2].a((0, N)), g[2].a((0, N)), c[0].a((0, N)), ALU.add),
        reads=[g[2].res, c[0].res], writes=[g[2].res], tag=tag + "g4")
    p.V(lambda e: e.tensor_tensor(g[0].a((0, N)), g[0].a((0, N)), g[2].a((0, N)), ALU.add),
        reads=[g[0].res, g[2].res], writes=[g[0].res], tag=tag + "qsum")
    p.V(lambda e: e.tensor_tensor(g[1].a((0, N)), g[0].a((0, N)), sgn.a((0, N)), ALU.mult),
        reads=[g[0].res, sgn.res], writes=[g[1].res], tag=tag + "qsgn")
    sbc = sbf.a((0, nblk)).broadcast_to([128, nblk, 16])
    p.V(lambda e: e.tensor_tensor(out_bf_ap.rearrange("p (nb k) -> p nb k", k=16),
                                  g[1].a((0, N)).rearrange("p (nb k) -> p nb k", k=16),
                                  sbc, ALU.mult),
        reads=[g[1].res, sbf.res], writes=[outres], tag=tag + "qout")


def build_nc():
    nc = bass.Bass("TRN2", target_bir_lowering=False,
                   detect_race_conditions=DETECT_RACES)

    d = {}
    def din(name, shape, dt=F32):
        d[name] = nc.dram_tensor(name, shape, dt, kind="ExternalInput").ap()
        return d[name]

    x_in = din("x", [NB, T, DIM])
    for w in ("wsqT", "wskT", "wsvT", "wvT", "wqT", "wkT", "woT"):
        din(w, [DIM, DIM])
    din("wgT", [DIM, H])
    for bn in ("bsq", "bsk", "bsv", "bq", "bk", "bv", "bo"):
        din(bn, [1, DIM])
    din("bg", [1, H])
    din("temp16", [H, 1])
    din("wgate", [FFI, DIM])
    din("wup", [FFI, DIM])
    din("wdown", [DIM, FFI])
    din("identb", [128, 128], BF16)
    din("identf", [128, 128], F32)
    din("triLneg", [128, 128], F32)
    din("triUneg", [128, 128], F32)
    din("maskLowNeg", [128, 128], BF16)
    din("pfull", [H, DIM], BF16)
    din("pfullT", [128, NFC * H], BF16)
    din("onesb", [1, 512], BF16)
    din("onesf", [1, 128], F32)

    out_t = nc.dram_tensor("out", [NB, T, DIM], F32, kind="ExternalOutput").ap()

    ssq = nc.dram_tensor("ssq", [NB, NCH, 128, NFC, 128], BF16, kind="Internal").ap()
    ssk = nc.dram_tensor("ssk", [NB, NCH, 128, NFC, 128], BF16, kind="Internal").ap()
    ssv = nc.dram_tensor("ssv", [NB, NCH, 128, DIM], BF16, kind="Internal").ap()
    sV = nc.dram_tensor("sV", [NB, NCH, 128, H, HD], BF16, kind="Internal").ap()
    sxq = nc.dram_tensor("sxq", [NT128, 128, NFC, 128], BF16, kind="Internal").ap()
    shq = nc.dram_tensor("shq", [NT128, 128, NIC, 128], BF16, kind="Internal").ap()
    soutp = nc.dram_tensor("soutp", [NT128, 128, DIM], F32, kind="Internal").ap()
    rsq = [[Res() for _ in range(NCH)] for _ in range(NB)]
    rsk = [[Res() for _ in range(NCH)] for _ in range(NB)]
    rsv = [[Res() for _ in range(NCH)] for _ in range(NB)]
    rV = [[Res() for _ in range(NCH)] for _ in range(NB)]
    rxq = [Res() for _ in range(NT128)]
    rhq = [Res() for _ in range(NT128)]
    routp = [Res() for _ in range(NT128)]

    stack = ExitStack()
    ec = stack.enter_context
    p = Prog()
    AF32 = Arena(nc, "AF", 13568, F32)
    AB = Arena(nc, "ABF", 79264, BF16)
    psW = nc.alloc_psum_tensor("psW", [128, 1024], F32)
    psX = nc.alloc_psum_tensor("psX", [128, 1024], F32)
    psA = nc.alloc_psum_tensor("psA", [128, 512], F32)
    psB = nc.alloc_psum_tensor("psB", [128, 512], F32)
    psTa = nc.alloc_psum_tensor("psTa", [128, 1024], BF16)
    psTb = nc.alloc_psum_tensor("psTb", [128, 1024], BF16)
    rBig = [Res(f"psBig{i}") for i in range(4)]
    rA, rB_, rTa, rTb = Res("psA"), Res("psB"), Res("psTa"), Res("psTb")
    def bigap(i, w=512):
        t = psW if i < 2 else psX
        j = i % 2
        return t.ap()[:, j * 512:j * 512 + w]

    return (nc, d, out_t, stack, ec, p, AF32, AB,
            dict(psW=psW, psX=psX, psA=psA, psB=psB, psTa=psTa, psTb=psTb,
                 rBig=rBig, rA=rA, rB=rB_, rTa=rTa, rTb=rTb, bigap=bigap),
            dict(ssq=ssq, ssk=ssk, ssv=ssv, sV=sV, sxq=sxq, shq=shq,
                 soutp=soutp, rsq=rsq, rsk=rsk, rsv=rsv, rV=rV, rxq=rxq,
                 rhq=rhq, routp=routp))


_DBG = {}


def build_program(upto="all"):
    (nc, d, out_t, stack, ec, p, AF32, AB, PS, SC) = build_nc()
    psW, psX, psA, psB = PS["psW"], PS["psX"], PS["psA"], PS["psB"]
    psTa, psTb = PS["psTa"], PS["psTb"]
    rBig, rA, rB, rTa, rTb = PS["rBig"], PS["rA"], PS["rB"], PS["rTa"], PS["rTb"]
    bigap = PS["bigap"]

    cst = {}
    def c_load(nm, arena, shape, pn=128, src=None):
        reg = arena.alloc(nm, shape)
        p.load(reg.a(p0=0, pn=pn), src if src is not None else d[nm],
               writes=[reg.res], tag="c_" + nm)
        return reg
    identb = c_load("identb", AB, [128])
    p.warm_ap = identb.a()
    identf = c_load("identf", AF32, [128])
    triLneg = c_load("triLneg", AF32, [128])
    triUneg = c_load("triUneg", AF32, [128])
    maskLN = c_load("maskLowNeg", AB, [128])
    pfull = c_load("pfull", AB, [DIM], pn=H)
    pfullT = c_load("pfullT", AB, [NFC, H])
    onesb = c_load("onesb", AB, [512], pn=1)
    onesf = c_load("onesf", AF32, [128], pn=1)
    bias_b = {}
    bf_tmp = AF32.alloc("btmp", [DIM])
    for bn in ("bsq", "bsk", "bsv", "bq", "bk", "bv", "bo", "bg"):
        n = H if bn == "bg" else DIM
        reg = AB.alloc(bn + "_b", [n])
        p.load(bf_tmp.a((0, n), p0=0, pn=1), d[bn], writes=[bf_tmp.res], tag=bn)
        p.V(lambda e, r=reg, n=n: e.tensor_copy(r.a(p0=0, pn=1),
                                                bf_tmp.a((0, n), p0=0, pn=1)),
            reads=[bf_tmp.res], writes=[reg.res], tag=bn + "c")
        bias_b[bn] = reg
    tmp16 = AF32.alloc("tmp16", [1])
    sp16 = AF32.alloc("sp16", [1])
    p.load(tmp16.a(p0=0, pn=H), d["temp16"], writes=[tmp16.res], tag="temp")
    p.S(lambda e: e.activation(tmp16.a(p0=0, pn=H), tmp16.a(p0=0, pn=H), AF.Exp),
        reads=[tmp16.res], writes=[tmp16.res], tag="spt1")
    p.S(lambda e: e.activation(sp16.a(p0=0, pn=H), tmp16.a(p0=0, pn=H),
                               AF.Ln, bias=1.0),
        reads=[tmp16.res], writes=[sp16.res], tag="spt2")

    logg = AF32.alloc("logg", [NT128, H])
    _DBG["logg"] = logg

    wA = {}
    ftmp = [AF32.alloc(f"ftmp{i}", [1024]) for i in range(2)]
    for nm, dn in (("wsq", "wsqT"), ("wsk", "wskT"), ("wsv", "wsvT"),
                   ("wv", "wvT")):
        reg = AB.alloc(nm, [NFC, 1024])
        for kc in range(NFC):
            ft = ftmp[kc % 2]
            p.load(ft.a(), d[dn][kc * 128:(kc + 1) * 128, :],
                   writes=[ft.res], tag=f"{nm}l{kc}")
            p.V(lambda e, r=reg, kc=kc, ft=ft: e.tensor_copy(r.a(kc), ft.a()),
                reads=[ft.res], writes=[reg.res], tag=f"{nm}c{kc}")
        wA[nm] = reg
    wg = AB.alloc("wg", [NFC, H])
    ftg = AF32.alloc("ftg", [NFC, H])
    p.load(ftg.a(), d["wgT"].rearrange("(kc p) h -> p kc h", p=128),
           writes=[ftg.res], tag="wgl")
    p.V(lambda e: e.tensor_copy(wg.a(), ftg.a()), reads=[ftg.res],
        writes=[wg.res], tag="wgc")

    xin = [AF32.alloc(f"xin{i}", [1024]) for i in range(2)]
    xnf = [AF32.alloc(f"xnf{i}", [1024]) for i in range(2)]
    x2s = AF32.alloc("x2s", [1024])
    sm = AF32.alloc("sm", [16])
    ezg = AF32.alloc("ezg", [H])
    xnbf = [AB.alloc(f"xnbf{i}", [1024]) for i in range(2)]
    xnfmr = [AB.alloc(f"xnfm{i}", [NFC, 512]) for i in range(2)]
    xqtm = [AB.alloc(f"xqtm{i}", [1024]) for i in range(2)]
    xqst = [AB.alloc(f"xqst{i}", [NFC, 128]) for i in range(2)]
    stg = {"sq": [AB.alloc("stsq0", [NFC, 512])],
           "sk": [AB.alloc("stsk0", [NFC, 512])]}
    svst = [AB.alloc(f"svst{i}", [1024]) for i in range(2)]
    Vst = [AB.alloc(f"Vst{i}", [1024]) for i in range(2)]
    def mk_qscr(QN, pfx, sgn_share=None):
        if sgn_share is not None:
            sgn_reg = sgn_share
        else:
            sgn_reg = None
        return dict(
            bm=AF32.alloc(pfx + "bm", [QN // 16]), s=AF32.alloc(pfx + "s", [QN // 16]),
            r=AF32.alloc(pfx + "r", [QN // 16]), sbf=AB.alloc(pfx + "sbf", [QN // 16]),
            an=AF32.alloc(pfx + "an", [QN]), aab=AF32.alloc(pfx + "aab", [QN]),
            sgn=(sgn_reg if sgn_reg is not None else AB.alloc(pfx + "sgn", [QN])),
            c=[AB.alloc(pfx + f"c{j}", [QN]) for j in range(2)],
            g=[AB.alloc(pfx + f"g{j}", [QN]) for j in range(3)])

    def free_qscr(q, skip_sgn=False):
        AF32.free(q["bm"], q["s"], q["r"], q["an"], q["aab"])
        AB.free(q["sbf"], *q["c"], *q["g"])
        if not skip_sgn:
            AB.free(q["sgn"])

    _q1 = mk_qscr(1024, "qa_")
    qscr2 = [_q1, _q1]

    ssq, ssk, ssv, sVt = SC["ssq"], SC["ssk"], SC["ssv"], SC["sV"]
    sxq, shq, soutp = SC["sxq"], SC["shq"], SC["soutp"]
    rsq, rsk, rsv, rV = SC["rsq"], SC["rsk"], SC["rsv"], SC["rV"]
    rxq, rhq, routp = SC["rxq"], SC["rhq"], SC["routp"]

    for tt in range(8):
        xnfm = xnfmr[tt % 2]
        b = tt // 4
        for j in range(4):
            ti = tt * 4 + j
            trow = (ti % 16) * 128
            xi = xin[ti % 2]
            p.load(xi.a(), d["x"][b, trow:trow + 128, :], writes=[xi.res],
                   tag=f"x{ti}")
            xn = xnf[ti % 2]
            p.V(lambda e, xi=xi: e.reduce_sum(sm.a((0, 1)), xi.a(), AX.X),
                reads=[xi.res], writes=[sm.res], tag=f"ln_s1_{ti}")
            p.S(lambda e, xi=xi: e.activation(x2s.a(), xi.a(), AF.Square,
                                              accum_out=sm.a((1, 1))),
                reads=[xi.res], writes=[x2s.res, sm.res], tag=f"ln_sq_{ti}")
            p.V(lambda e: e.tensor_scalar(sm.a((2, 1)), sm.a((0, 1)),
                                          1.0 / DIM, None, ALU.mult),
                reads=[sm.res], writes=[sm.res], tag=f"ln_m_{ti}")
            p.V(lambda e: e.tensor_scalar(sm.a((3, 1)), sm.a((1, 1)),
                                          1.0 / DIM, LN_EPS, ALU.mult, ALU.add),
                reads=[sm.res], writes=[sm.res], tag=f"ln_e2_{ti}")
            p.V(lambda e: e.tensor_tensor(sm.a((4, 1)), sm.a((2, 1)),
                                          sm.a((2, 1)), ALU.mult),
                reads=[sm.res], writes=[sm.res], tag=f"ln_m2_{ti}")
            p.V(lambda e: e.tensor_tensor(sm.a((5, 1)), sm.a((3, 1)),
                                          sm.a((4, 1)), ALU.subtract),
                reads=[sm.res], writes=[sm.res], tag=f"ln_var_{ti}")
            p.S(lambda e: e.activation(sm.a((6, 1)), sm.a((5, 1)), AF.Ln),
                reads=[sm.res], writes=[sm.res], tag=f"ln_ln_{ti}")
            p.S(lambda e: e.activation(sm.a((7, 1)), sm.a((6, 1)), AF.Exp,
                                       scale=-0.5),
                reads=[sm.res], writes=[sm.res], tag=f"ln_rs_{ti}")
            p.V(lambda e, xi=xi, xn=xn: e.tensor_scalar(
                    xn.a(), xi.a(), sm.a((2, 1)), sm.a((7, 1)),
                    ALU.subtract, ALU.mult),
                reads=[xi.res, sm.res], writes=[xn.res], tag=f"ln_ap_{ti}")
            xb_ = xnbf[ti % 2]
            p.V(lambda e, xn=xn, xb_=xb_: e.tensor_copy(xb_.a(), xn.a()),
                reads=[xn.res], writes=[xb_.res], tag=f"cst_{ti}")
            for fc in range(NFC):
                p.T(lambda e, fc=fc, xb_=xb_: e.transpose(
                        psTa.ap()[:, fc * 128:(fc + 1) * 128],
                        xb_.a((fc * 128, 128)), identb.a()),
                    reads=[xb_.res, identb.res], writes=[rTa],
                    tag=f"trx_{ti}_{fc}")
            p.S(lambda e, j=j, xnfm=xnfm: e.copy(
                    xnfm.a(None, (j * 128, 128)),
                    psTa.ap().rearrange("p (fc t) -> p fc t", t=128)),
                reads=[rTa], writes=[xnfm.res], tag=f"evx_{ti}")
            xq = xqtm[ti % 2]
            qscr = qscr2[ti % 2]
            qscr["xres"] = xn.res
            qscr["outres"] = xq.res
            _quant_emit(p, nc, None, xn.a(), xq.a(), DIM // 16, qscr,
                        tag=f"qx{ti}_")
            for fc in range(NFC):
                p.T(lambda e, fc=fc, xq=xq: e.transpose(
                        psTb.ap()[:, fc * 128:(fc + 1) * 128],
                        xq.a((fc * 128, 128)), identb.a()),
                    reads=[xq.res, identb.res], writes=[rTb],
                    tag=f"trq_{ti}_{fc}")
            xs = xqst[ti % 2]
            p.S(lambda e, xs=xs: e.copy(
                    xs.a(), psTb.ap().rearrange("p (fc t) -> p fc t", t=128)),
                reads=[rTb], writes=[xs.res], tag=f"evq_{ti}")
            p.store(sxq[ti], xs.a(), reads=[xs.res], writes=[rxq[ti]],
                    tag=f"sxq{ti}")
            for kc in range(NFC):
                p.T(lambda e, kc=kc, j=j, xnfm=xnfm: e.matmul(
                        psA.ap()[:, 0:H], xnfm.a(kc, (j * 128, 128)),
                        wg.a(kc), start=(kc == 0), stop=False),
                    reads=[xnfm.res, wg.res], writes=[rA], tag=f"zg_{ti}_{kc}")
            p.T(lambda e: e.matmul(psA.ap()[:, 0:H],
                                   onesb.a((0, 128), p0=0, pn=1),
                                   bias_b["bg"].a(p0=0, pn=1),
                                   start=False, stop=True),
                reads=[onesb.res, bias_b["bg"].res], writes=[rA],
                tag=f"zg_b_{ti}")
            p.S(lambda e: e.activation(ezg.a(), psA.ap()[:, 0:H], AF.Exp),
                reads=[rA], writes=[ezg.res], tag=f"zg_e_{ti}")
            p.S(lambda e, ti=ti: e.activation(logg.a(ti), ezg.a(), AF.Ln,
                                              bias=1.0),
                reads=[ezg.res], writes=[logg.res], tag=f"zg_l_{ti}")
        c0 = (tt % 4) * 4
        for prn, wn, bn, sres, sdst in (("sq", "wsq", "bsq", rsq, ssq),
                                        ("sk", "wsk", "bsk", rsk, ssk)):
            st = stg[prn][0]
            for ncx in range(NFC):
                bank = ncx % 2
                for kc in range(NFC):
                    p.T(lambda e, kc=kc, ncx=ncx, bank=bank, wn=wn, xnfm=xnfm: e.matmul(
                            bigap(bank), wA[wn].a(kc, (ncx * 128, 128)),
                            xnfm.a(kc), start=(kc == 0), stop=False),
                        reads=[wA[wn].res, xnfm.res], writes=[rBig[bank]],
                        tag=f"{prn}_{tt}_{ncx}_{kc}")
                p.T(lambda e, ncx=ncx, bank=bank, bn=bn: e.matmul(
                        bigap(bank), bias_b[bn].a((ncx * 128, 128), p0=0, pn=1),
                        onesb.a(p0=0, pn=1), start=False, stop=True),
                    reads=[bias_b[bn].res, onesb.res], writes=[rBig[bank]],
                    tag=f"{prn}b_{tt}_{ncx}")
                p.S(lambda e, ncx=ncx, bank=bank, st=st: e.copy(
                        st.a(ncx), bigap(bank)),
                    reads=[rBig[bank]], writes=[st.res], tag=f"{prn}e_{tt}_{ncx}")
            for q in range(4):
                cix = (tt % 4) * 4 + q
                p.store(sdst[tt // 4, cix],
                        st.a(None, (q * 128, 128)),
                        reads=[st.res], writes=[sres[tt // 4][cix]],
                        tag=f"{prn}s_{tt}_{q}")
        for j in range(4):
            ti = tt * 4 + j
            cix = ti % 16
            for prn, wn, bn, stb in (("sv", "wsv", "bsv", svst),
                                     ("V", "wv", "bv", Vst)):
                for kc in range(NFC):
                    for nh in range(2):
                        p.T(lambda e, kc=kc, nh=nh, j=j, wn=wn, xnfm=xnfm: e.matmul(
                                bigap(2 + nh), xnfm.a(kc, (j * 128, 128)),
                                wA[wn].a(kc, (nh * 512, 512)),
                                start=(kc == 0), stop=False),
                            reads=[xnfm.res, wA[wn].res], writes=[rBig[2 + nh]],
                            tag=f"{prn}_{ti}_{kc}_{nh}")
                for nh in range(2):
                    p.T(lambda e, nh=nh, bn=bn: e.matmul(
                            bigap(2 + nh), onesb.a((0, 128), p0=0, pn=1),
                            bias_b[bn].a((nh * 512, 512), p0=0, pn=1),
                            start=False, stop=True),
                        reads=[onesb.res, bias_b[bn].res], writes=[rBig[2 + nh]],
                        tag=f"{prn}b_{ti}_{nh}")
                sb_ = stb[ti % 2]
                p.V(lambda e, sb_=sb_: e.tensor_copy(
                        sb_.a(), psX.ap()),
                    reads=[rBig[2], rBig[3]], writes=[sb_.res],
                    tag=f"{prn}e_{ti}")
                if prn == "sv":
                    p.store(ssv[b, cix], sb_.a(), reads=[sb_.res],
                            writes=[rsv[b][cix]], tag=f"svs_{ti}")
                else:
                    p.store(sVt[b, cix], sb_.a().rearrange(
                                "p (h e) -> p h e", e=HD),
                            reads=[sb_.res], writes=[rV[b][cix]],
                            tag=f"Vs_{ti}")
    AF32.free(xin[0], xin[1], x2s, ftmp[0], ftmp[1], ftg, bf_tmp,
              ezg, xnf[0], xnf[1], tmp16, sm)
    AB.free(wA["wsq"], wA["wsk"], wA["wsv"], wA["wv"], wg,
            xnbf[0], xnbf[1], xnfmr[0], xnfmr[1], xqtm[0], xqtm[1],
            xqst[0], xqst[1],
            stg["sq"][0], stg["sk"][0], svst[0], svst[1], Vst[0], Vst[1],
            bias_b["bsq"], bias_b["bsk"], bias_b["bsv"], bias_b["bv"],
            bias_b["bg"])
    free_qscr(qscr2[0])
    if upto == "a1":
        p.lower(nc, ec)
        return nc, stack
    _phases_bcd(nc, d, out_t, p, AF32, AB, PS, SC, locals(), upto)
    p.lower(nc, ec)
    return nc, stack


def _phases_bcd(nc, d, out_t, p, AF32, AB, PS, SC, env, upto):
    psW, psX, psA, psB = PS["psW"], PS["psX"], PS["psA"], PS["psB"]
    psTa, psTb = PS["psTa"], PS["psTb"]
    rBig, rA, rB, rTa, rTb = PS["rBig"], PS["rA"], PS["rB"], PS["rTa"], PS["rTb"]
    bigap = PS["bigap"]
    ssq, ssk, ssv, sVt = SC["ssq"], SC["ssk"], SC["ssv"], SC["sV"]
    sxq, shq, soutp = SC["sxq"], SC["shq"], SC["soutp"]
    rsq, rsk, rsv, rV = SC["rsq"], SC["rsk"], SC["rsv"], SC["rV"]
    rxq, rhq, routp = SC["rxq"], SC["rhq"], SC["routp"]
    logg = env["logg"]
    identb, identf = env["identb"], env["identf"]
    triLneg, triUneg, maskLN = env["triLneg"], env["triUneg"], env["maskLN"]
    pfull, pfullT = env["pfull"], env["pfullT"]
    onesb, onesf = env["onesb"], env["onesf"]
    bias_b, sp16 = env["bias_b"], env["sp16"]

    ftc = [AF32.alloc(f"ftc{i}", [1024]) for i in range(2)]

    def load_w(nm, dn, tagp):
        reg = AB.alloc(nm, [NFC, 1024])
        for kc in range(NFC):
            ft = ftc[kc % 2]
            p.load(ft.a(), d[dn][kc * 128:(kc + 1) * 128, :], writes=[ft.res],
                   tag=f"{tagp}{nm}l{kc}")
            p.V(lambda e, r=reg, kc=kc, ft=ft: e.tensor_copy(r.a(kc), ft.a()),
                reads=[ft.res], writes=[reg.res], tag=f"{tagp}{nm}c{kc}")
        return reg

    Mf = AF32.alloc("Mf", [H, HD])
    lld = AF32.alloc("lld", [32])
    nl = AF32.alloc("nl", [16])
    ek2 = AF32.alloc("ek2", [16])
    eend = AF32.alloc("eend", [NFC])
    lTr = AF32.alloc("lTr", [H, 128])
    lTsb = AF32.alloc("lTsb", [128])

    for b in range(NB):
        Hseq = AB.alloc("Hseq", [NFC, T])
        _DBG["Hseq"] = Hseq
        sqg = [AB.alloc(f"sqg{i}", [NFC, 128]) for i in range(2)]
        skg = [AB.alloc(f"skg{i}", [NFC, 128]) for i in range(2)]
        svc = [AB.alloc(f"svc{i}", [DIM]) for i in range(2)]
        Mbf = AB.alloc("Mbf", [H, HD])
        qp = AB.alloc("qp", [NFC, 128])
        Dm = [AB.alloc(f"Dm{i}", [128]) for i in range(16)]
        SDm = [AB.alloc(f"SDm{i}", [128]) for i in range(2)]
        elT = AB.alloc("elT", [128])
        ktil = AB.alloc("ktil", [H, HD])
        p.V(lambda e: e.memset(Mf.a(), 0.0), writes=[Mf.res], tag=f"M0_{b}")
        for c in range(NCH):
            sq_, sk_ = sqg[c % 2], skg[c % 2]
            p.load(sq_.a(), ssq[b, c], reads=[rsq[b][c]],
                   writes=[sq_.res], tag=f"Bq{b}_{c}")
            p.load(sk_.a(), ssk[b, c], reads=[rsk[b][c]],
                   writes=[sk_.res], tag=f"Bk{b}_{c}")
            sv_ = svc[c % 2]
            p.load(sv_.a(), ssv[b, c], reads=[rsv[b][c]], writes=[sv_.res],
                   tag=f"Bv{b}_{c}")
            ti = b * NCH + c
            p.T(lambda e, ti=ti: e.matmul(psA.ap()[:, 0:H], triLneg.a(),
                                          logg.a(ti), start=True, stop=True),
                reads=[triLneg.res, logg.res], writes=[rA], tag=f"cums{ti}")
            p.T(lambda e, ti=ti: e.matmul(psA.ap()[:, H:2 * H], triUneg.a(),
                                          logg.a(ti), start=True, stop=True),
                reads=[triUneg.res, logg.res], writes=[rA], tag=f"ldif{ti}")
            p.V(lambda e: e.tensor_copy(lld.a((0, 32)), psA.ap()[:, 0:32]),
                reads=[rA], writes=[lld.res], tag=f"lev{ti}")
            p.V(lambda e: e.tensor_scalar(nl.a(), lld.a((0, 16)), -1.0, None,
                                          ALU.mult),
                reads=[lld.res], writes=[nl.res], tag=f"nl{ti}")
            p.S(lambda e: e.activation(ek2.a(), lld.a((16, 16)), AF.Exp),
                reads=[lld.res], writes=[ek2.res], tag=f"ek2{ti}")
            p.T(lambda e: e.transpose(psB.ap()[0:16, 0:128], lld.a((0, 16)),
                                      identf.a()),
                reads=[lld.res, identf.res], writes=[rB], tag=f"lT{ti}")
            p.S(lambda e: e.activation(elT.a(p0=0, pn=16),
                                       psB.ap()[0:16, 0:128], AF.Exp),
                reads=[rB], writes=[elT.res], tag=f"elT{ti}")
            for h in range(H):
                p.T(lambda e, h=h: e.transpose(psB.ap()[0:1, 128:256],
                                               lld.a((h, 1)), identf.a()),
                    reads=[lld.res, identf.res], writes=[rB], tag=f"lr{ti}_{h}")
                p.V(lambda e, h=h: e.tensor_copy(lTr.a(h, p0=0, pn=1),
                                                 psB.ap()[0:1, 128:256]),
                    reads=[rB], writes=[lTr.res], tag=f"lrc{ti}_{h}")
            for fc in range(NFC):
                p.T(lambda e, fc=fc: e.matmul(
                        bigap(fc // 4)[:, (fc % 4) * 128:(fc % 4) * 128 + 128],
                        pfull.a((fc * 128, 128), p0=0, pn=16),
                        elT.a(p0=0, pn=16), start=True, stop=True),
                    reads=[pfull.res, elT.res], writes=[rBig[fc // 4]],
                    tag=f"eg{ti}_{fc}")
            for fc in range(NFC):
                p.V(lambda e, fc=fc, sq_=sq_: e.tensor_tensor(
                        qp.a(fc), sq_.a(fc),
                        bigap(fc // 4)[:, (fc % 4) * 128:(fc % 4) * 128 + 128],
                        ALU.mult),
                    reads=[sq_.res, rBig[fc // 4]], writes=[qp.res],
                    tag=f"qp{ti}_{fc}")
            p.V(lambda e: e.tensor_copy(
                    eend.a(), bass.AP(psW, 127, [[1024, 128], [128, 8]])),
                reads=[rBig[0], rBig[1]], writes=[eend.res], tag=f"ee{ti}")
            p.V(lambda e: e.tensor_copy(Mbf.a(), Mf.a()),
                reads=[Mf.res], writes=[Mbf.res], tag=f"mc{ti}")
            for h in range(H):
                pban = psA if h % 2 == 0 else psB
                rban = rA if h % 2 == 0 else rB
                p.T(lambda e, h=h, pban=pban: e.matmul(
                        pban.ap()[:, 256:384], onesf.a((0, 128), p0=0, pn=1),
                        lTr.a(h, p0=0, pn=1), start=True, stop=False),
                    reads=[onesf.res, lTr.res], writes=[rban],
                    tag=f"lbc{ti}_{h}")
                p.T(lambda e, pban=pban: e.matmul(
                        pban.ap()[:, 256:384], identb.a(), maskLN.a(),
                        start=False, stop=True),
                    reads=[identb.res, maskLN.res], writes=[rban],
                    tag=f"lbm{ti}_{h}")
                p.S(lambda e, h=h, pban=pban: e.activation(
                        Dm[h].a(), pban.ap()[:, 256:384], AF.Exp,
                        bias=nl.a((h, 1))),
                    reads=[rban, nl.res], writes=[Dm[h].res],
                    tag=f"dex{ti}_{h}")
            for fc in range(NFC):
                p.T(lambda e, fc=fc, sk_=sk_: e.transpose(
                        psTa.ap()[:, fc * 128:(fc + 1) * 128],
                        sk_.a(fc), identb.a()),
                    reads=[sk_.res, identb.res], writes=[rTa],
                    tag=f"ktr{ti}_{fc}")
            for h in range(H):
                fc, off = h // 2, (h % 2) * 64
                p.V(lambda e, h=h, fc=fc, off=off: e.tensor_scalar(
                        ktil.a(h),
                        psTa.ap()[:, fc * 128 + off:fc * 128 + off + 64],
                        ek2.a((h, 1)), None, ALU.mult),
                    reads=[rTa, ek2.res], writes=[ktil.res],
                    tag=f"kts{ti}_{h}")
            for h in range(H):
                fc, off = h // 2, (h % 2) * 64
                sb_ = 2 + h % 2
                hb = h % 2
                if h % 2 == 0:
                    def qkfn(e, fc=fc, sk_=sk_, sq_=sq_):
                        e.matmul(bigap(2)[:, 0:128],
                                 sk_.a(fc, p0=0, pn=64),
                                 sq_.a(fc, p0=0, pn=64),
                                 start=True, stop=True)
                        return e.matmul(bigap(3)[:, 0:128],
                                        sk_.a(fc, p0=64, pn=64),
                                        sq_.a(fc, p0=64, pn=64),
                                        start=True, stop=True)
                    p.T(qkfn, reads=[sk_.res, sq_.res],
                        writes=[rBig[2], rBig[3]], tag=f"qk{ti}_{h}")
                p.V(lambda e, h=h, sb_=sb_: e.scalar_tensor_tensor(
                        SDm[h % 2].a(), bigap(sb_)[:, 0:128], 1.0,
                        Dm[h].a(), ALU.mult, ALU.mult),
                    reads=[rBig[sb_], Dm[h].res], writes=[SDm[h % 2].res],
                    tag=f"sdm{ti}_{h}")
                p.T(lambda e, h=h, hb=hb: e.matmul(
                        bigap(hb)[0:64, 128:256], Mbf.a(h), qp.a(h // 2),
                        start=True, stop=False),
                    reads=[Mbf.res, qp.res], writes=[rBig[hb]],
                    tag=f"hin{ti}_{h}")
                p.T(lambda e, h=h, hb=hb, sv_=sv_: e.matmul(
                        bigap(hb)[0:64, 128:256], sv_.a((h * HD, HD)),
                        SDm[h % 2].a(), start=False, stop=True),
                    reads=[sv_.res, SDm[h % 2].res], writes=[rBig[hb]],
                    tag=f"hsv{ti}_{h}")
                p.V(lambda e, h=h, hb=hb, fc=fc, off=off, c=c: e.tensor_copy(
                        Hseq.a(fc, (c * 128, 128), p0=off, pn=64),
                        bigap(hb)[0:64, 128:256]),
                    reads=[rBig[hb]], writes=[Hseq.res], tag=f"hev{ti}_{h}")
                db = psA if h % 2 == 0 else psB
                rdb = rA if h % 2 == 0 else rB
                p.T(lambda e, h=h, db=db, off=off, sv_=sv_: e.matmul(
                        db.ap()[off:off + 64, 384:448], ktil.a(h),
                        sv_.a((h * HD, HD)), start=True, stop=True,
                        tile_position=(0, off)),
                    reads=[ktil.res, sv_.res], writes=[rdb], tag=f"dlt{ti}_{h}")
                p.V(lambda e, h=h, db=db, off=off, fc=fc:
                    e.scalar_tensor_tensor(
                        Mf.a(h, p0=off, pn=64), Mf.a(h, p0=off, pn=64),
                        eend.a((fc, 1), p0=off, pn=64),
                        db.ap()[off:off + 64, 384:448], ALU.mult, ALU.add),
                    reads=[Mf.res, eend.res, rdb], writes=[Mf.res],
                    tag=f"mup{ti}_{h}")
        AB.free(sqg[0], sqg[1], skg[0], skg[1], svc[0], svc[1], Mbf, qp,
                *Dm, SDm[0], SDm[1], elT, ktil)
        if upto == "b":
            if b == 0:
                AB.free(Hseq)
            continue
        Qhat = AB.alloc("Qhat", [NFC, T])
        Khat = AB.alloc("Khat", [NFC, T])
        _DBG["Qhat"], _DBG["Khat"] = Qhat, Khat
        Rraw = AF32.alloc("Rraw", [NFC, 512])
        sqb = AB.alloc("sqb", [NFC, 512])
        rq = AF32.alloc("rq", [512])
        rqb = AB.alloc("rqb", [512])
        for prn, wn, bn, dst in (("q", "wqT", "bq", Qhat),
                                 ("k", "wkT", "bk", Khat)):
            wqk = load_w("wqk", wn, f"c{b}{prn}")
            for qg in range(4):
                for ncx in range(NFC):
                    bank = ncx % 2
                    for kc in range(NFC):
                        p.T(lambda e, kc=kc, ncx=ncx, bank=bank, qg=qg,
                                   wqk=wqk: e.matmul(
                                bigap(bank), wqk.a(kc, (ncx * 128, 128)),
                                Hseq.a(kc, (qg * 512, 512)),
                                start=(kc == 0), stop=False),
                            reads=[wqk.res, Hseq.res], writes=[rBig[bank]],
                            tag=f"{prn}p{b}_{qg}_{ncx}_{kc}")
                    p.T(lambda e, ncx=ncx, bank=bank, bn=bn: e.matmul(
                            bigap(bank),
                            bias_b[bn].a((ncx * 128, 128), p0=0, pn=1),
                            onesb.a(p0=0, pn=1), start=False, stop=True),
                        reads=[bias_b[bn].res, onesb.res], writes=[rBig[bank]],
                        tag=f"{prn}pb{b}_{qg}_{ncx}")
                    p.V(lambda e, ncx=ncx, bank=bank: e.tensor_copy(
                            Rraw.a(ncx), bigap(bank)),
                        reads=[rBig[bank]], writes=[Rraw.res],
                        tag=f"{prn}ev{b}_{qg}_{ncx}")
                    p.S(lambda e, ncx=ncx: e.activation(
                            sqb.a(ncx), Rraw.a(ncx), AF.Square),
                        reads=[Rraw.res], writes=[sqb.res],
                        tag=f"{prn}sq{b}_{qg}_{ncx}")
                for fc in range(NFC):
                    p.T(lambda e, fc=fc: e.matmul(
                            psA.ap()[0:16, 0:512], pfullT.a(fc), sqb.a(fc),
                            start=(fc == 0), stop=(fc == NFC - 1)),
                        reads=[pfullT.res, sqb.res], writes=[rA],
                        tag=f"{prn}ss{b}_{qg}_{fc}")
                p.V(lambda e: e.tensor_scalar(rq.a(p0=0, pn=16),
                                              psA.ap()[0:16, 0:512],
                                              1e-24, None, ALU.max),
                    reads=[rA], writes=[rq.res], tag=f"{prn}mx{b}_{qg}")
                p.S(lambda e: e.activation(rq.a(p0=0, pn=16),
                                           rq.a(p0=0, pn=16), AF.Ln),
                    reads=[rq.res], writes=[rq.res], tag=f"{prn}ln{b}_{qg}")
                p.S(lambda e: e.activation(rq.a(p0=0, pn=16),
                                           rq.a(p0=0, pn=16), AF.Exp,
                                           scale=-0.5),
                    reads=[rq.res], writes=[rq.res], tag=f"{prn}ex{b}_{qg}")
                if prn == "k":
                    p.V(lambda e: e.tensor_scalar(rq.a(p0=0, pn=16),
                                                  rq.a(p0=0, pn=16),
                                                  sp16.a(p0=0, pn=16), None,
                                                  ALU.mult),
                        reads=[rq.res, sp16.res], writes=[rq.res],
                        tag=f"ksp{b}_{qg}")
                p.V(lambda e: e.tensor_copy(rqb.a(p0=0, pn=16),
                                            rq.a(p0=0, pn=16)),
                    reads=[rq.res], writes=[rqb.res], tag=f"{prn}cb{b}_{qg}")
                for fc in range(NFC):
                    pban = psA if fc % 2 == 0 else psB
                    rban = rA if fc % 2 == 0 else rB
                    p.T(lambda e, fc=fc, pban=pban: e.matmul(
                            pban.ap()[:, 0:512],
                            pfull.a((fc * 128, 128), p0=0, pn=16),
                            rqb.a(p0=0, pn=16), start=True, stop=True),
                        reads=[pfull.res, rqb.res], writes=[rban],
                        tag=f"{prn}rg{b}_{qg}_{fc}")
                    p.V(lambda e, fc=fc, pban=pban, rban=rban, dst=dst, qg=qg:
                        e.tensor_tensor(dst.a(fc, (qg * 512, 512)),
                                        Rraw.a(fc), pban.ap()[:, 0:512],
                                        ALU.mult),
                        reads=[Rraw.res, rban], writes=[dst.res],
                        tag=f"{prn}hm{b}_{qg}_{fc}")
            AB.free(wqk)
        AB.free(sqb, rqb)
        AF32.free(Rraw, rq)
        AB.free(Hseq)
        if upto == "c":
            if b == 0:
                AB.free(Qhat, Khat)
            continue
        wo = load_w("wo", "woT", f"d{b}")
        Vaug = AB.alloc("Vaug", [H, NCH, 65])
        Esb = [AB.alloc(f"Esb{i}", [2048]) for i in range(2)]
        attnfm = AB.alloc("attnfm", [NFC, 512])
        r64 = [AB.alloc(f"r64{i}", [512]) for i in range(2)]
        rden = AF32.alloc("rden", [512])
        rden2 = AF32.alloc("rden2", [512])
        r64b = [AB.alloc(f"r64b{i}", [512]) for i in range(2)]
        xres = [AF32.alloc(f"xres{i}", [1024]) for i in range(2)]
        ofin = [AF32.alloc(f"ofin{i}", [1024]) for i in range(2)]
        p.V(lambda e: e.memset(Vaug.a(None, None, (64, 1)), 1.0),
            writes=[Vaug.res], tag=f"vaug1_{b}")
        for c in range(NCH):
            p.load(Vaug.a(None, c, (0, 64)), sVt[b, c],
                   reads=[rV[b][c]], writes=[Vaug.res], tag=f"VL{b}_{c}")
        for qg in range(4):
            for hp in range(NFC):
                h0, h1 = 2 * hp, 2 * hp + 1
                cfg = [(h0, 0, psW, (rBig[0], rBig[1]), psA, rA, Esb[0]),
                       (h1, 64, psX, (rBig[2], rBig[3]), psB, rB, Esb[1])]

                def sc_pair(g):
                    for j in range(2):
                        kc = g * 2 + j
                        def fn(e, kc=kc, j=j, qg=qg, hp=hp):
                            e.matmul(
                                psW.ap()[:, j * 512:(j + 1) * 512],
                                Khat.a(hp, (kc * 128, 128), p0=0, pn=64),
                                Qhat.a(hp, (qg * 512, 512), p0=0, pn=64),
                                start=True, stop=True)
                            return e.matmul(
                                psX.ap()[:, j * 512:(j + 1) * 512],
                                Khat.a(hp, (kc * 128, 128), p0=64, pn=64),
                                Qhat.a(hp, (qg * 512, 512), p0=64, pn=64),
                                start=True, stop=True)
                        p.T(fn, reads=[Khat.res, Qhat.res],
                            writes=[rBig[j], rBig[2 + j]],
                            tag=f"sc{b}_{qg}_{hp}_{g}_{j}")

                def sc_group(ci, g):
                    if ci == 0:
                        sc_pair(g)

                def exp_group(ci, g):
                    h, off, pt, prs, _, _, eb = cfg[ci]
                    p.S(lambda e, eb=eb, pt=pt: e.activation(
                            eb.a((0, 1024)), pt.ap(), AF.Exp),
                        reads=[prs[0], prs[1]], writes=[eb.res],
                        tag=f"ex{b}_{qg}_{h}_{g}")

                def ev_group(ci, g):
                    h, off, pt, prs, pacc, racc, eb = cfg[ci]
                    for j in range(2):
                        kc = g * 2 + j
                        p.T(lambda e, h=h, kc=kc, j=j, eb=eb, pacc=pacc:
                            e.matmul(
                                pacc.ap()[0:65, 0:512], Vaug.a(h, kc),
                                eb.a((j * 512, 512)),
                                start=(kc == 0), stop=(kc == 15)),
                            reads=[Vaug.res, eb.res], writes=[racc],
                            tag=f"ev{b}_{qg}_{h}_{kc}")

                for g in range(8):
                    for ci in (0, 1):
                        sc_group(ci, g)
                        exp_group(ci, g)
                        ev_group(ci, g)
                for ci in (0, 1):
                    h, off, pt, prs, pacc, racc, eb = cfg[ci]
                    rd = rden if ci == 0 else rden2
                    rr0 = r64[0] if ci == 0 else r64b[0]
                    rr1 = r64[1] if ci == 0 else r64b[1]
                    p.V(lambda e, pacc=pacc, rd=rd: e.tensor_copy(
                            rd.a((0, 512), p0=0, pn=1),
                            pacc.ap()[64:65, 0:512]),
                        reads=[racc], writes=[rd.res], tag=f"rcp{b}_{qg}_{h}")
                    p.S(lambda e, rd=rd: e.activation(
                            rd.a((0, 512), p0=0, pn=1),
                            rd.a((0, 512), p0=0, pn=1), AF.Ln),
                        reads=[rd.res], writes=[rd.res], tag=f"rc{b}_{qg}_{h}")
                    p.S(lambda e, rd=rd, rr0=rr0: e.activation(
                            rr0.a(p0=0, pn=1), rd.a((0, 512), p0=0, pn=1),
                            AF.Exp, scale=-1.0),
                        reads=[rd.res], writes=[rr0.res], tag=f"rcb{b}_{qg}_{h}")
                    p.T(lambda e, pt=pt, rr0=rr0: e.matmul(
                            pt.ap()[0:64, 0:512], onesb.a((0, 64), p0=0, pn=1),
                            rr0.a(p0=0, pn=1), start=True, stop=True),
                        reads=[onesb.res, rr0.res], writes=[prs[0]],
                        tag=f"rbc{b}_{qg}_{h}")
                    p.S(lambda e, pt=pt, rr1=rr1: e.activation(
                            rr1.a(p0=0, pn=64), pt.ap()[0:64, 0:512], AF.Copy),
                        reads=[prs[0]], writes=[rr1.res], tag=f"rev{b}_{qg}_{h}")
                    p.V(lambda e, h=h, off=off, pacc=pacc, rr1=rr1, hp=hp:
                        e.tensor_tensor(
                            attnfm.a(hp, p0=off, pn=64),
                            pacc.ap()[0:64, 0:512], rr1.a(p0=0, pn=64),
                            ALU.mult),
                        reads=[racc, rr1.res], writes=[attnfm.res],
                        tag=f"at{b}_{qg}_{h}")
            for t4 in range(4):
                ti = b * NCH + qg * 4 + t4
                trow = (qg * 4 + t4) * 128
                xr = xres[t4 % 2]
                p.load(xr.a(), d["x"][b, trow:trow + 128, :], writes=[xr.res],
                       tag=f"xl{b}_{qg}_{t4}")
                for nh in range(2):
                    for kc in range(NFC):
                        p.T(lambda e, kc=kc, nh=nh, t4=t4: e.matmul(
                                psX.ap()[:, nh * 512:(nh + 1) * 512],
                                attnfm.a(kc, (t4 * 128, 128)),
                                wo.a(kc, (nh * 512, 512)),
                                start=(kc == 0), stop=False),
                            reads=[attnfm.res, wo.res],
                            writes=[rBig[2 + nh]],
                            tag=f"wo{b}_{qg}_{t4}_{kc}_{nh}")
                    p.T(lambda e, nh=nh: e.matmul(
                            psX.ap()[:, nh * 512:(nh + 1) * 512],
                            onesb.a((0, 128), p0=0, pn=1),
                            bias_b["bo"].a((nh * 512, 512), p0=0, pn=1),
                            start=False, stop=True),
                        reads=[onesb.res, bias_b["bo"].res],
                        writes=[rBig[2 + nh]], tag=f"wob{b}_{qg}_{t4}_{nh}")
                of = ofin[t4 % 2]
                p.V(lambda e, xr=xr, of=of: e.tensor_tensor(
                        of.a(), psX.ap(), xr.a(), ALU.add),
                    reads=[rBig[2], rBig[3], xr.res], writes=[of.res],
                    tag=f"oadd{b}_{qg}_{t4}")
                p.store(soutp[ti], of.a(), reads=[of.res],
                        writes=[routp[ti]], tag=f"ost{b}_{qg}_{t4}")
        AB.free(Vaug, Esb[0], Esb[1], attnfm, r64[0], r64[1], r64b[0],
                r64b[1], Qhat, Khat, wo)
        AF32.free(rden, rden2, xres[0], xres[1], ofin[0], ofin[1])
    AF32.free(Mf, lld, nl, ek2, eend, lTr, lTsb, ftc[0], ftc[1])
    if upto in ("b", "c", "d"):
        return
    _phase_a2(nc, d, out_t, p, AF32, AB, PS, SC, env)


def _phase_a2(nc, d, out_t, p, AF32, AB, PS, SC, env):
    psW, psX, psA, psB = PS["psW"], PS["psX"], PS["psA"], PS["psB"]
    psTa, psTb = PS["psTa"], PS["psTb"]
    rBig, rA, rB, rTa, rTb = PS["rBig"], PS["rA"], PS["rB"], PS["rTa"], PS["rTb"]
    bigap = PS["bigap"]
    sxq, shq, soutp = SC["sxq"], SC["shq"], SC["soutp"]
    rxq, rhq, routp = SC["rxq"], SC["rhq"], SC["routp"]
    identb = env["identb"]
    mk_qscr, free_qscr = env["mk_qscr"], env["free_qscr"]

    IG = [(0, 512), (512, 512), (1024, 512), (1536, 512), (2048, 512),
          (2560, 256)]
    AF32.free(env["logg"])

    wqg = AB.alloc("wqg", [NFC, FFI])
    wqu = AB.alloc("wqu", [NFC, FFI])
    xq = [AB.alloc(f"a2xq{i}", [NFC, 128]) for i in range(2)]
    hqr = [AB.alloc(f"hq{i}", [FFI]) for i in range(2)]
    qtile = hqr[0]
    hqs = [AB.alloc(f"hqs{i}", [4, 128]) for i in range(2)]
    qs2 = [mk_qscr(1408, "qw_"), mk_qscr(1408, "qx_")]
    qs = qs2[0]
    ftw = [AF32.alloc(f"ftw{i}", [1024]) for i in range(2)]

    def prep_w_rows(dsrc, wdst, nrt, rowlen, tag):
        for rt in range(nrt):
            ft = ftw[rt % 2]
            p.load(ft.a((0, rowlen)), dsrc[rt * 128:(rt + 1) * 128, :],
                   writes=[ft.res], tag=f"{tag}l{rt}")
            for qi in range((rowlen + 1407) // 1408):
                off = qi * 1408
                ln = min(1408, rowlen - off)
                qv = qs2[rt % 2]
                qv["xres"] = ft.res
                qv["outres"] = qtile.res
                _quant_emit(p, nc, None, ft.a((off, ln)),
                            qtile.a((off, ln)), ln // 16, qv,
                            tag=f"{tag}q{rt}_{qi}_")
            nch = rowlen // 128
            for ch in range(nch):
                pt = psTa if ch % 2 == 0 else psTb
                rt_ = rTa if ch % 2 == 0 else rTb
                p.T(lambda e, ch=ch, pt=pt: e.transpose(
                        pt.ap()[:, 0:128], qtile.a((ch * 128, 128)), identb.a()),
                    reads=[qtile.res, identb.res], writes=[rt_],
                    tag=f"{tag}t{rt}_{ch}")
                p.S(lambda e, ch=ch, rt=rt, pt=pt, wdst=wdst: e.copy(
                        wdst.a(ch, (rt * 128, 128)), pt.ap()[:, 0:128]),
                    reads=[rt_], writes=[wdst.res], tag=f"{tag}e{rt}_{ch}")
    prep_w_rows(d["wgate"], wqg, NIC, 1024, "wg_")
    prep_w_rows(d["wup"], wqu, NIC, 1024, "wu_")
    AF32.free(ftw[0], ftw[1])

    hbr = [AF32.alloc(f"hbuf{i}", [FFI]) for i in range(2)]
    sgr = [AF32.alloc("sg0", [512])]
    qh = qs
    for ti in range(NT128):
        xt = xq[ti % 2]
        hb, sg, hq = hbr[ti % 2], sgr[0], hqr[ti % 2]
        p.load(xt.a(), sxq[ti], reads=[rxq[ti]], writes=[xt.res],
               tag=f"a2x{ti}")
        for gi, (i0, ilen) in enumerate(IG):
            for kc in range(NFC):
                p.T(lambda e, kc=kc, i0=i0, ilen=ilen, xt=xt: e.matmul(
                        bigap(0)[:, 0:ilen], xt.a(kc), wqg.a(kc, (i0, ilen)),
                        start=(kc == 0), stop=(kc == NFC - 1)),
                    reads=[xt.res, wqg.res], writes=[rBig[0]],
                    tag=f"gt{ti}_{gi}_{kc}")
                p.T(lambda e, kc=kc, i0=i0, ilen=ilen, xt=xt: e.matmul(
                        bigap(1)[:, 0:ilen], xt.a(kc), wqu.a(kc, (i0, ilen)),
                        start=(kc == 0), stop=(kc == NFC - 1)),
                    reads=[xt.res, wqu.res], writes=[rBig[1]],
                    tag=f"up{ti}_{gi}_{kc}")
            p.S(lambda e, ilen=ilen, sg=sg: e.activation(sg.a((0, ilen)),
                                                  bigap(0)[:, 0:ilen], AF.Silu),
                reads=[rBig[0]], writes=[sg.res], tag=f"si{ti}_{gi}")
            p.V(lambda e, i0=i0, ilen=ilen, hb=hb, sg=sg: e.tensor_tensor(
                    hb.a((i0, ilen)), sg.a((0, ilen)), bigap(1)[:, 0:ilen],
                    ALU.mult),
                reads=[sg.res, rBig[1]], writes=[hb.res], tag=f"hm{ti}_{gi}")
        for qi in range(2):
            off, ln = qi * 1408, 1408
            qv = qs2[qi]
            qv["xres"] = hb.res
            qv["outres"] = hq.res
            _quant_emit(p, nc, None, hb.a((off, ln)), hq.a((off, ln)),
                        ln // 16, qv, tag=f"qh{ti}_{qi}_")
        ch = 0
        grp = 0
        while ch < NIC:
            n4 = min(4, NIC - ch)
            hs = hqs[grp % 2]
            for q2 in range(n4):
                pt = psTa if q2 % 2 == 0 else psTb
                rt_ = rTa if q2 % 2 == 0 else rTb
                p.T(lambda e, ch=ch, q2=q2, pt=pt, hq=hq: e.transpose(
                        pt.ap()[:, 0:128], hq.a(((ch + q2) * 128, 128)),
                        identb.a()),
                    reads=[hq.res, identb.res], writes=[rt_],
                    tag=f"ht{ti}_{ch + q2}")
                p.S(lambda e, q2=q2, pt=pt, hs=hs: e.copy(
                        hs.a(q2), pt.ap()[:, 0:128]),
                    reads=[rt_], writes=[hs.res], tag=f"he{ti}_{ch + q2}")
            p.store(shq[ti][:, ch:ch + n4, :], hs.a((0, n4)), reads=[hs.res],
                    writes=[rhq[ti]], tag=f"hs{ti}_{grp}")
            ch += n4
            grp += 1
    AB.free(wqg, wqu, xq[0], xq[1], hqr[0], hqr[1], hqs[0], hqs[1])
    AF32.free(hbr[0], hbr[1], sgr[0])

    wqd = AB.alloc("wqd", [NIC, 1024])
    ftd = AF32.alloc("ftd", [FFI])
    for rt in range(NFC):
        p.load(ftd.a(), d["wdown"][rt * 128:(rt + 1) * 128, :],
               writes=[ftd.res], tag=f"wd_l{rt}")
        qdt2 = AB.alloc(f"qdt2_{rt}", [FFI]) if rt == 0 else env["qdt2"]
        env["qdt2"] = qdt2
        for qi in range(2):
            off, ln = qi * 1408, 1408
            qv = qs2[qi]
            qv["xres"] = ftd.res
            qv["outres"] = qdt2.res
            _quant_emit(p, nc, None, ftd.a((off, ln)), qdt2.a((off, ln)),
                        ln // 16, qv, tag=f"wd_q{rt}_{qi}_")
        for ch in range(NIC):
            pt = psTa if ch % 2 == 0 else psTb
            rt_ = rTa if ch % 2 == 0 else rTb
            p.T(lambda e, ch=ch, pt=pt, qdt2=qdt2: e.transpose(
                    pt.ap()[:, 0:128], qdt2.a((ch * 128, 128)), identb.a()),
                reads=[qdt2.res, identb.res], writes=[rt_],
                tag=f"wd_t{rt}_{ch}")
            p.S(lambda e, ch=ch, rt=rt, pt=pt: e.copy(
                    wqd.a(ch, (rt * 128, 128)), pt.ap()[:, 0:128]),
                reads=[rt_], writes=[wqd.res], tag=f"wd_e{rt}_{ch}")
    free_qscr(qs2[0])
    free_qscr(qs2[1])
    AF32.free(ftd)

    hqt = [AB.alloc(f"hqt{i}", [NIC, 128]) for i in range(2)]
    opt = [AF32.alloc(f"opt{i}", [1024]) for i in range(2)]
    fin = [AF32.alloc(f"fin{i}", [1024]) for i in range(2)]
    for ti in range(NT128):
        b, trow = ti // NCH, (ti % NCH) * 128
        ht = hqt[ti % 2]
        ot = opt[ti % 2]
        p.load(ht.a(), shq[ti], reads=[rhq[ti]], writes=[ht.res],
               tag=f"p2h{ti}")
        p.load(ot.a(), soutp[ti], reads=[routp[ti]], writes=[ot.res],
               tag=f"p2o{ti}")
        for nh in range(2):
            for ic in range(NIC):
                p.T(lambda e, ic=ic, nh=nh, ht=ht: e.matmul(
                        bigap(nh), ht.a(ic), wqd.a(ic, (nh * 512, 512)),
                        start=(ic == 0), stop=(ic == NIC - 1)),
                    reads=[ht.res, wqd.res], writes=[rBig[nh]],
                    tag=f"dn{ti}_{ic}_{nh}")
        fi = fin[ti % 2]
        p.V(lambda e, ot=ot, fi=fi: e.tensor_tensor(
                fi.a(), psW.ap(), ot.a(), ALU.add),
            reads=[rBig[0], rBig[1], ot.res], writes=[fi.res],
            tag=f"fadd{ti}")
        p.store(out_t[b, trow:trow + 128, :], fi.a(), reads=[fi.res],
                tag=f"out{ti}")


_NC_CACHE = {}


def _get_nc():
    if "nc" not in _NC_CACHE:
        nc, stack = build_program(upto="all")
        _NC_CACHE["nc"] = nc
        _NC_CACHE["stack"] = stack
    return _NC_CACHE["nc"]


def _consts():
    i128 = np.eye(128, dtype=np.float32)
    k = np.arange(128)
    import ml_dtypes
    BF = ml_dtypes.bfloat16
    pf = np.zeros((16, 1024), np.float32)
    for h in range(16):
        pf[h, h * 64:(h + 1) * 64] = 1.0
    pfT = np.zeros((128, NFC * 16), np.float32)
    for fc in range(8):
        for h2 in range(2):
            pfT[h2 * 64:(h2 + 1) * 64, fc * 16 + 2 * fc + h2] = 1.0
    return {
        "identb": i128.astype(BF), "identf": i128,
        "triLneg": -((k[:, None] <= k[None, :]).astype(np.float32)),
        "triUneg": -((k[:, None] > k[None, :]).astype(np.float32)),
        "maskLowNeg": ((k[:, None] > k[None, :]).astype(np.float32)
                       * np.float32(-1e30)).astype(BF),
        "pfull": pf.astype(BF), "pfullT": pfT.astype(BF),
        "onesb": np.ones((1, 512), BF), "onesf": np.ones((1, 128), np.float32),
    }


def kernel(**inputs):
    nc = _get_nc()
    consts = _consts()
    xfull = np.ascontiguousarray(np.asarray(inputs["L0_attn_x"], np.float32))
    shared = {
        "wsqT": np.ascontiguousarray(inputs["L0_attn_Wsq"].T),
        "wskT": np.ascontiguousarray(inputs["L0_attn_Wsk"].T),
        "wsvT": np.ascontiguousarray(inputs["L0_attn_Wsv"].T),
        "wvT": np.ascontiguousarray(inputs["L0_attn_Wv"].T),
        "wqT": np.ascontiguousarray(inputs["L0_attn_Wq"].T),
        "wkT": np.ascontiguousarray(inputs["L0_attn_Wk"].T),
        "woT": np.ascontiguousarray(inputs["L0_attn_Wo"].T),
        "wgT": np.ascontiguousarray(inputs["L0_attn_Wg"].T),
        "bsq": np.ascontiguousarray(inputs["L0_attn_bsq"]).reshape(1, -1),
        "bsk": np.ascontiguousarray(inputs["L0_attn_bsk"]).reshape(1, -1),
        "bsv": np.ascontiguousarray(inputs["L0_attn_bsv"]).reshape(1, -1),
        "bq": np.ascontiguousarray(inputs["L0_attn_bq"]).reshape(1, -1),
        "bk": np.ascontiguousarray(inputs["L0_attn_bk"]).reshape(1, -1),
        "bv": np.ascontiguousarray(inputs["L0_attn_bv"]).reshape(1, -1),
        "bo": np.ascontiguousarray(inputs["L0_attn_bo"]).reshape(1, -1),
        "bg": np.ascontiguousarray(inputs["L0_attn_bg"]).reshape(1, -1),
        "temp16": np.ascontiguousarray(
            np.asarray(inputs["L0_attn_temp"], np.float32).reshape(16, 1)),
        "wgate": np.ascontiguousarray(inputs["L0_ffn_w_gate"]),
        "wup": np.ascontiguousarray(inputs["L0_ffn_w_up"]),
        "wdown": np.ascontiguousarray(inputs["L0_ffn_w_down"]),
    }
    shared = {k: np.ascontiguousarray(np.asarray(v, np.float32))
              for k, v in shared.items()}
    shared.update(consts)
    in_maps = []
    for core in range(NCORES):
        m = dict(shared)
        m["x"] = np.ascontiguousarray(xfull[core * NB:(core + 1) * NB])
        in_maps.append(m)
    res = run_bass_kernel_spmd(nc, in_maps, core_ids=list(range(NCORES)))
    out = np.concatenate([r["out"] for r in res.results], axis=0)
    return out.astype(np.float32)


# version 28
# speedup vs baseline: 1.0074x; 1.0026x over previous
import numpy as np
from contextlib import ExitStack

import concourse.bass as bass
import concourse.mybir as mybir
from concourse.bass_utils import run_bass_kernel_spmd

F32 = mybir.dt.float32
BF16 = mybir.dt.bfloat16
AF = mybir.ActivationFunctionType
ALU = mybir.AluOpType
AX = mybir.AxisListType

B, T, DIM, H = 16, 2048, 1024, 16
HD = DIM // H
FFI = 2816
NCORES = 8
NB = B // NCORES
TOK = NB * T
NT128 = TOK // 128
NCH = T // 128
NFC = DIM // 128
NIC = FFI // 128
LN_EPS = 1e-5

FP4_MID = [0.25, 0.75, 1.25, 1.75, 2.5, 3.5, 5.0]
FP4_W = [0.5, 0.5, 0.5, 0.5, 1.0, 1.0, 2.0]

SAME_ENGINE_WAITS = False
DETECT_RACES = False


class Res:
    __slots__ = ("name", "writer", "readers", "preds")

    def __init__(self, name=""):
        self.name = name
        self.writer = None
        self.readers = []
        self.preds = []


class Op:
    __slots__ = ("eng", "fn", "ticket", "deps", "need_inc", "tag")

    def __init__(self, eng, fn, tag):
        self.eng = eng
        self.fn = fn
        self.ticket = None
        self.deps = []
        self.need_inc = False
        self.tag = tag


class Prog:
    NLANES = 8

    def __init__(self):
        self.warm_ap = None
        self.op_start = {}
        self.op_finish = {}
        self.ops = {k: [] for k in ("T", "V", "S", "G", "Y")}
        self.dma_count = {"DY": 0, "DG": 0}
        self.lane_last = {}
        self.lane_uses = {}
        self.n_ops = 0

    def emit(self, eng, fn, reads=(), writes=(), tag=""):
        op = Op(eng, fn, tag)
        seen = set()

        def dep(x):
            if x is not None and id(x) not in seen and x is not op:
                seen.add(id(x))
                op.deps.append(x)

        for r in reads:
            dep(r.writer)
        for r in writes:
            dep(r.writer)
            for rd in r.readers:
                dep(rd)
            for p in r.preds:
                dep(p)
        issue_eng = {"DY": "Y", "DG": "G"}.get(eng, eng)
        if eng in ("DY", "DG"):
            lane = self.dma_count[eng] % self.NLANES
            self.dma_count[eng] += 1
            key = (eng, lane)
            dep(self.lane_last.get(key))
            self.lane_last[key] = op
            self.lane_uses[key] = self.lane_uses.get(key, 0) + 1
            op.ticket = (f"{eng}{lane}", 16 * self.lane_uses[key])
        self.ops[issue_eng].append(op)
        self.n_ops += 1
        for r in reads:
            r.readers.append(op)
        for r in writes:
            r.writer = op
            r.readers = []
            r.preds = []
        return op

    def T(self, fn, reads=(), writes=(), tag=""):
        return self.emit("T", fn, reads, writes, tag)

    def V(self, fn, reads=(), writes=(), tag=""):
        return self.emit("V", fn, reads, writes, tag)

    def S(self, fn, reads=(), writes=(), tag=""):
        return self.emit("S", fn, reads, writes, tag)

    def G(self, fn, reads=(), writes=(), tag=""):
        return self.emit("G", fn, reads, writes, tag)

    def load(self, dst_ap, src_ap, reads=(), writes=(), tag="ld"):
        return self.emit("DY", lambda e: e.dma_start(dst_ap, src_ap),
                         reads, writes, tag)

    def store(self, dst_ap, src_ap, reads=(), writes=(), tag="st"):
        return self.emit("DG", lambda e: e.dma_start(dst_ap, src_ap),
                         reads, writes, tag)

    COST = {"T": 0.3, "V": 1.05, "S": 0.85, "G": 0.7, "Y": 0.1}

    def schedule(self):
        import heapq
        eng_of = {}
        idx_of = {}
        nsucc_sched = {}
        for eng, lst in self.ops.items():
            for i, op in enumerate(lst):
                eng_of[id(op)] = eng
                idx_of[id(op)] = i
        npred = {}
        succs = {}
        for eng, lst in self.ops.items():
            for op in lst:
                npred[id(op)] = len(op.deps)
                for dp in op.deps:
                    succs.setdefault(id(dp), []).append(op)
        finish = {}
        clock = {e: 0.0 for e in self.ops}
        ready = {e: [] for e in self.ops}
        new_order = {e: [] for e in self.ops}
        n_total = sum(len(v) for v in self.ops.values())

        def push_ready(op):
            av = max((finish[id(dp)] for dp in op.deps), default=0.0)
            e = eng_of[id(op)]
            heapq.heappush(ready[e], (av, idx_of[id(op)], id(op), op))

        for eng, lst in self.ops.items():
            for op in lst:
                if npred[id(op)] == 0:
                    push_ready(op)
        done = 0
        while done < n_total:
            best, bh = None, None
            for e, h in ready.items():
                if not h:
                    continue
                av = h[0][0]
                start = max(clock[e], av)
                key = (start, av)
                if best is None or key < bh:
                    best, bh = e, key
            assert best is not None, "scheduler deadlock (cycle in deps)"
            av, _, _, op = heapq.heappop(ready[best])
            e = best
            dma = op.ticket is not None and op.ticket[0][0] == "D"
            dur = 2.5 if dma else self.COST[e]
            start = max(clock[e], av)
            fin = start + dur
            clock[e] = start + (0.15 if dma else dur)
            finish[id(op)] = fin
            self.op_start[id(op)] = start
            self.op_finish[id(op)] = fin
            new_order[e].append(op)
            done += 1
            for su in succs.get(id(op), ()):
                npred[id(su)] -= 1
                if npred[id(su)] == 0:
                    push_ready(su)
        self.ops = new_order

    def _insert_warmers(self):
        if self.warm_ap is None:
            return
        wap = self.warm_ap
        new_T = []
        prev_end = 0.0
        nfill = 0
        for op in self.ops["T"]:
            st = self.op_start.get(id(op), prev_end)
            if st - prev_end > 1.0 and nfill < 9000:
                seen = set()
                anc = []
                frontier = list(op.deps)
                depth = 0
                while frontier and depth < 24:
                    nxt = []
                    for a in frontier:
                        if id(a) in seen:
                            continue
                        seen.add(id(a))
                        f = self.op_finish.get(id(a))
                        if f is not None and prev_end + 0.4 < f < st - 0.2:
                            anc.append((f, a))
                        nxt.extend(a.deps)
                    frontier = nxt
                    depth += 1
                anc.sort(key=lambda x: x[0])
                last_t = prev_end
                for f, a in anc:
                    if f >= last_t + 0.6:
                        fo = Op("T", lambda e: e.ldweights(wap), "warm")
                        fo.deps = [a]
                        new_T.append(fo)
                        nfill += 1
                        last_t = f
            new_T.append(op)
            prev_end = max(prev_end, self.op_finish.get(id(op), prev_end))
        self.ops["T"] = new_T

    def lower(self, nc, ec):
        self.schedule()
        self._insert_warmers()
        for eng in self.ops:
            for op in self.ops[eng]:
                for d in op.deps:
                    d.need_inc = True
        for eng in ("T", "V", "S", "G"):
            c = 0
            for op in self.ops[eng]:
                if op.ticket is not None:
                    continue
                if op.need_inc:
                    c += 1
                    op.ticket = (eng, c)

        sem_names = ["T", "V", "S", "G"]
        for q in ("DY", "DG"):
            for ln in range(self.NLANES):
                sem_names.append(f"{q}{ln}")
        sems = {k: ec(nc.semaphore("sem" + k)) for k in sem_names}

        def runner(key):
            def run(h):
                waited = {}
                for op in self.ops[key]:
                    for d in op.deps:
                        if d.ticket is None:
                            continue
                        sname, val = d.ticket
                        if sname == key and not SAME_ENGINE_WAITS:
                            continue
                        if waited.get(sname, -1) >= val:
                            continue
                        h.wait_ge(sems[sname], val)
                        waited[sname] = val
                    inst = op.fn(h)
                    if op.ticket is not None:
                        sname, val = op.ticket
                        inst.then_inc(sems[sname],
                                      16 if sname[0] == "D" else 1)
                q = {"G": "DG", "Y": "DY"}.get(key)
                if q:
                    for ln in range(self.NLANES):
                        u = self.lane_uses.get((q, ln), 0)
                        if u:
                            h.wait_ge(sems[f"{q}{ln}"], 16 * u)
            return run

        block = ec(nc.Block())
        block.tensor(runner("T"))
        block.vector(runner("V"))
        block.scalar(runner("S"))
        block.gpsimd(runner("G"))
        block.sync(runner("Y"))


class Arena:

    def __init__(self, nc, name, per_part_elems, dtype):
        self.t = nc.alloc_sbuf_tensor(name, [128, per_part_elems], dtype)
        self.cap = per_part_elems
        self.cursor = 0
        self.live = []
        self.dead = []
        self.name_start = {}

    def alloc(self, name, shape):
        n = int(np.prod(shape))
        ivs = sorted((s, e) for (s, e, _) in self.live)
        start = None
        pos = 0
        for (s0, e0) in ivs:
            cand = (pos + 31) & ~31
            if cand + n <= s0:
                start = cand
                break
            pos = max(pos, e0)
        if start is None:
            cand = (pos + 31) & ~31
            assert cand + n <= self.cap, \
                f"arena {self.t.name} overflow: {name}: {cand + n} > {self.cap}"
            start = cand
        end = start + n
        prev = self.name_start.setdefault(name, start)
        assert prev == start, \
            f"region {name} realloc moved: {prev} -> {start} (closure hazard)"
        reg = Region(self, name, start, list(shape))
        for (s0, e0, r0) in self.dead:
            if s0 < end and start < e0:
                if r0.res.writer is not None:
                    reg.res.preds.append(r0.res.writer)
                reg.res.preds.extend(r0.res.readers)
        self.live.append((start, end, reg))
        return reg

    def free(self, *regs):
        ids = {id(r) for r in regs}
        keep = []
        for (s, e, r) in self.live:
            (self.dead if id(r) in ids else keep).append((s, e, r))
        self.live = keep


class Region:

    def __init__(self, arena, name, start, shape):
        self.arena = arena
        self.name = name
        self.start = start
        self.shape = shape
        self.res = Res(name)
        st = []
        acc = 1
        for s in reversed(shape):
            st.append(acc)
            acc *= s
        self.strides = list(reversed(st))

    def a(self, *idx, p0=0, pn=128):
        pat = [[self.arena.cap, 128]]
        base = 0
        for d, s in enumerate(self.shape):
            sel = idx[d] if d < len(idx) else None
            if sel is None:
                pat.append([self.strides[d], s])
            elif isinstance(sel, int):
                base += sel * self.strides[d]
            else:
                st, cnt = sel
                base += st * self.strides[d]
                pat.append([self.strides[d], cnt])
        ap = bass.AP(self.arena.t, self.start + base, pat)
        if p0 or pn != 128:
            ap = ap[p0:p0 + pn]
        return ap


def _quant_emit(p, nc, ctx, xf32_ap, out_bf_ap, nblk, scr, tag=""):
    N = nblk * 16
    bm, s, r, sbf = scr["bm"], scr["s"], scr["r"], scr["sbf"]
    an, aab, sgn = scr["an"], scr["aab"], scr["sgn"]
    c = scr["c"]; g = scr["g"]
    xres = scr["xres"]; outres = scr["outres"]
    blk = lambda ap: ap

    x3 = xf32_ap.rearrange("p (nb k) -> p nb k", k=16)
    p.V(lambda e: e.tensor_reduce(bm.a((0, nblk)), x3, AX.X, ALU.max,
                                  apply_absolute_value=True),
        reads=[xres], writes=[bm.res], tag=tag + "bm")
    p.V(lambda e: e.tensor_scalar(s.a((0, nblk)), bm.a((0, nblk)),
                                  1.0 / 6.0, 1e-12, ALU.mult, ALU.max),
        reads=[bm.res], writes=[s.res], tag=tag + "s")
    p.V(lambda e: e.reciprocal(r.a((0, nblk)), s.a((0, nblk))),
        reads=[s.res], writes=[r.res], tag=tag + "r")
    p.V(lambda e: e.tensor_copy(sbf.a((0, nblk)), s.a((0, nblk))),
        reads=[s.res], writes=[sbf.res], tag=tag + "sbf")
    rbc = r.a((0, nblk)).broadcast_to([128, nblk, 16])
    p.V(lambda e: e.tensor_tensor(an.a((0, N)).rearrange("p (nb k) -> p nb k", k=16),
                                  x3, rbc, ALU.mult),
        reads=[xres, r.res], writes=[an.res], tag=tag + "an")
    p.S(lambda e: e.activation(aab.a((0, N)), an.a((0, N)), AF.Abs),
        reads=[an.res], writes=[aab.res], tag=tag + "aab")
    p.S(lambda e: e.activation(sgn.a((0, N)), an.a((0, N)), AF.Sign),
        reads=[an.res], writes=[sgn.res], tag=tag + "sgn")
    def cmp(j, dst):
        p.V(lambda e, j=j, dst=dst: e.tensor_scalar(dst.a((0, N)), aab.a((0, N)),
                                                    FP4_MID[j], FP4_W[j],
                                                    ALU.is_gt, ALU.mult),
            reads=[aab.res], writes=[dst.res], tag=tag + f"c{j}")
    cmp(0, c[0]); cmp(1, c[1])
    p.V(lambda e: e.tensor_tensor(g[0].a((0, N)), c[0].a((0, N)), c[1].a((0, N)), ALU.add),
        reads=[c[0].res, c[1].res], writes=[g[0].res], tag=tag + "g0")
    cmp(2, c[0]); cmp(3, c[1])
    p.V(lambda e: e.tensor_tensor(g[1].a((0, N)), c[0].a((0, N)), c[1].a((0, N)), ALU.add),
        reads=[c[0].res, c[1].res], writes=[g[1].res], tag=tag + "g1")
    cmp(4, c[0]); cmp(5, c[1])
    p.V(lambda e: e.tensor_tensor(g[2].a((0, N)), c[0].a((0, N)), c[1].a((0, N)), ALU.add),
        reads=[c[0].res, c[1].res], writes=[g[2].res], tag=tag + "g2")
    cmp(6, c[0])
    p.V(lambda e: e.tensor_tensor(g[0].a((0, N)), g[0].a((0, N)), g[1].a((0, N)), ALU.add),
        reads=[g[0].res, g[1].res], writes=[g[0].res], tag=tag + "g3")
    p.V(lambda e: e.tensor_tensor(g[2].a((0, N)), g[2].a((0, N)), c[0].a((0, N)), ALU.add),
        reads=[g[2].res, c[0].res], writes=[g[2].res], tag=tag + "g4")
    p.V(lambda e: e.tensor_tensor(g[0].a((0, N)), g[0].a((0, N)), g[2].a((0, N)), ALU.add),
        reads=[g[0].res, g[2].res], writes=[g[0].res], tag=tag + "qsum")
    p.V(lambda e: e.tensor_tensor(g[1].a((0, N)), g[0].a((0, N)), sgn.a((0, N)), ALU.mult),
        reads=[g[0].res, sgn.res], writes=[g[1].res], tag=tag + "qsgn")
    sbc = sbf.a((0, nblk)).broadcast_to([128, nblk, 16])
    p.V(lambda e: e.tensor_tensor(out_bf_ap.rearrange("p (nb k) -> p nb k", k=16),
                                  g[1].a((0, N)).rearrange("p (nb k) -> p nb k", k=16),
                                  sbc, ALU.mult),
        reads=[g[1].res, sbf.res], writes=[outres], tag=tag + "qout")


def build_nc():
    nc = bass.Bass("TRN2", target_bir_lowering=False,
                   detect_race_conditions=DETECT_RACES)

    d = {}
    def din(name, shape, dt=F32):
        d[name] = nc.dram_tensor(name, shape, dt, kind="ExternalInput").ap()
        return d[name]

    x_in = din("x", [NB, T, DIM])
    for w in ("wsqT", "wskT", "wsvT", "wvT", "wqT", "wkT", "woT"):
        din(w, [DIM, DIM])
    din("wgT", [DIM, H])
    for bn in ("bsq", "bsk", "bsv", "bq", "bk", "bv", "bo"):
        din(bn, [1, DIM])
    din("bg", [1, H])
    din("temp16", [H, 1])
    din("wgate", [FFI, DIM])
    din("wup", [FFI, DIM])
    din("wdown", [DIM, FFI])
    din("identb", [128, 128], BF16)
    din("identf", [128, 128], F32)
    din("triLneg", [128, 128], F32)
    din("triUneg", [128, 128], F32)
    din("maskLowNeg", [128, 128], BF16)
    din("pfull", [H, DIM], BF16)
    din("pfullT", [128, NFC * H], BF16)
    din("onesb", [1, 512], BF16)
    din("onesf", [1, 128], F32)

    out_t = nc.dram_tensor("out", [NB, T, DIM], F32, kind="ExternalOutput").ap()

    ssq = nc.dram_tensor("ssq", [NB, NCH, 128, NFC, 128], BF16, kind="Internal").ap()
    ssk = nc.dram_tensor("ssk", [NB, NCH, 128, NFC, 128], BF16, kind="Internal").ap()
    ssv = nc.dram_tensor("ssv", [NB, NCH, 128, DIM], BF16, kind="Internal").ap()
    sV = nc.dram_tensor("sV", [NB, NCH, 128, H, HD], BF16, kind="Internal").ap()
    sxq = nc.dram_tensor("sxq", [NT128, 128, NFC, 128], BF16, kind="Internal").ap()
    shq = nc.dram_tensor("shq", [NT128, 128, NIC, 128], BF16, kind="Internal").ap()
    soutp = nc.dram_tensor("soutp", [NT128, 128, DIM], F32, kind="Internal").ap()
    rsq = [[Res() for _ in range(NCH)] for _ in range(NB)]
    rsk = [[Res() for _ in range(NCH)] for _ in range(NB)]
    rsv = [[Res() for _ in range(NCH)] for _ in range(NB)]
    rV = [[Res() for _ in range(NCH)] for _ in range(NB)]
    rxq = [Res() for _ in range(NT128)]
    rhq = [Res() for _ in range(NT128)]
    routp = [Res() for _ in range(NT128)]

    stack = ExitStack()
    ec = stack.enter_context
    p = Prog()
    AF32 = Arena(nc, "AF", 13568, F32)
    AB = Arena(nc, "ABF", 79264, BF16)
    psW = nc.alloc_psum_tensor("psW", [128, 1024], F32)
    psX = nc.alloc_psum_tensor("psX", [128, 1024], F32)
    psA = nc.alloc_psum_tensor("psA", [128, 512], F32)
    psB = nc.alloc_psum_tensor("psB", [128, 512], F32)
    psTa = nc.alloc_psum_tensor("psTa", [128, 1024], BF16)
    psTb = nc.alloc_psum_tensor("psTb", [128, 1024], BF16)
    rBig = [Res(f"psBig{i}") for i in range(4)]
    rA, rB_, rTa, rTb = Res("psA"), Res("psB"), Res("psTa"), Res("psTb")
    def bigap(i, w=512):
        t = psW if i < 2 else psX
        j = i % 2
        return t.ap()[:, j * 512:j * 512 + w]

    return (nc, d, out_t, stack, ec, p, AF32, AB,
            dict(psW=psW, psX=psX, psA=psA, psB=psB, psTa=psTa, psTb=psTb,
                 rBig=rBig, rA=rA, rB=rB_, rTa=rTa, rTb=rTb, bigap=bigap),
            dict(ssq=ssq, ssk=ssk, ssv=ssv, sV=sV, sxq=sxq, shq=shq,
                 soutp=soutp, rsq=rsq, rsk=rsk, rsv=rsv, rV=rV, rxq=rxq,
                 rhq=rhq, routp=routp))


_DBG = {}


def build_program(upto="all"):
    (nc, d, out_t, stack, ec, p, AF32, AB, PS, SC) = build_nc()
    psW, psX, psA, psB = PS["psW"], PS["psX"], PS["psA"], PS["psB"]
    psTa, psTb = PS["psTa"], PS["psTb"]
    rBig, rA, rB, rTa, rTb = PS["rBig"], PS["rA"], PS["rB"], PS["rTa"], PS["rTb"]
    bigap = PS["bigap"]

    cst = {}
    def c_load(nm, arena, shape, pn=128, src=None):
        reg = arena.alloc(nm, shape)
        p.load(reg.a(p0=0, pn=pn), src if src is not None else d[nm],
               writes=[reg.res], tag="c_" + nm)
        return reg
    identb = c_load("identb", AB, [128])
    p.warm_ap = identb.a()
    identf = c_load("identf", AF32, [128])
    triLneg = c_load("triLneg", AF32, [128])
    triUneg = c_load("triUneg", AF32, [128])
    maskLN = c_load("maskLowNeg", AB, [128])
    pfull = c_load("pfull", AB, [DIM], pn=H)
    pfullT = c_load("pfullT", AB, [NFC, H])
    onesb = c_load("onesb", AB, [512], pn=1)
    onesf = c_load("onesf", AF32, [128], pn=1)
    bias_b = {}
    bf_tmp = AF32.alloc("btmp", [DIM])
    for bn in ("bsq", "bsk", "bsv", "bq", "bk", "bv", "bo", "bg"):
        n = H if bn == "bg" else DIM
        reg = AB.alloc(bn + "_b", [n])
        p.load(bf_tmp.a((0, n), p0=0, pn=1), d[bn], writes=[bf_tmp.res], tag=bn)
        p.V(lambda e, r=reg, n=n: e.tensor_copy(r.a(p0=0, pn=1),
                                                bf_tmp.a((0, n), p0=0, pn=1)),
            reads=[bf_tmp.res], writes=[reg.res], tag=bn + "c")
        bias_b[bn] = reg
    tmp16 = AF32.alloc("tmp16", [1])
    sp16 = AF32.alloc("sp16", [1])
    p.load(tmp16.a(p0=0, pn=H), d["temp16"], writes=[tmp16.res], tag="temp")
    p.S(lambda e: e.activation(tmp16.a(p0=0, pn=H), tmp16.a(p0=0, pn=H), AF.Exp),
        reads=[tmp16.res], writes=[tmp16.res], tag="spt1")
    p.S(lambda e: e.activation(sp16.a(p0=0, pn=H), tmp16.a(p0=0, pn=H),
                               AF.Ln, bias=1.0),
        reads=[tmp16.res], writes=[sp16.res], tag="spt2")

    logg = AF32.alloc("logg", [NT128, H])
    _DBG["logg"] = logg

    wA = {}
    ftmp = [AF32.alloc(f"ftmp{i}", [1024]) for i in range(2)]
    for nm, dn in (("wsq", "wsqT"), ("wsk", "wskT"), ("wsv", "wsvT"),
                   ("wv", "wvT")):
        reg = AB.alloc(nm, [NFC, 1024])
        for kc in range(NFC):
            ft = ftmp[kc % 2]
            p.load(ft.a(), d[dn][kc * 128:(kc + 1) * 128, :],
                   writes=[ft.res], tag=f"{nm}l{kc}")
            p.V(lambda e, r=reg, kc=kc, ft=ft: e.tensor_copy(r.a(kc), ft.a()),
                reads=[ft.res], writes=[reg.res], tag=f"{nm}c{kc}")
        wA[nm] = reg
    wg = AB.alloc("wg", [NFC, H])
    ftg = AF32.alloc("ftg", [NFC, H])
    p.load(ftg.a(), d["wgT"].rearrange("(kc p) h -> p kc h", p=128),
           writes=[ftg.res], tag="wgl")
    p.V(lambda e: e.tensor_copy(wg.a(), ftg.a()), reads=[ftg.res],
        writes=[wg.res], tag="wgc")

    xin = [AF32.alloc(f"xin{i}", [1024]) for i in range(2)]
    xnf = [AF32.alloc(f"xnf{i}", [1024]) for i in range(2)]
    x2s = AF32.alloc("x2s", [1024])
    sm = AF32.alloc("sm", [16])
    ezg = AF32.alloc("ezg", [H])
    xnbf = [AB.alloc(f"xnbf{i}", [1024]) for i in range(2)]
    xnfmr = [AB.alloc(f"xnfm{i}", [NFC, 512]) for i in range(2)]
    xqtm = [AB.alloc(f"xqtm{i}", [1024]) for i in range(2)]
    xqst = [AB.alloc(f"xqst{i}", [NFC, 128]) for i in range(2)]
    stg = {"sq": [AB.alloc("stsq0", [NFC, 512])],
           "sk": [AB.alloc("stsk0", [NFC, 512])]}
    svst = [AB.alloc(f"svst{i}", [1024]) for i in range(2)]
    Vst = [AB.alloc(f"Vst{i}", [1024]) for i in range(2)]
    def mk_qscr(QN, pfx, sgn_share=None):
        if sgn_share is not None:
            sgn_reg = sgn_share
        else:
            sgn_reg = None
        return dict(
            bm=AF32.alloc(pfx + "bm", [QN // 16]), s=AF32.alloc(pfx + "s", [QN // 16]),
            r=AF32.alloc(pfx + "r", [QN // 16]), sbf=AB.alloc(pfx + "sbf", [QN // 16]),
            an=AF32.alloc(pfx + "an", [QN]), aab=AF32.alloc(pfx + "aab", [QN]),
            sgn=(sgn_reg if sgn_reg is not None else AB.alloc(pfx + "sgn", [QN])),
            c=[AB.alloc(pfx + f"c{j}", [QN]) for j in range(2)],
            g=[AB.alloc(pfx + f"g{j}", [QN]) for j in range(3)])

    def free_qscr(q, skip_sgn=False):
        AF32.free(q["bm"], q["s"], q["r"], q["an"], q["aab"])
        AB.free(q["sbf"], *q["c"], *q["g"])
        if not skip_sgn:
            AB.free(q["sgn"])

    _q1 = mk_qscr(1024, "qa_")
    qscr2 = [_q1, _q1]

    ssq, ssk, ssv, sVt = SC["ssq"], SC["ssk"], SC["ssv"], SC["sV"]
    sxq, shq, soutp = SC["sxq"], SC["shq"], SC["soutp"]
    rsq, rsk, rsv, rV = SC["rsq"], SC["rsk"], SC["rsv"], SC["rV"]
    rxq, rhq, routp = SC["rxq"], SC["rhq"], SC["routp"]

    for tt in range(8):
        xnfm = xnfmr[tt % 2]
        b = tt // 4
        for j in range(4):
            ti = tt * 4 + j
            trow = (ti % 16) * 128
            xi = xin[ti % 2]
            p.load(xi.a(), d["x"][b, trow:trow + 128, :], writes=[xi.res],
                   tag=f"x{ti}")
            xn = xnf[ti % 2]
            p.V(lambda e, xi=xi: e.reduce_sum(sm.a((0, 1)), xi.a(), AX.X),
                reads=[xi.res], writes=[sm.res], tag=f"ln_s1_{ti}")
            p.S(lambda e, xi=xi: e.activation(x2s.a(), xi.a(), AF.Square,
                                              accum_out=sm.a((1, 1))),
                reads=[xi.res], writes=[x2s.res, sm.res], tag=f"ln_sq_{ti}")
            p.V(lambda e: e.tensor_scalar(sm.a((2, 1)), sm.a((0, 1)),
                                          1.0 / DIM, None, ALU.mult),
                reads=[sm.res], writes=[sm.res], tag=f"ln_m_{ti}")
            p.V(lambda e: e.tensor_scalar(sm.a((3, 1)), sm.a((1, 1)),
                                          1.0 / DIM, LN_EPS, ALU.mult, ALU.add),
                reads=[sm.res], writes=[sm.res], tag=f"ln_e2_{ti}")
            p.V(lambda e: e.tensor_tensor(sm.a((4, 1)), sm.a((2, 1)),
                                          sm.a((2, 1)), ALU.mult),
                reads=[sm.res], writes=[sm.res], tag=f"ln_m2_{ti}")
            p.V(lambda e: e.tensor_tensor(sm.a((5, 1)), sm.a((3, 1)),
                                          sm.a((4, 1)), ALU.subtract),
                reads=[sm.res], writes=[sm.res], tag=f"ln_var_{ti}")
            p.S(lambda e: e.activation(sm.a((6, 1)), sm.a((5, 1)), AF.Ln),
                reads=[sm.res], writes=[sm.res], tag=f"ln_ln_{ti}")
            p.S(lambda e: e.activation(sm.a((7, 1)), sm.a((6, 1)), AF.Exp,
                                       scale=-0.5),
                reads=[sm.res], writes=[sm.res], tag=f"ln_rs_{ti}")
            p.V(lambda e, xi=xi, xn=xn: e.tensor_scalar(
                    xn.a(), xi.a(), sm.a((2, 1)), sm.a((7, 1)),
                    ALU.subtract, ALU.mult),
                reads=[xi.res, sm.res], writes=[xn.res], tag=f"ln_ap_{ti}")
            xb_ = xnbf[ti % 2]
            p.V(lambda e, xn=xn, xb_=xb_: e.tensor_copy(xb_.a(), xn.a()),
                reads=[xn.res], writes=[xb_.res], tag=f"cst_{ti}")
            for fc in range(NFC):
                p.T(lambda e, fc=fc, xb_=xb_: e.transpose(
                        psTa.ap()[:, fc * 128:(fc + 1) * 128],
                        xb_.a((fc * 128, 128)), identb.a()),
                    reads=[xb_.res, identb.res], writes=[rTa],
                    tag=f"trx_{ti}_{fc}")
            p.S(lambda e, j=j, xnfm=xnfm: e.copy(
                    xnfm.a(None, (j * 128, 128)),
                    psTa.ap().rearrange("p (fc t) -> p fc t", t=128)),
                reads=[rTa], writes=[xnfm.res], tag=f"evx_{ti}")
            xq = xqtm[ti % 2]
            qscr = qscr2[ti % 2]
            qscr["xres"] = xn.res
            qscr["outres"] = xq.res
            _quant_emit(p, nc, None, xn.a(), xq.a(), DIM // 16, qscr,
                        tag=f"qx{ti}_")
            for fc in range(NFC):
                p.T(lambda e, fc=fc, xq=xq: e.transpose(
                        psTb.ap()[:, fc * 128:(fc + 1) * 128],
                        xq.a((fc * 128, 128)), identb.a()),
                    reads=[xq.res, identb.res], writes=[rTb],
                    tag=f"trq_{ti}_{fc}")
            xs = xqst[ti % 2]
            p.S(lambda e, xs=xs: e.copy(
                    xs.a(), psTb.ap().rearrange("p (fc t) -> p fc t", t=128)),
                reads=[rTb], writes=[xs.res], tag=f"evq_{ti}")
            p.store(sxq[ti], xs.a(), reads=[xs.res], writes=[rxq[ti]],
                    tag=f"sxq{ti}")
            for kc in range(NFC):
                p.T(lambda e, kc=kc, j=j, xnfm=xnfm: e.matmul(
                        psA.ap()[:, 0:H], xnfm.a(kc, (j * 128, 128)),
                        wg.a(kc), start=(kc == 0), stop=False),
                    reads=[xnfm.res, wg.res], writes=[rA], tag=f"zg_{ti}_{kc}")
            p.T(lambda e: e.matmul(psA.ap()[:, 0:H],
                                   onesb.a((0, 128), p0=0, pn=1),
                                   bias_b["bg"].a(p0=0, pn=1),
                                   start=False, stop=True),
                reads=[onesb.res, bias_b["bg"].res], writes=[rA],
                tag=f"zg_b_{ti}")
            p.S(lambda e: e.activation(ezg.a(), psA.ap()[:, 0:H], AF.Exp),
                reads=[rA], writes=[ezg.res], tag=f"zg_e_{ti}")
            p.S(lambda e, ti=ti: e.activation(logg.a(ti), ezg.a(), AF.Ln,
                                              bias=1.0),
                reads=[ezg.res], writes=[logg.res], tag=f"zg_l_{ti}")
        c0 = (tt % 4) * 4
        for prn, wn, bn, sres, sdst in (("sq", "wsq", "bsq", rsq, ssq),
                                        ("sk", "wsk", "bsk", rsk, ssk)):
            st = stg[prn][0]
            for ncx in range(NFC):
                bank = ncx % 2
                for kc in range(NFC):
                    p.T(lambda e, kc=kc, ncx=ncx, bank=bank, wn=wn, xnfm=xnfm: e.matmul(
                            bigap(bank), wA[wn].a(kc, (ncx * 128, 128)),
                            xnfm.a(kc), start=(kc == 0), stop=False),
                        reads=[wA[wn].res, xnfm.res], writes=[rBig[bank]],
                        tag=f"{prn}_{tt}_{ncx}_{kc}")
                p.T(lambda e, ncx=ncx, bank=bank, bn=bn: e.matmul(
                        bigap(bank), bias_b[bn].a((ncx * 128, 128), p0=0, pn=1),
                        onesb.a(p0=0, pn=1), start=False, stop=True),
                    reads=[bias_b[bn].res, onesb.res], writes=[rBig[bank]],
                    tag=f"{prn}b_{tt}_{ncx}")
                p.S(lambda e, ncx=ncx, bank=bank, st=st: e.copy(
                        st.a(ncx), bigap(bank)),
                    reads=[rBig[bank]], writes=[st.res], tag=f"{prn}e_{tt}_{ncx}")
            for q in range(4):
                cix = (tt % 4) * 4 + q
                p.store(sdst[tt // 4, cix],
                        st.a(None, (q * 128, 128)),
                        reads=[st.res], writes=[sres[tt // 4][cix]],
                        tag=f"{prn}s_{tt}_{q}")
        for j in range(4):
            ti = tt * 4 + j
            cix = ti % 16
            for prn, wn, bn, stb in (("sv", "wsv", "bsv", svst),
                                     ("V", "wv", "bv", Vst)):
                for kc in range(NFC):
                    for nh in range(2):
                        p.T(lambda e, kc=kc, nh=nh, j=j, wn=wn, xnfm=xnfm: e.matmul(
                                bigap(2 + nh), xnfm.a(kc, (j * 128, 128)),
                                wA[wn].a(kc, (nh * 512, 512)),
                                start=(kc == 0), stop=False),
                            reads=[xnfm.res, wA[wn].res], writes=[rBig[2 + nh]],
                            tag=f"{prn}_{ti}_{kc}_{nh}")
                for nh in range(2):
                    p.T(lambda e, nh=nh, bn=bn: e.matmul(
                            bigap(2 + nh), onesb.a((0, 128), p0=0, pn=1),
                            bias_b[bn].a((nh * 512, 512), p0=0, pn=1),
                            start=False, stop=True),
                        reads=[onesb.res, bias_b[bn].res], writes=[rBig[2 + nh]],
                        tag=f"{prn}b_{ti}_{nh}")
                sb_ = stb[ti % 2]
                p.V(lambda e, sb_=sb_: e.tensor_copy(
                        sb_.a(), psX.ap()),
                    reads=[rBig[2], rBig[3]], writes=[sb_.res],
                    tag=f"{prn}e_{ti}")
                if prn == "sv":
                    p.store(ssv[b, cix], sb_.a(), reads=[sb_.res],
                            writes=[rsv[b][cix]], tag=f"svs_{ti}")
                else:
                    p.store(sVt[b, cix], sb_.a().rearrange(
                                "p (h e) -> p h e", e=HD),
                            reads=[sb_.res], writes=[rV[b][cix]],
                            tag=f"Vs_{ti}")
    AF32.free(xin[0], xin[1], x2s, ftmp[0], ftmp[1], ftg, bf_tmp,
              ezg, xnf[0], xnf[1], tmp16, sm)
    AB.free(wA["wsq"], wA["wsk"], wA["wsv"], wA["wv"], wg,
            xnbf[0], xnbf[1], xnfmr[0], xnfmr[1], xqtm[0], xqtm[1],
            xqst[0], xqst[1],
            stg["sq"][0], stg["sk"][0], svst[0], svst[1], Vst[0], Vst[1],
            bias_b["bsq"], bias_b["bsk"], bias_b["bsv"], bias_b["bv"],
            bias_b["bg"])
    free_qscr(qscr2[0])
    if upto == "a1":
        p.lower(nc, ec)
        return nc, stack
    _phases_bcd(nc, d, out_t, p, AF32, AB, PS, SC, locals(), upto)
    p.lower(nc, ec)
    return nc, stack


def _phases_bcd(nc, d, out_t, p, AF32, AB, PS, SC, env, upto):
    psW, psX, psA, psB = PS["psW"], PS["psX"], PS["psA"], PS["psB"]
    psTa, psTb = PS["psTa"], PS["psTb"]
    rBig, rA, rB, rTa, rTb = PS["rBig"], PS["rA"], PS["rB"], PS["rTa"], PS["rTb"]
    bigap = PS["bigap"]
    ssq, ssk, ssv, sVt = SC["ssq"], SC["ssk"], SC["ssv"], SC["sV"]
    sxq, shq, soutp = SC["sxq"], SC["shq"], SC["soutp"]
    rsq, rsk, rsv, rV = SC["rsq"], SC["rsk"], SC["rsv"], SC["rV"]
    rxq, rhq, routp = SC["rxq"], SC["rhq"], SC["routp"]
    logg = env["logg"]
    identb, identf = env["identb"], env["identf"]
    triLneg, triUneg, maskLN = env["triLneg"], env["triUneg"], env["maskLN"]
    pfull, pfullT = env["pfull"], env["pfullT"]
    onesb, onesf = env["onesb"], env["onesf"]
    bias_b, sp16 = env["bias_b"], env["sp16"]

    ftc = [AF32.alloc(f"ftc{i}", [1024]) for i in range(2)]

    def load_w(nm, dn, tagp):
        reg = AB.alloc(nm, [NFC, 1024])
        for kc in range(NFC):
            ft = ftc[kc % 2]
            p.load(ft.a(), d[dn][kc * 128:(kc + 1) * 128, :], writes=[ft.res],
                   tag=f"{tagp}{nm}l{kc}")
            p.V(lambda e, r=reg, kc=kc, ft=ft: e.tensor_copy(r.a(kc), ft.a()),
                reads=[ft.res], writes=[reg.res], tag=f"{tagp}{nm}c{kc}")
        return reg

    Mf = AF32.alloc("Mf", [H, HD])
    lld = AF32.alloc("lld", [32])
    nl = AF32.alloc("nl", [16])
    ek2 = AF32.alloc("ek2", [16])
    eend = AF32.alloc("eend", [NFC])
    lTr = AF32.alloc("lTr", [H, 128])
    lTsb = AF32.alloc("lTsb", [128])

    for b in range(NB):
        Hseq = AB.alloc("Hseq", [NFC, T])
        _DBG["Hseq"] = Hseq
        sqg = [AB.alloc(f"sqg{i}", [NFC, 128]) for i in range(2)]
        skg = [AB.alloc(f"skg{i}", [NFC, 128]) for i in range(2)]
        svc = [AB.alloc(f"svc{i}", [DIM]) for i in range(2)]
        Mbf = AB.alloc("Mbf", [H, HD])
        qp = AB.alloc("qp", [NFC, 128])
        Dm = [AB.alloc(f"Dm{i}", [128]) for i in range(16)]
        SDm = [AB.alloc(f"SDm{i}", [128]) for i in range(2)]
        elT = AB.alloc("elT", [128])
        ktil = AB.alloc("ktil", [H, HD])
        p.V(lambda e: e.memset(Mf.a(), 0.0), writes=[Mf.res], tag=f"M0_{b}")
        for c in range(NCH):
            sq_, sk_ = sqg[c % 2], skg[c % 2]
            p.load(sq_.a(), ssq[b, c], reads=[rsq[b][c]],
                   writes=[sq_.res], tag=f"Bq{b}_{c}")
            p.load(sk_.a(), ssk[b, c], reads=[rsk[b][c]],
                   writes=[sk_.res], tag=f"Bk{b}_{c}")
            sv_ = svc[c % 2]
            p.load(sv_.a(), ssv[b, c], reads=[rsv[b][c]], writes=[sv_.res],
                   tag=f"Bv{b}_{c}")
            ti = b * NCH + c
            p.T(lambda e, ti=ti: e.matmul(psA.ap()[:, 0:H], triLneg.a(),
                                          logg.a(ti), start=True, stop=True),
                reads=[triLneg.res, logg.res], writes=[rA], tag=f"cums{ti}")
            p.T(lambda e, ti=ti: e.matmul(psA.ap()[:, H:2 * H], triUneg.a(),
                                          logg.a(ti), start=True, stop=True),
                reads=[triUneg.res, logg.res], writes=[rA], tag=f"ldif{ti}")
            p.V(lambda e: e.tensor_copy(lld.a((0, 32)), psA.ap()[:, 0:32]),
                reads=[rA], writes=[lld.res], tag=f"lev{ti}")
            p.V(lambda e: e.tensor_scalar(nl.a(), lld.a((0, 16)), -1.0, None,
                                          ALU.mult),
                reads=[lld.res], writes=[nl.res], tag=f"nl{ti}")
            p.S(lambda e: e.activation(ek2.a(), lld.a((16, 16)), AF.Exp),
                reads=[lld.res], writes=[ek2.res], tag=f"ek2{ti}")
            p.T(lambda e: e.transpose(psB.ap()[0:16, 0:128], lld.a((0, 16)),
                                      identf.a()),
                reads=[lld.res, identf.res], writes=[rB], tag=f"lT{ti}")
            p.S(lambda e: e.activation(elT.a(p0=0, pn=16),
                                       psB.ap()[0:16, 0:128], AF.Exp),
                reads=[rB], writes=[elT.res], tag=f"elT{ti}")
            for h in range(H):
                p.T(lambda e, h=h: e.transpose(psB.ap()[0:1, 128:256],
                                               lld.a((h, 1)), identf.a()),
                    reads=[lld.res, identf.res], writes=[rB], tag=f"lr{ti}_{h}")
                p.V(lambda e, h=h: e.tensor_copy(lTr.a(h, p0=0, pn=1),
                                                 psB.ap()[0:1, 128:256]),
                    reads=[rB], writes=[lTr.res], tag=f"lrc{ti}_{h}")
            for fc in range(NFC):
                p.T(lambda e, fc=fc: e.matmul(
                        bigap(fc // 4)[:, (fc % 4) * 128:(fc % 4) * 128 + 128],
                        pfull.a((fc * 128, 128), p0=0, pn=16),
                        elT.a(p0=0, pn=16), start=True, stop=True),
                    reads=[pfull.res, elT.res], writes=[rBig[fc // 4]],
                    tag=f"eg{ti}_{fc}")
            for fc in range(NFC):
                p.V(lambda e, fc=fc, sq_=sq_: e.tensor_tensor(
                        qp.a(fc), sq_.a(fc),
                        bigap(fc // 4)[:, (fc % 4) * 128:(fc % 4) * 128 + 128],
                        ALU.mult),
                    reads=[sq_.res, rBig[fc // 4]], writes=[qp.res],
                    tag=f"qp{ti}_{fc}")
            p.V(lambda e: e.tensor_copy(
                    eend.a(), bass.AP(psW, 127, [[1024, 128], [128, 8]])),
                reads=[rBig[0], rBig[1]], writes=[eend.res], tag=f"ee{ti}")
            p.V(lambda e: e.tensor_copy(Mbf.a(), Mf.a()),
                reads=[Mf.res], writes=[Mbf.res], tag=f"mc{ti}")
            for h in range(H):
                pban = psA if h % 2 == 0 else psB
                rban = rA if h % 2 == 0 else rB
                p.T(lambda e, h=h, pban=pban: e.matmul(
                        pban.ap()[:, 256:384], onesf.a((0, 128), p0=0, pn=1),
                        lTr.a(h, p0=0, pn=1), start=True, stop=False),
                    reads=[onesf.res, lTr.res], writes=[rban],
                    tag=f"lbc{ti}_{h}")
                p.T(lambda e, pban=pban: e.matmul(
                        pban.ap()[:, 256:384], identb.a(), maskLN.a(),
                        start=False, stop=True),
                    reads=[identb.res, maskLN.res], writes=[rban],
                    tag=f"lbm{ti}_{h}")
                p.S(lambda e, h=h, pban=pban: e.activation(
                        Dm[h].a(), pban.ap()[:, 256:384], AF.Exp,
                        bias=nl.a((h, 1))),
                    reads=[rban, nl.res], writes=[Dm[h].res],
                    tag=f"dex{ti}_{h}")
            for fc in range(NFC):
                p.T(lambda e, fc=fc, sk_=sk_: e.transpose(
                        psTa.ap()[:, fc * 128:(fc + 1) * 128],
                        sk_.a(fc), identb.a()),
                    reads=[sk_.res, identb.res], writes=[rTa],
                    tag=f"ktr{ti}_{fc}")
            for h in range(H):
                fc, off = h // 2, (h % 2) * 64
                p.V(lambda e, h=h, fc=fc, off=off: e.tensor_scalar(
                        ktil.a(h),
                        psTa.ap()[:, fc * 128 + off:fc * 128 + off + 64],
                        ek2.a((h, 1)), None, ALU.mult),
                    reads=[rTa, ek2.res], writes=[ktil.res],
                    tag=f"kts{ti}_{h}")
            for h in range(H):
                fc, off = h // 2, (h % 2) * 64
                sb_ = 2 + h % 2
                hb = h % 2
                if h % 2 == 0:
                    def qkfn(e, fc=fc, sk_=sk_, sq_=sq_):
                        e.matmul(bigap(2)[:, 0:128],
                                 sk_.a(fc, p0=0, pn=64),
                                 sq_.a(fc, p0=0, pn=64),
                                 start=True, stop=True)
                        return e.matmul(bigap(3)[:, 0:128],
                                        sk_.a(fc, p0=64, pn=64),
                                        sq_.a(fc, p0=64, pn=64),
                                        start=True, stop=True)
                    p.T(qkfn, reads=[sk_.res, sq_.res],
                        writes=[rBig[2], rBig[3]], tag=f"qk{ti}_{h}")
                p.V(lambda e, h=h, sb_=sb_: e.scalar_tensor_tensor(
                        SDm[h % 2].a(), bigap(sb_)[:, 0:128], 1.0,
                        Dm[h].a(), ALU.mult, ALU.mult),
                    reads=[rBig[sb_], Dm[h].res], writes=[SDm[h % 2].res],
                    tag=f"sdm{ti}_{h}")
                p.T(lambda e, h=h, hb=hb: e.matmul(
                        bigap(hb)[0:64, 128:256], Mbf.a(h), qp.a(h // 2),
                        start=True, stop=False),
                    reads=[Mbf.res, qp.res], writes=[rBig[hb]],
                    tag=f"hin{ti}_{h}")
                p.T(lambda e, h=h, hb=hb, sv_=sv_: e.matmul(
                        bigap(hb)[0:64, 128:256], sv_.a((h * HD, HD)),
                        SDm[h % 2].a(), start=False, stop=True),
                    reads=[sv_.res, SDm[h % 2].res], writes=[rBig[hb]],
                    tag=f"hsv{ti}_{h}")
                p.V(lambda e, h=h, hb=hb, fc=fc, off=off, c=c: e.tensor_copy(
                        Hseq.a(fc, (c * 128, 128), p0=off, pn=64),
                        bigap(hb)[0:64, 128:256]),
                    reads=[rBig[hb]], writes=[Hseq.res], tag=f"hev{ti}_{h}")
                db = psA if h % 2 == 0 else psB
                rdb = rA if h % 2 == 0 else rB
                p.T(lambda e, h=h, db=db, off=off, sv_=sv_: e.matmul(
                        db.ap()[off:off + 64, 384:448], ktil.a(h),
                        sv_.a((h * HD, HD)), start=True, stop=True,
                        tile_position=(0, off)),
                    reads=[ktil.res, sv_.res], writes=[rdb], tag=f"dlt{ti}_{h}")
                p.V(lambda e, h=h, db=db, off=off, fc=fc:
                    e.scalar_tensor_tensor(
                        Mf.a(h, p0=off, pn=64), Mf.a(h, p0=off, pn=64),
                        eend.a((fc, 1), p0=off, pn=64),
                        db.ap()[off:off + 64, 384:448], ALU.mult, ALU.add),
                    reads=[Mf.res, eend.res, rdb], writes=[Mf.res],
                    tag=f"mup{ti}_{h}")
        AB.free(sqg[0], sqg[1], skg[0], skg[1], svc[0], svc[1], Mbf, qp,
                *Dm, SDm[0], SDm[1], elT, ktil)
        if upto == "b":
            if b == 0:
                AB.free(Hseq)
            continue
        Qhat = AB.alloc("Qhat", [NFC, T])
        Khat = AB.alloc("Khat", [NFC, T])
        _DBG["Qhat"], _DBG["Khat"] = Qhat, Khat
        Rraw = AF32.alloc("Rraw", [NFC, 512])
        sqb = AB.alloc("sqb", [NFC, 512])
        rq = AF32.alloc("rq", [512])
        rqb = AB.alloc("rqb", [512])
        for prn, wn, bn, dst in (("q", "wqT", "bq", Qhat),
                                 ("k", "wkT", "bk", Khat)):
            wqk = load_w("wqk", wn, f"c{b}{prn}")
            for qg in range(4):
                for ncx in range(NFC):
                    bank = ncx % 2
                    for kc in range(NFC):
                        p.T(lambda e, kc=kc, ncx=ncx, bank=bank, qg=qg,
                                   wqk=wqk: e.matmul(
                                bigap(bank), wqk.a(kc, (ncx * 128, 128)),
                                Hseq.a(kc, (qg * 512, 512)),
                                start=(kc == 0), stop=False),
                            reads=[wqk.res, Hseq.res], writes=[rBig[bank]],
                            tag=f"{prn}p{b}_{qg}_{ncx}_{kc}")
                    p.T(lambda e, ncx=ncx, bank=bank, bn=bn: e.matmul(
                            bigap(bank),
                            bias_b[bn].a((ncx * 128, 128), p0=0, pn=1),
                            onesb.a(p0=0, pn=1), start=False, stop=True),
                        reads=[bias_b[bn].res, onesb.res], writes=[rBig[bank]],
                        tag=f"{prn}pb{b}_{qg}_{ncx}")
                    p.V(lambda e, ncx=ncx, bank=bank: e.tensor_copy(
                            Rraw.a(ncx), bigap(bank)),
                        reads=[rBig[bank]], writes=[Rraw.res],
                        tag=f"{prn}ev{b}_{qg}_{ncx}")
                    p.S(lambda e, ncx=ncx: e.activation(
                            sqb.a(ncx), Rraw.a(ncx), AF.Square),
                        reads=[Rraw.res], writes=[sqb.res],
                        tag=f"{prn}sq{b}_{qg}_{ncx}")
                for fc in range(NFC):
                    p.T(lambda e, fc=fc: e.matmul(
                            psA.ap()[0:16, 0:512], pfullT.a(fc), sqb.a(fc),
                            start=(fc == 0), stop=(fc == NFC - 1)),
                        reads=[pfullT.res, sqb.res], writes=[rA],
                        tag=f"{prn}ss{b}_{qg}_{fc}")
                p.V(lambda e: e.tensor_scalar(rq.a(p0=0, pn=16),
                                              psA.ap()[0:16, 0:512],
                                              1e-24, None, ALU.max),
                    reads=[rA], writes=[rq.res], tag=f"{prn}mx{b}_{qg}")
                p.S(lambda e: e.activation(rq.a(p0=0, pn=16),
                                           rq.a(p0=0, pn=16), AF.Ln),
                    reads=[rq.res], writes=[rq.res], tag=f"{prn}ln{b}_{qg}")
                p.S(lambda e: e.activation(rq.a(p0=0, pn=16),
                                           rq.a(p0=0, pn=16), AF.Exp,
                                           scale=-0.5),
                    reads=[rq.res], writes=[rq.res], tag=f"{prn}ex{b}_{qg}")
                if prn == "k":
                    p.V(lambda e: e.tensor_scalar(rq.a(p0=0, pn=16),
                                                  rq.a(p0=0, pn=16),
                                                  sp16.a(p0=0, pn=16), None,
                                                  ALU.mult),
                        reads=[rq.res, sp16.res], writes=[rq.res],
                        tag=f"ksp{b}_{qg}")
                p.V(lambda e: e.tensor_copy(rqb.a(p0=0, pn=16),
                                            rq.a(p0=0, pn=16)),
                    reads=[rq.res], writes=[rqb.res], tag=f"{prn}cb{b}_{qg}")
                for fc in range(NFC):
                    pban = psA if fc % 2 == 0 else psB
                    rban = rA if fc % 2 == 0 else rB
                    p.T(lambda e, fc=fc, pban=pban: e.matmul(
                            pban.ap()[:, 0:512],
                            pfull.a((fc * 128, 128), p0=0, pn=16),
                            rqb.a(p0=0, pn=16), start=True, stop=True),
                        reads=[pfull.res, rqb.res], writes=[rban],
                        tag=f"{prn}rg{b}_{qg}_{fc}")
                    p.V(lambda e, fc=fc, pban=pban, rban=rban, dst=dst, qg=qg:
                        e.tensor_tensor(dst.a(fc, (qg * 512, 512)),
                                        Rraw.a(fc), pban.ap()[:, 0:512],
                                        ALU.mult),
                        reads=[Rraw.res, rban], writes=[dst.res],
                        tag=f"{prn}hm{b}_{qg}_{fc}")
            AB.free(wqk)
        AB.free(sqb, rqb)
        AF32.free(Rraw, rq)
        AB.free(Hseq)
        if upto == "c":
            if b == 0:
                AB.free(Qhat, Khat)
            continue
        wo = load_w("wo", "woT", f"d{b}")
        Vaug = AB.alloc("Vaug", [H, NCH, 65])
        Esb = [AB.alloc(f"Esb{i}", [2048]) for i in range(2)]
        attnfm = AB.alloc("attnfm", [NFC, 512])
        r64 = [AB.alloc(f"r64{i}", [512]) for i in range(2)]
        rden = AF32.alloc("rden", [512])
        rden2 = AF32.alloc("rden2", [512])
        r64b = [AB.alloc(f"r64b{i}", [512]) for i in range(2)]
        xres = [AF32.alloc(f"xres{i}", [1024]) for i in range(2)]
        ofin = [AF32.alloc(f"ofin{i}", [1024]) for i in range(2)]
        p.V(lambda e: e.memset(Vaug.a(None, None, (64, 1)), 1.0),
            writes=[Vaug.res], tag=f"vaug1_{b}")
        for c in range(NCH):
            p.load(Vaug.a(None, c, (0, 64)), sVt[b, c],
                   reads=[rV[b][c]], writes=[Vaug.res], tag=f"VL{b}_{c}")
        for qg in range(4):
            for hp in range(NFC):
                h0, h1 = 2 * hp, 2 * hp + 1
                cfg = [(h0, 0, psW, (rBig[0], rBig[1]), psA, rA, Esb[0]),
                       (h1, 64, psX, (rBig[2], rBig[3]), psB, rB, Esb[1])]

                def sc_pair(g):
                    for j in range(2):
                        kc = g * 2 + j
                        def fn(e, kc=kc, j=j, qg=qg, hp=hp):
                            e.matmul(
                                psW.ap()[:, j * 512:(j + 1) * 512],
                                Khat.a(hp, (kc * 128, 128), p0=0, pn=64),
                                Qhat.a(hp, (qg * 512, 512), p0=0, pn=64),
                                start=True, stop=True)
                            return e.matmul(
                                psX.ap()[:, j * 512:(j + 1) * 512],
                                Khat.a(hp, (kc * 128, 128), p0=64, pn=64),
                                Qhat.a(hp, (qg * 512, 512), p0=64, pn=64),
                                start=True, stop=True)
                        p.T(fn, reads=[Khat.res, Qhat.res],
                            writes=[rBig[j], rBig[2 + j]],
                            tag=f"sc{b}_{qg}_{hp}_{g}_{j}")

                def sc_group(ci, g):
                    if ci == 0:
                        sc_pair(g)

                def exp_group(ci, g):
                    h, off, pt, prs, _, _, eb = cfg[ci]
                    p.S(lambda e, eb=eb, pt=pt: e.activation(
                            eb.a((0, 1024)), pt.ap(), AF.Exp),
                        reads=[prs[0], prs[1]], writes=[eb.res],
                        tag=f"ex{b}_{qg}_{h}_{g}")

                def ev_group(ci, g):
                    h, off, pt, prs, pacc, racc, eb = cfg[ci]
                    for j in range(2):
                        kc = g * 2 + j
                        p.T(lambda e, h=h, kc=kc, j=j, eb=eb, pacc=pacc:
                            e.matmul(
                                pacc.ap()[0:65, 0:512], Vaug.a(h, kc),
                                eb.a((j * 512, 512)),
                                start=(kc == 0), stop=(kc == 15)),
                            reads=[Vaug.res, eb.res], writes=[racc],
                            tag=f"ev{b}_{qg}_{h}_{kc}")

                for g in range(8):
                    for ci in (0, 1):
                        sc_group(ci, g)
                        exp_group(ci, g)
                        ev_group(ci, g)
                for ci in (0, 1):
                    h, off, pt, prs, pacc, racc, eb = cfg[ci]
                    rd = rden if ci == 0 else rden2
                    rr0 = r64[0] if ci == 0 else r64b[0]
                    rr1 = r64[1] if ci == 0 else r64b[1]
                    p.V(lambda e, pacc=pacc, rd=rd: e.tensor_copy(
                            rd.a((0, 512), p0=0, pn=1),
                            pacc.ap()[64:65, 0:512]),
                        reads=[racc], writes=[rd.res], tag=f"rcp{b}_{qg}_{h}")
                    p.S(lambda e, rd=rd: e.activation(
                            rd.a((0, 512), p0=0, pn=1),
                            rd.a((0, 512), p0=0, pn=1), AF.Ln),
                        reads=[rd.res], writes=[rd.res], tag=f"rc{b}_{qg}_{h}")
                    p.S(lambda e, rd=rd, rr0=rr0: e.activation(
                            rr0.a(p0=0, pn=1), rd.a((0, 512), p0=0, pn=1),
                            AF.Exp, scale=-1.0),
                        reads=[rd.res], writes=[rr0.res], tag=f"rcb{b}_{qg}_{h}")
                    p.T(lambda e, pt=pt, rr0=rr0: e.matmul(
                            pt.ap()[0:64, 0:512], onesb.a((0, 64), p0=0, pn=1),
                            rr0.a(p0=0, pn=1), start=True, stop=True),
                        reads=[onesb.res, rr0.res], writes=[prs[0]],
                        tag=f"rbc{b}_{qg}_{h}")
                    p.S(lambda e, pt=pt, rr1=rr1: e.activation(
                            rr1.a(p0=0, pn=64), pt.ap()[0:64, 0:512], AF.Copy),
                        reads=[prs[0]], writes=[rr1.res], tag=f"rev{b}_{qg}_{h}")
                    p.V(lambda e, h=h, off=off, pacc=pacc, rr1=rr1, hp=hp:
                        e.tensor_tensor(
                            attnfm.a(hp, p0=off, pn=64),
                            pacc.ap()[0:64, 0:512], rr1.a(p0=0, pn=64),
                            ALU.mult),
                        reads=[racc, rr1.res], writes=[attnfm.res],
                        tag=f"at{b}_{qg}_{h}")
            for t4 in range(4):
                ti = b * NCH + qg * 4 + t4
                trow = (qg * 4 + t4) * 128
                xr = xres[t4 % 2]
                p.load(xr.a(), d["x"][b, trow:trow + 128, :], writes=[xr.res],
                       tag=f"xl{b}_{qg}_{t4}")
                for nh in range(2):
                    for kc in range(NFC):
                        p.T(lambda e, kc=kc, nh=nh, t4=t4: e.matmul(
                                psX.ap()[:, nh * 512:(nh + 1) * 512],
                                attnfm.a(kc, (t4 * 128, 128)),
                                wo.a(kc, (nh * 512, 512)),
                                start=(kc == 0), stop=False),
                            reads=[attnfm.res, wo.res],
                            writes=[rBig[2 + nh]],
                            tag=f"wo{b}_{qg}_{t4}_{kc}_{nh}")
                    p.T(lambda e, nh=nh: e.matmul(
                            psX.ap()[:, nh * 512:(nh + 1) * 512],
                            onesb.a((0, 128), p0=0, pn=1),
                            bias_b["bo"].a((nh * 512, 512), p0=0, pn=1),
                            start=False, stop=True),
                        reads=[onesb.res, bias_b["bo"].res],
                        writes=[rBig[2 + nh]], tag=f"wob{b}_{qg}_{t4}_{nh}")
                of = ofin[t4 % 2]
                p.V(lambda e, xr=xr, of=of: e.tensor_tensor(
                        of.a(), psX.ap(), xr.a(), ALU.add),
                    reads=[rBig[2], rBig[3], xr.res], writes=[of.res],
                    tag=f"oadd{b}_{qg}_{t4}")
                p.store(soutp[ti], of.a(), reads=[of.res],
                        writes=[routp[ti]], tag=f"ost{b}_{qg}_{t4}")
        AB.free(Vaug, Esb[0], Esb[1], attnfm, r64[0], r64[1], r64b[0],
                r64b[1], Qhat, Khat, wo)
        AF32.free(rden, rden2, xres[0], xres[1], ofin[0], ofin[1])
    AF32.free(Mf, lld, nl, ek2, eend, lTr, lTsb, ftc[0], ftc[1])
    if upto in ("b", "c", "d"):
        return
    _phase_a2(nc, d, out_t, p, AF32, AB, PS, SC, env)


def _phase_a2(nc, d, out_t, p, AF32, AB, PS, SC, env):
    psW, psX, psA, psB = PS["psW"], PS["psX"], PS["psA"], PS["psB"]
    psTa, psTb = PS["psTa"], PS["psTb"]
    rBig, rA, rB, rTa, rTb = PS["rBig"], PS["rA"], PS["rB"], PS["rTa"], PS["rTb"]
    bigap = PS["bigap"]
    sxq, shq, soutp = SC["sxq"], SC["shq"], SC["soutp"]
    rxq, rhq, routp = SC["rxq"], SC["rhq"], SC["routp"]
    identb = env["identb"]
    mk_qscr, free_qscr = env["mk_qscr"], env["free_qscr"]

    IG = [(0, 512), (512, 512), (1024, 512), (1536, 512), (2048, 512),
          (2560, 256)]
    AF32.free(env["logg"])

    wqg = AB.alloc("wqg", [NFC, FFI])
    wqu = AB.alloc("wqu", [NFC, FFI])
    xq = [AB.alloc(f"a2xq{i}", [NFC, 128]) for i in range(2)]
    hqr = [AB.alloc(f"hq{i}", [FFI]) for i in range(2)]
    qtile = hqr[0]
    hqs = [AB.alloc(f"hqs{i}", [4, 128]) for i in range(2)]
    qs2 = [mk_qscr(1408, "qw_"), mk_qscr(1408, "qx_")]
    qs = qs2[0]
    ftw = [AF32.alloc(f"ftw{i}", [1024]) for i in range(2)]

    def prep_w_rows(dsrc, wdst, nrt, rowlen, tag):
        for rt in range(nrt):
            ft = ftw[rt % 2]
            p.load(ft.a((0, rowlen)), dsrc[rt * 128:(rt + 1) * 128, :],
                   writes=[ft.res], tag=f"{tag}l{rt}")
            for qi in range((rowlen + 1407) // 1408):
                off = qi * 1408
                ln = min(1408, rowlen - off)
                qv = qs2[rt % 2]
                qv["xres"] = ft.res
                qv["outres"] = qtile.res
                _quant_emit(p, nc, None, ft.a((off, ln)),
                            qtile.a((off, ln)), ln // 16, qv,
                            tag=f"{tag}q{rt}_{qi}_")
            nch = rowlen // 128
            for ch in range(nch):
                pt = psTa if ch % 2 == 0 else psTb
                rt_ = rTa if ch % 2 == 0 else rTb
                p.T(lambda e, ch=ch, pt=pt: e.transpose(
                        pt.ap()[:, 0:128], qtile.a((ch * 128, 128)), identb.a()),
                    reads=[qtile.res, identb.res], writes=[rt_],
                    tag=f"{tag}t{rt}_{ch}")
                p.S(lambda e, ch=ch, rt=rt, pt=pt, wdst=wdst: e.copy(
                        wdst.a(ch, (rt * 128, 128)), pt.ap()[:, 0:128]),
                    reads=[rt_], writes=[wdst.res], tag=f"{tag}e{rt}_{ch}")
    prep_w_rows(d["wgate"], wqg, NIC, 1024, "wg_")
    prep_w_rows(d["wup"], wqu, NIC, 1024, "wu_")
    AF32.free(ftw[0], ftw[1])

    hbr = [AF32.alloc(f"hbuf{i}", [FFI]) for i in range(2)]
    sgr = [AF32.alloc("sg0", [512])]
    qh = qs
    for ti in range(NT128):
        xt = xq[ti % 2]
        hb, sg, hq = hbr[ti % 2], sgr[0], hqr[ti % 2]
        p.load(xt.a(), sxq[ti], reads=[rxq[ti]], writes=[xt.res],
               tag=f"a2x{ti}")
        for gi, (i0, ilen) in enumerate(IG):
            for kc in range(NFC):
                p.T(lambda e, kc=kc, i0=i0, ilen=ilen, xt=xt: e.matmul(
                        bigap(0)[:, 0:ilen], xt.a(kc), wqg.a(kc, (i0, ilen)),
                        start=(kc == 0), stop=(kc == NFC - 1)),
                    reads=[xt.res, wqg.res], writes=[rBig[0]],
                    tag=f"gt{ti}_{gi}_{kc}")
                p.T(lambda e, kc=kc, i0=i0, ilen=ilen, xt=xt: e.matmul(
                        bigap(1)[:, 0:ilen], xt.a(kc), wqu.a(kc, (i0, ilen)),
                        start=(kc == 0), stop=(kc == NFC - 1)),
                    reads=[xt.res, wqu.res], writes=[rBig[1]],
                    tag=f"up{ti}_{gi}_{kc}")
            p.S(lambda e, ilen=ilen, sg=sg: e.activation(sg.a((0, ilen)),
                                                  bigap(0)[:, 0:ilen], AF.Silu),
                reads=[rBig[0]], writes=[sg.res], tag=f"si{ti}_{gi}")
            p.V(lambda e, i0=i0, ilen=ilen, hb=hb, sg=sg: e.tensor_tensor(
                    hb.a((i0, ilen)), sg.a((0, ilen)), bigap(1)[:, 0:ilen],
                    ALU.mult),
                reads=[sg.res, rBig[1]], writes=[hb.res], tag=f"hm{ti}_{gi}")
        for qi in range(2):
            off, ln = qi * 1408, 1408
            qv = qs2[qi]
            qv["xres"] = hb.res
            qv["outres"] = hq.res
            _quant_emit(p, nc, None, hb.a((off, ln)), hq.a((off, ln)),
                        ln // 16, qv, tag=f"qh{ti}_{qi}_")
        ch = 0
        grp = 0
        while ch < NIC:
            n4 = min(4, NIC - ch)
            hs = hqs[grp % 2]
            for q2 in range(n4):
                pt = psTa if q2 % 2 == 0 else psTb
                rt_ = rTa if q2 % 2 == 0 else rTb
                p.T(lambda e, ch=ch, q2=q2, pt=pt, hq=hq: e.transpose(
                        pt.ap()[:, 0:128], hq.a(((ch + q2) * 128, 128)),
                        identb.a()),
                    reads=[hq.res, identb.res], writes=[rt_],
                    tag=f"ht{ti}_{ch + q2}")
                p.S(lambda e, q2=q2, pt=pt, hs=hs: e.copy(
                        hs.a(q2), pt.ap()[:, 0:128]),
                    reads=[rt_], writes=[hs.res], tag=f"he{ti}_{ch + q2}")
            p.store(shq[ti][:, ch:ch + n4, :], hs.a((0, n4)), reads=[hs.res],
                    writes=[rhq[ti]], tag=f"hs{ti}_{grp}")
            ch += n4
            grp += 1
    AB.free(wqg, wqu, xq[0], xq[1], hqr[0], hqr[1], hqs[0], hqs[1])
    AF32.free(hbr[0], hbr[1], sgr[0])

    wqd = AB.alloc("wqd", [NIC, 1024])
    ftd = AF32.alloc("ftd", [FFI])
    for rt in range(NFC):
        p.load(ftd.a(), d["wdown"][rt * 128:(rt + 1) * 128, :],
               writes=[ftd.res], tag=f"wd_l{rt}")
        qdt2 = AB.alloc(f"qdt2_{rt}", [FFI]) if rt == 0 else env["qdt2"]
        env["qdt2"] = qdt2
        for qi in range(2):
            off, ln = qi * 1408, 1408
            qv = qs2[qi]
            qv["xres"] = ftd.res
            qv["outres"] = qdt2.res
            _quant_emit(p, nc, None, ftd.a((off, ln)), qdt2.a((off, ln)),
                        ln // 16, qv, tag=f"wd_q{rt}_{qi}_")
        for ch in range(NIC):
            pt = psTa if ch % 2 == 0 else psTb
            rt_ = rTa if ch % 2 == 0 else rTb
            p.T(lambda e, ch=ch, pt=pt, qdt2=qdt2: e.transpose(
                    pt.ap()[:, 0:128], qdt2.a((ch * 128, 128)), identb.a()),
                reads=[qdt2.res, identb.res], writes=[rt_],
                tag=f"wd_t{rt}_{ch}")
            p.S(lambda e, ch=ch, rt=rt, pt=pt: e.copy(
                    wqd.a(ch, (rt * 128, 128)), pt.ap()[:, 0:128]),
                reads=[rt_], writes=[wqd.res], tag=f"wd_e{rt}_{ch}")
    free_qscr(qs2[0])
    free_qscr(qs2[1])
    AF32.free(ftd)

    hqt = [AB.alloc(f"hqt{i}", [NIC, 128]) for i in range(2)]
    opt = [AF32.alloc(f"opt{i}", [1024]) for i in range(2)]
    fin = [AF32.alloc(f"fin{i}", [1024]) for i in range(2)]
    for ti in range(NT128):
        b, trow = ti // NCH, (ti % NCH) * 128
        ht = hqt[ti % 2]
        ot = opt[ti % 2]
        p.load(ht.a(), shq[ti], reads=[rhq[ti]], writes=[ht.res],
               tag=f"p2h{ti}")
        p.load(ot.a(), soutp[ti], reads=[routp[ti]], writes=[ot.res],
               tag=f"p2o{ti}")
        for nh in range(2):
            for ic in range(NIC):
                p.T(lambda e, ic=ic, nh=nh, ht=ht: e.matmul(
                        bigap(nh), ht.a(ic), wqd.a(ic, (nh * 512, 512)),
                        start=(ic == 0), stop=(ic == NIC - 1)),
                    reads=[ht.res, wqd.res], writes=[rBig[nh]],
                    tag=f"dn{ti}_{ic}_{nh}")
        fi = fin[ti % 2]
        p.V(lambda e, ot=ot, fi=fi: e.tensor_tensor(
                fi.a(), psW.ap(), ot.a(), ALU.add),
            reads=[rBig[0], rBig[1], ot.res], writes=[fi.res],
            tag=f"fadd{ti}")
        p.store(out_t[b, trow:trow + 128, :], fi.a(), reads=[fi.res],
                tag=f"out{ti}")


_NC_CACHE = {}


def _get_nc():
    if "nc" not in _NC_CACHE:
        nc, stack = build_program(upto="all")
        _NC_CACHE["nc"] = nc
        _NC_CACHE["stack"] = stack
    return _NC_CACHE["nc"]


def _consts():
    i128 = np.eye(128, dtype=np.float32)
    k = np.arange(128)
    import ml_dtypes
    BF = ml_dtypes.bfloat16
    pf = np.zeros((16, 1024), np.float32)
    for h in range(16):
        pf[h, h * 64:(h + 1) * 64] = 1.0
    pfT = np.zeros((128, NFC * 16), np.float32)
    for fc in range(8):
        for h2 in range(2):
            pfT[h2 * 64:(h2 + 1) * 64, fc * 16 + 2 * fc + h2] = 1.0
    return {
        "identb": i128.astype(BF), "identf": i128,
        "triLneg": -((k[:, None] <= k[None, :]).astype(np.float32)),
        "triUneg": -((k[:, None] > k[None, :]).astype(np.float32)),
        "maskLowNeg": ((k[:, None] > k[None, :]).astype(np.float32)
                       * np.float32(-1e30)).astype(BF),
        "pfull": pf.astype(BF), "pfullT": pfT.astype(BF),
        "onesb": np.ones((1, 512), BF), "onesf": np.ones((1, 128), np.float32),
    }


def kernel(**inputs):
    nc = _get_nc()
    consts = _consts()
    xfull = np.ascontiguousarray(np.asarray(inputs["L0_attn_x"], np.float32))
    shared = {
        "wsqT": np.ascontiguousarray(inputs["L0_attn_Wsq"].T),
        "wskT": np.ascontiguousarray(inputs["L0_attn_Wsk"].T),
        "wsvT": np.ascontiguousarray(inputs["L0_attn_Wsv"].T),
        "wvT": np.ascontiguousarray(inputs["L0_attn_Wv"].T),
        "wqT": np.ascontiguousarray(inputs["L0_attn_Wq"].T),
        "wkT": np.ascontiguousarray(inputs["L0_attn_Wk"].T),
        "woT": np.ascontiguousarray(inputs["L0_attn_Wo"].T),
        "wgT": np.ascontiguousarray(inputs["L0_attn_Wg"].T),
        "bsq": np.ascontiguousarray(inputs["L0_attn_bsq"]).reshape(1, -1),
        "bsk": np.ascontiguousarray(inputs["L0_attn_bsk"]).reshape(1, -1),
        "bsv": np.ascontiguousarray(inputs["L0_attn_bsv"]).reshape(1, -1),
        "bq": np.ascontiguousarray(inputs["L0_attn_bq"]).reshape(1, -1),
        "bk": np.ascontiguousarray(inputs["L0_attn_bk"]).reshape(1, -1),
        "bv": np.ascontiguousarray(inputs["L0_attn_bv"]).reshape(1, -1),
        "bo": np.ascontiguousarray(inputs["L0_attn_bo"]).reshape(1, -1),
        "bg": np.ascontiguousarray(inputs["L0_attn_bg"]).reshape(1, -1),
        "temp16": np.ascontiguousarray(
            np.asarray(inputs["L0_attn_temp"], np.float32).reshape(16, 1)),
        "wgate": np.ascontiguousarray(inputs["L0_ffn_w_gate"]),
        "wup": np.ascontiguousarray(inputs["L0_ffn_w_up"]),
        "wdown": np.ascontiguousarray(inputs["L0_ffn_w_down"]),
    }
    shared = {k: np.ascontiguousarray(np.asarray(v, np.float32))
              for k, v in shared.items()}
    shared.update(consts)
    in_maps = []
    for core in range(NCORES):
        m = dict(shared)
        m["x"] = np.ascontiguousarray(xfull[core * NB:(core + 1) * NB])
        in_maps.append(m)
    res = run_bass_kernel_spmd(nc, in_maps, core_ids=list(range(NCORES)))
    out = np.concatenate([r["out"] for r in res.results], axis=0)
    return out.astype(np.float32)


# version 29
# speedup vs baseline: 1.0183x; 1.0108x over previous
import numpy as np
from contextlib import ExitStack

import concourse.bass as bass
import concourse.mybir as mybir
from concourse.bass_utils import run_bass_kernel_spmd

F32 = mybir.dt.float32
BF16 = mybir.dt.bfloat16
AF = mybir.ActivationFunctionType
ALU = mybir.AluOpType
AX = mybir.AxisListType

B, T, DIM, H = 16, 2048, 1024, 16
HD = DIM // H
FFI = 2816
NCORES = 8
NB = B // NCORES
TOK = NB * T
NT128 = TOK // 128
NCH = T // 128
NFC = DIM // 128
NIC = FFI // 128
LN_EPS = 1e-5

FP4_MID = [0.25, 0.75, 1.25, 1.75, 2.5, 3.5, 5.0]
FP4_W = [0.5, 0.5, 0.5, 0.5, 1.0, 1.0, 2.0]

SAME_ENGINE_WAITS = False
DETECT_RACES = False


class Res:
    __slots__ = ("name", "writer", "readers", "preds")

    def __init__(self, name=""):
        self.name = name
        self.writer = None
        self.readers = []
        self.preds = []


class Op:
    __slots__ = ("eng", "fn", "ticket", "deps", "need_inc", "tag")

    def __init__(self, eng, fn, tag):
        self.eng = eng
        self.fn = fn
        self.ticket = None
        self.deps = []
        self.need_inc = False
        self.tag = tag


class Prog:
    NLANES = 8

    def __init__(self):
        self.ops = {k: [] for k in ("T", "V", "S", "G", "Y")}
        self.dma_count = {"DY": 0, "DG": 0}
        self.lane_last = {}
        self.lane_uses = {}
        self.n_ops = 0

    def emit(self, eng, fn, reads=(), writes=(), tag=""):
        op = Op(eng, fn, tag)
        seen = set()

        def dep(x):
            if x is not None and id(x) not in seen and x is not op:
                seen.add(id(x))
                op.deps.append(x)

        for r in reads:
            dep(r.writer)
        for r in writes:
            dep(r.writer)
            for rd in r.readers:
                dep(rd)
            for p in r.preds:
                dep(p)
        issue_eng = {"DY": "Y", "DG": "G"}.get(eng, eng)
        if eng in ("DY", "DG"):
            lane = self.dma_count[eng] % self.NLANES
            self.dma_count[eng] += 1
            key = (eng, lane)
            dep(self.lane_last.get(key))
            self.lane_last[key] = op
            self.lane_uses[key] = self.lane_uses.get(key, 0) + 1
            op.ticket = (f"{eng}{lane}", 16 * self.lane_uses[key])
        self.ops[issue_eng].append(op)
        self.n_ops += 1
        for r in reads:
            r.readers.append(op)
        for r in writes:
            r.writer = op
            r.readers = []
            r.preds = []
        return op

    def T(self, fn, reads=(), writes=(), tag=""):
        return self.emit("T", fn, reads, writes, tag)

    def V(self, fn, reads=(), writes=(), tag=""):
        return self.emit("V", fn, reads, writes, tag)

    def S(self, fn, reads=(), writes=(), tag=""):
        return self.emit("S", fn, reads, writes, tag)

    def G(self, fn, reads=(), writes=(), tag=""):
        return self.emit("G", fn, reads, writes, tag)

    def load(self, dst_ap, src_ap, reads=(), writes=(), tag="ld"):
        return self.emit("DY", lambda e: e.dma_start(dst_ap, src_ap),
                         reads, writes, tag)

    def store(self, dst_ap, src_ap, reads=(), writes=(), tag="st"):
        return self.emit("DG", lambda e: e.dma_start(dst_ap, src_ap),
                         reads, writes, tag)

    COST = {"T": 0.25, "V": 0.9, "S": 0.9, "G": 0.7, "Y": 0.1}

    def schedule(self):
        import heapq
        eng_of = {}
        idx_of = {}
        nsucc_sched = {}
        for eng, lst in self.ops.items():
            for i, op in enumerate(lst):
                eng_of[id(op)] = eng
                idx_of[id(op)] = i
        npred = {}
        succs = {}
        for eng, lst in self.ops.items():
            for op in lst:
                npred[id(op)] = len(op.deps)
                for dp in op.deps:
                    succs.setdefault(id(dp), []).append(op)
        finish = {}
        clock = {e: 0.0 for e in self.ops}
        ready = {e: [] for e in self.ops}
        new_order = {e: [] for e in self.ops}
        n_total = sum(len(v) for v in self.ops.values())

        def push_ready(op):
            av = max((finish[id(dp)] for dp in op.deps), default=0.0)
            e = eng_of[id(op)]
            heapq.heappush(ready[e], (av, idx_of[id(op)], id(op), op))

        for eng, lst in self.ops.items():
            for op in lst:
                if npred[id(op)] == 0:
                    push_ready(op)
        done = 0
        while done < n_total:
            best, bh = None, None
            for e, h in ready.items():
                if not h:
                    continue
                av = h[0][0]
                start = max(clock[e], av)
                key = (start, av)
                if best is None or key < bh:
                    best, bh = e, key
            assert best is not None, "scheduler deadlock (cycle in deps)"
            av, _, _, op = heapq.heappop(ready[best])
            e = best
            dma = op.ticket is not None and op.ticket[0][0] == "D"
            dur = 3.2 if dma else self.COST[e]
            start = max(clock[e], av)
            fin = start + dur
            clock[e] = start + (0.15 if dma else dur)
            finish[id(op)] = fin
            new_order[e].append(op)
            done += 1
            for su in succs.get(id(op), ()):
                npred[id(su)] -= 1
                if npred[id(su)] == 0:
                    push_ready(su)
        self.ops = new_order

    def lower(self, nc, ec):
        self.schedule()
        for eng in self.ops:
            for op in self.ops[eng]:
                for d in op.deps:
                    d.need_inc = True
        for eng in ("T", "V", "S", "G"):
            c = 0
            for op in self.ops[eng]:
                if op.ticket is not None:
                    continue
                if op.need_inc:
                    c += 1
                    op.ticket = (eng, c)

        sem_names = ["T", "V", "S", "G"]
        for q in ("DY", "DG"):
            for ln in range(self.NLANES):
                sem_names.append(f"{q}{ln}")
        sems = {k: ec(nc.semaphore("sem" + k)) for k in sem_names}

        def runner(key):
            def run(h):
                waited = {}
                for op in self.ops[key]:
                    for d in op.deps:
                        if d.ticket is None:
                            continue
                        sname, val = d.ticket
                        if sname == key and not SAME_ENGINE_WAITS:
                            continue
                        if waited.get(sname, -1) >= val:
                            continue
                        h.wait_ge(sems[sname], val)
                        waited[sname] = val
                    inst = op.fn(h)
                    if op.ticket is not None:
                        sname, val = op.ticket
                        inst.then_inc(sems[sname],
                                      16 if sname[0] == "D" else 1)
                q = {"G": "DG", "Y": "DY"}.get(key)
                if q:
                    for ln in range(self.NLANES):
                        u = self.lane_uses.get((q, ln), 0)
                        if u:
                            h.wait_ge(sems[f"{q}{ln}"], 16 * u)
            return run

        block = ec(nc.Block())
        block.tensor(runner("T"))
        block.vector(runner("V"))
        block.scalar(runner("S"))
        block.gpsimd(runner("G"))
        block.sync(runner("Y"))


class Arena:

    def __init__(self, nc, name, per_part_elems, dtype):
        self.t = nc.alloc_sbuf_tensor(name, [128, per_part_elems], dtype)
        self.cap = per_part_elems
        self.cursor = 0
        self.live = []
        self.dead = []
        self.name_start = {}

    def alloc(self, name, shape):
        n = int(np.prod(shape))
        ivs = sorted((s, e) for (s, e, _) in self.live)
        start = None
        pos = 0
        for (s0, e0) in ivs:
            cand = (pos + 31) & ~31
            if cand + n <= s0:
                start = cand
                break
            pos = max(pos, e0)
        if start is None:
            cand = (pos + 31) & ~31
            assert cand + n <= self.cap, \
                f"arena {self.t.name} overflow: {name}: {cand + n} > {self.cap}"
            start = cand
        end = start + n
        prev = self.name_start.setdefault(name, start)
        assert prev == start, \
            f"region {name} realloc moved: {prev} -> {start} (closure hazard)"
        reg = Region(self, name, start, list(shape))
        for (s0, e0, r0) in self.dead:
            if s0 < end and start < e0:
                if r0.res.writer is not None:
                    reg.res.preds.append(r0.res.writer)
                reg.res.preds.extend(r0.res.readers)
        self.live.append((start, end, reg))
        return reg

    def free(self, *regs):
        ids = {id(r) for r in regs}
        keep = []
        for (s, e, r) in self.live:
            (self.dead if id(r) in ids else keep).append((s, e, r))
        self.live = keep


class Region:

    def __init__(self, arena, name, start, shape):
        self.arena = arena
        self.name = name
        self.start = start
        self.shape = shape
        self.res = Res(name)
        st = []
        acc = 1
        for s in reversed(shape):
            st.append(acc)
            acc *= s
        self.strides = list(reversed(st))

    def a(self, *idx, p0=0, pn=128):
        pat = [[self.arena.cap, 128]]
        base = 0
        for d, s in enumerate(self.shape):
            sel = idx[d] if d < len(idx) else None
            if sel is None:
                pat.append([self.strides[d], s])
            elif isinstance(sel, int):
                base += sel * self.strides[d]
            else:
                st, cnt = sel
                base += st * self.strides[d]
                pat.append([self.strides[d], cnt])
        ap = bass.AP(self.arena.t, self.start + base, pat)
        if p0 or pn != 128:
            ap = ap[p0:p0 + pn]
        return ap


def _quant_emit(p, nc, ctx, xf32_ap, out_bf_ap, nblk, scr, tag=""):
    N = nblk * 16
    bm, s, r, sbf = scr["bm"], scr["s"], scr["r"], scr["sbf"]
    an, aab, sgn = scr["an"], scr["aab"], scr["sgn"]
    c = scr["c"]; g = scr["g"]
    xres = scr["xres"]; outres = scr["outres"]
    blk = lambda ap: ap

    x3 = xf32_ap.rearrange("p (nb k) -> p nb k", k=16)
    p.V(lambda e: e.tensor_reduce(bm.a((0, nblk)), x3, AX.X, ALU.max,
                                  apply_absolute_value=True),
        reads=[xres], writes=[bm.res], tag=tag + "bm")
    p.V(lambda e: e.tensor_scalar(s.a((0, nblk)), bm.a((0, nblk)),
                                  1.0 / 6.0, 1e-12, ALU.mult, ALU.max),
        reads=[bm.res], writes=[s.res], tag=tag + "s")
    p.V(lambda e: e.reciprocal(r.a((0, nblk)), s.a((0, nblk))),
        reads=[s.res], writes=[r.res], tag=tag + "r")
    p.V(lambda e: e.tensor_copy(sbf.a((0, nblk)), s.a((0, nblk))),
        reads=[s.res], writes=[sbf.res], tag=tag + "sbf")
    rbc = r.a((0, nblk)).broadcast_to([128, nblk, 16])
    p.V(lambda e: e.tensor_tensor(an.a((0, N)).rearrange("p (nb k) -> p nb k", k=16),
                                  x3, rbc, ALU.mult),
        reads=[xres, r.res], writes=[an.res], tag=tag + "an")
    p.S(lambda e: e.activation(aab.a((0, N)), an.a((0, N)), AF.Abs),
        reads=[an.res], writes=[aab.res], tag=tag + "aab")
    p.S(lambda e: e.activation(sgn.a((0, N)), an.a((0, N)), AF.Sign),
        reads=[an.res], writes=[sgn.res], tag=tag + "sgn")
    def cmp(j, dst):
        p.V(lambda e, j=j, dst=dst: e.tensor_scalar(dst.a((0, N)), aab.a((0, N)),
                                                    FP4_MID[j], FP4_W[j],
                                                    ALU.is_gt, ALU.mult),
            reads=[aab.res], writes=[dst.res], tag=tag + f"c{j}")
    cmp(0, c[0]); cmp(1, c[1])
    p.V(lambda e: e.tensor_tensor(g[0].a((0, N)), c[0].a((0, N)), c[1].a((0, N)), ALU.add),
        reads=[c[0].res, c[1].res], writes=[g[0].res], tag=tag + "g0")
    cmp(2, c[0]); cmp(3, c[1])
    p.V(lambda e: e.tensor_tensor(g[1].a((0, N)), c[0].a((0, N)), c[1].a((0, N)), ALU.add),
        reads=[c[0].res, c[1].res], writes=[g[1].res], tag=tag + "g1")
    cmp(4, c[0]); cmp(5, c[1])
    p.V(lambda e: e.tensor_tensor(g[2].a((0, N)), c[0].a((0, N)), c[1].a((0, N)), ALU.add),
        reads=[c[0].res, c[1].res], writes=[g[2].res], tag=tag + "g2")
    cmp(6, c[0])
    p.V(lambda e: e.tensor_tensor(g[0].a((0, N)), g[0].a((0, N)), g[1].a((0, N)), ALU.add),
        reads=[g[0].res, g[1].res], writes=[g[0].res], tag=tag + "g3")
    p.V(lambda e: e.tensor_tensor(g[2].a((0, N)), g[2].a((0, N)), c[0].a((0, N)), ALU.add),
        reads=[g[2].res, c[0].res], writes=[g[2].res], tag=tag + "g4")
    p.V(lambda e: e.tensor_tensor(g[0].a((0, N)), g[0].a((0, N)), g[2].a((0, N)), ALU.add),
        reads=[g[0].res, g[2].res], writes=[g[0].res], tag=tag + "qsum")
    p.V(lambda e: e.tensor_tensor(g[1].a((0, N)), g[0].a((0, N)), sgn.a((0, N)), ALU.mult),
        reads=[g[0].res, sgn.res], writes=[g[1].res], tag=tag + "qsgn")
    sbc = sbf.a((0, nblk)).broadcast_to([128, nblk, 16])
    p.V(lambda e: e.tensor_tensor(out_bf_ap.rearrange("p (nb k) -> p nb k", k=16),
                                  g[1].a((0, N)).rearrange("p (nb k) -> p nb k", k=16),
                                  sbc, ALU.mult),
        reads=[g[1].res, sbf.res], writes=[outres], tag=tag + "qout")


def build_nc():
    nc = bass.Bass("TRN2", target_bir_lowering=False,
                   detect_race_conditions=DETECT_RACES)

    d = {}
    def din(name, shape, dt=F32):
        d[name] = nc.dram_tensor(name, shape, dt, kind="ExternalInput").ap()
        return d[name]

    x_in = din("x", [NB, T, DIM])
    for w in ("wsqT", "wskT", "wsvT", "wvT", "wqT", "wkT", "woT"):
        din(w, [DIM, DIM])
    din("wgT", [DIM, H])
    for bn in ("bsq", "bsk", "bsv", "bq", "bk", "bv", "bo"):
        din(bn, [1, DIM])
    din("bg", [1, H])
    din("temp16", [H, 1])
    din("wgate", [FFI, DIM])
    din("wup", [FFI, DIM])
    din("wdown", [DIM, FFI])
    din("identb", [128, 128], BF16)
    din("identf", [128, 128], F32)
    din("triLneg", [128, 128], F32)
    din("triUneg", [128, 128], F32)
    din("maskLowNeg", [128, 128], BF16)
    din("pfull", [H, DIM], BF16)
    din("pfullT", [128, NFC * H], BF16)
    din("onesb", [1, 512], BF16)
    din("onesf", [1, 128], F32)

    out_t = nc.dram_tensor("out", [NB, T, DIM], F32, kind="ExternalOutput").ap()

    ssq = nc.dram_tensor("ssq", [NB, NCH, 128, NFC, 128], BF16, kind="Internal").ap()
    ssk = nc.dram_tensor("ssk", [NB, NCH, 128, NFC, 128], BF16, kind="Internal").ap()
    ssv = nc.dram_tensor("ssv", [NB, NCH, 128, DIM], BF16, kind="Internal").ap()
    sV = nc.dram_tensor("sV", [NB, NCH, 128, H, HD], BF16, kind="Internal").ap()
    sxq = nc.dram_tensor("sxq", [NT128, 128, NFC, 128], BF16, kind="Internal").ap()
    shq = nc.dram_tensor("shq", [NT128, 128, NIC, 128], BF16, kind="Internal").ap()
    soutp = nc.dram_tensor("soutp", [NT128, 128, DIM], F32, kind="Internal").ap()
    rsq = [[Res() for _ in range(NCH)] for _ in range(NB)]
    rsk = [[Res() for _ in range(NCH)] for _ in range(NB)]
    rsv = [[Res() for _ in range(NCH)] for _ in range(NB)]
    rV = [[Res() for _ in range(NCH)] for _ in range(NB)]
    rxq = [Res() for _ in range(NT128)]
    rhq = [Res() for _ in range(NT128)]
    routp = [Res() for _ in range(NT128)]

    stack = ExitStack()
    ec = stack.enter_context
    p = Prog()
    AF32 = Arena(nc, "AF", 13568, F32)
    AB = Arena(nc, "ABF", 79264, BF16)
    psW = nc.alloc_psum_tensor("psW", [128, 1024], F32)
    psX = nc.alloc_psum_tensor("psX", [128, 1024], F32)
    psA = nc.alloc_psum_tensor("psA", [128, 512], F32)
    psB = nc.alloc_psum_tensor("psB", [128, 512], F32)
    psTa = nc.alloc_psum_tensor("psTa", [128, 1024], BF16)
    psTb = nc.alloc_psum_tensor("psTb", [128, 1024], BF16)
    rBig = [Res(f"psBig{i}") for i in range(4)]
    rA, rB_, rTa, rTb = Res("psA"), Res("psB"), Res("psTa"), Res("psTb")
    def bigap(i, w=512):
        t = psW if i < 2 else psX
        j = i % 2
        return t.ap()[:, j * 512:j * 512 + w]

    return (nc, d, out_t, stack, ec, p, AF32, AB,
            dict(psW=psW, psX=psX, psA=psA, psB=psB, psTa=psTa, psTb=psTb,
                 rBig=rBig, rA=rA, rB=rB_, rTa=rTa, rTb=rTb, bigap=bigap),
            dict(ssq=ssq, ssk=ssk, ssv=ssv, sV=sV, sxq=sxq, shq=shq,
                 soutp=soutp, rsq=rsq, rsk=rsk, rsv=rsv, rV=rV, rxq=rxq,
                 rhq=rhq, routp=routp))


_DBG = {}


def build_program(upto="all"):
    (nc, d, out_t, stack, ec, p, AF32, AB, PS, SC) = build_nc()
    psW, psX, psA, psB = PS["psW"], PS["psX"], PS["psA"], PS["psB"]
    psTa, psTb = PS["psTa"], PS["psTb"]
    rBig, rA, rB, rTa, rTb = PS["rBig"], PS["rA"], PS["rB"], PS["rTa"], PS["rTb"]
    bigap = PS["bigap"]

    cst = {}
    def c_load(nm, arena, shape, pn=128, src=None):
        reg = arena.alloc(nm, shape)
        p.load(reg.a(p0=0, pn=pn), src if src is not None else d[nm],
               writes=[reg.res], tag="c_" + nm)
        return reg
    identb = c_load("identb", AB, [128])
    identf = c_load("identf", AF32, [128])
    triLneg = c_load("triLneg", AF32, [128])
    triUneg = c_load("triUneg", AF32, [128])
    maskLN = c_load("maskLowNeg", AB, [128])
    pfull = c_load("pfull", AB, [DIM], pn=H)
    pfullT = c_load("pfullT", AB, [NFC, H])
    onesb = c_load("onesb", AB, [512], pn=1)
    onesf = c_load("onesf", AF32, [128], pn=1)
    bias_b = {}
    bf_tmp = AF32.alloc("btmp", [DIM])
    for bn in ("bsq", "bsk", "bsv", "bq", "bk", "bv", "bo", "bg"):
        n = H if bn == "bg" else DIM
        reg = AB.alloc(bn + "_b", [n])
        p.load(bf_tmp.a((0, n), p0=0, pn=1), d[bn], writes=[bf_tmp.res], tag=bn)
        p.V(lambda e, r=reg, n=n: e.tensor_copy(r.a(p0=0, pn=1),
                                                bf_tmp.a((0, n), p0=0, pn=1)),
            reads=[bf_tmp.res], writes=[reg.res], tag=bn + "c")
        bias_b[bn] = reg
    tmp16 = AF32.alloc("tmp16", [1])
    sp16 = AF32.alloc("sp16", [1])
    p.load(tmp16.a(p0=0, pn=H), d["temp16"], writes=[tmp16.res], tag="temp")
    p.S(lambda e: e.activation(tmp16.a(p0=0, pn=H), tmp16.a(p0=0, pn=H), AF.Exp),
        reads=[tmp16.res], writes=[tmp16.res], tag="spt1")
    p.S(lambda e: e.activation(sp16.a(p0=0, pn=H), tmp16.a(p0=0, pn=H),
                               AF.Ln, bias=1.0),
        reads=[tmp16.res], writes=[sp16.res], tag="spt2")

    logg = AF32.alloc("logg", [NT128, H])
    _DBG["logg"] = logg

    wA = {}
    ftmp = [AF32.alloc(f"ftmp{i}", [1024]) for i in range(2)]
    for nm, dn in (("wsq", "wsqT"), ("wsk", "wskT"), ("wsv", "wsvT"),
                   ("wv", "wvT")):
        reg = AB.alloc(nm, [NFC, 1024])
        for kc in range(NFC):
            ft = ftmp[kc % 2]
            p.load(ft.a(), d[dn][kc * 128:(kc + 1) * 128, :],
                   writes=[ft.res], tag=f"{nm}l{kc}")
            p.V(lambda e, r=reg, kc=kc, ft=ft: e.tensor_copy(r.a(kc), ft.a()),
                reads=[ft.res], writes=[reg.res], tag=f"{nm}c{kc}")
        wA[nm] = reg
    wg = AB.alloc("wg", [NFC, H])
    ftg = AF32.alloc("ftg", [NFC, H])
    p.load(ftg.a(), d["wgT"].rearrange("(kc p) h -> p kc h", p=128),
           writes=[ftg.res], tag="wgl")
    p.V(lambda e: e.tensor_copy(wg.a(), ftg.a()), reads=[ftg.res],
        writes=[wg.res], tag="wgc")

    xin = [AF32.alloc(f"xin{i}", [1024]) for i in range(2)]
    xnf = [AF32.alloc(f"xnf{i}", [1024]) for i in range(2)]
    x2s = AF32.alloc("x2s", [1024])
    sm = AF32.alloc("sm", [16])
    ezg = AF32.alloc("ezg", [H])
    xnbf = [AB.alloc(f"xnbf{i}", [1024]) for i in range(2)]
    xnfmr = [AB.alloc(f"xnfm{i}", [NFC, 512]) for i in range(2)]
    xqtm = [AB.alloc(f"xqtm{i}", [1024]) for i in range(2)]
    xqst = [AB.alloc(f"xqst{i}", [NFC, 128]) for i in range(2)]
    stg = {"sq": [AB.alloc("stsq0", [NFC, 512])],
           "sk": [AB.alloc("stsk0", [NFC, 512])]}
    svst = [AB.alloc(f"svst{i}", [1024]) for i in range(2)]
    Vst = [AB.alloc(f"Vst{i}", [1024]) for i in range(2)]
    def mk_qscr(QN, pfx, sgn_share=None):
        if sgn_share is not None:
            sgn_reg = sgn_share
        else:
            sgn_reg = None
        return dict(
            bm=AF32.alloc(pfx + "bm", [QN // 16]), s=AF32.alloc(pfx + "s", [QN // 16]),
            r=AF32.alloc(pfx + "r", [QN // 16]), sbf=AB.alloc(pfx + "sbf", [QN // 16]),
            an=AF32.alloc(pfx + "an", [QN]), aab=AF32.alloc(pfx + "aab", [QN]),
            sgn=(sgn_reg if sgn_reg is not None else AB.alloc(pfx + "sgn", [QN])),
            c=[AB.alloc(pfx + f"c{j}", [QN]) for j in range(2)],
            g=[AB.alloc(pfx + f"g{j}", [QN]) for j in range(3)])

    def free_qscr(q, skip_sgn=False):
        AF32.free(q["bm"], q["s"], q["r"], q["an"], q["aab"])
        AB.free(q["sbf"], *q["c"], *q["g"])
        if not skip_sgn:
            AB.free(q["sgn"])

    _q1 = mk_qscr(1024, "qa_")
    qscr2 = [_q1, _q1]

    ssq, ssk, ssv, sVt = SC["ssq"], SC["ssk"], SC["ssv"], SC["sV"]
    sxq, shq, soutp = SC["sxq"], SC["shq"], SC["soutp"]
    rsq, rsk, rsv, rV = SC["rsq"], SC["rsk"], SC["rsv"], SC["rV"]
    rxq, rhq, routp = SC["rxq"], SC["rhq"], SC["routp"]

    for tt in range(8):
        xnfm = xnfmr[tt % 2]
        b = tt // 4
        for j in range(4):
            ti = tt * 4 + j
            trow = (ti % 16) * 128
            xi = xin[ti % 2]
            p.load(xi.a(), d["x"][b, trow:trow + 128, :], writes=[xi.res],
                   tag=f"x{ti}")
            xn = xnf[ti % 2]
            p.V(lambda e, xi=xi: e.reduce_sum(sm.a((0, 1)), xi.a(), AX.X),
                reads=[xi.res], writes=[sm.res], tag=f"ln_s1_{ti}")
            p.S(lambda e, xi=xi: e.activation(x2s.a(), xi.a(), AF.Square,
                                              accum_out=sm.a((1, 1))),
                reads=[xi.res], writes=[x2s.res, sm.res], tag=f"ln_sq_{ti}")
            p.V(lambda e: e.tensor_scalar(sm.a((2, 1)), sm.a((0, 1)),
                                          1.0 / DIM, None, ALU.mult),
                reads=[sm.res], writes=[sm.res], tag=f"ln_m_{ti}")
            p.V(lambda e: e.tensor_scalar(sm.a((3, 1)), sm.a((1, 1)),
                                          1.0 / DIM, LN_EPS, ALU.mult, ALU.add),
                reads=[sm.res], writes=[sm.res], tag=f"ln_e2_{ti}")
            p.V(lambda e: e.tensor_tensor(sm.a((4, 1)), sm.a((2, 1)),
                                          sm.a((2, 1)), ALU.mult),
                reads=[sm.res], writes=[sm.res], tag=f"ln_m2_{ti}")
            p.V(lambda e: e.tensor_tensor(sm.a((5, 1)), sm.a((3, 1)),
                                          sm.a((4, 1)), ALU.subtract),
                reads=[sm.res], writes=[sm.res], tag=f"ln_var_{ti}")
            p.S(lambda e: e.activation(sm.a((6, 1)), sm.a((5, 1)), AF.Ln),
                reads=[sm.res], writes=[sm.res], tag=f"ln_ln_{ti}")
            p.S(lambda e: e.activation(sm.a((7, 1)), sm.a((6, 1)), AF.Exp,
                                       scale=-0.5),
                reads=[sm.res], writes=[sm.res], tag=f"ln_rs_{ti}")
            p.V(lambda e, xi=xi, xn=xn: e.tensor_scalar(
                    xn.a(), xi.a(), sm.a((2, 1)), sm.a((7, 1)),
                    ALU.subtract, ALU.mult),
                reads=[xi.res, sm.res], writes=[xn.res], tag=f"ln_ap_{ti}")
            xb_ = xnbf[ti % 2]
            p.V(lambda e, xn=xn, xb_=xb_: e.tensor_copy(xb_.a(), xn.a()),
                reads=[xn.res], writes=[xb_.res], tag=f"cst_{ti}")
            for fc in range(NFC):
                p.T(lambda e, fc=fc, xb_=xb_: e.transpose(
                        psTa.ap()[:, fc * 128:(fc + 1) * 128],
                        xb_.a((fc * 128, 128)), identb.a()),
                    reads=[xb_.res, identb.res], writes=[rTa],
                    tag=f"trx_{ti}_{fc}")
            p.S(lambda e, j=j, xnfm=xnfm: e.copy(
                    xnfm.a(None, (j * 128, 128)),
                    psTa.ap().rearrange("p (fc t) -> p fc t", t=128)),
                reads=[rTa], writes=[xnfm.res], tag=f"evx_{ti}")
            xq = xqtm[ti % 2]
            qscr = qscr2[ti % 2]
            qscr["xres"] = xn.res
            qscr["outres"] = xq.res
            _quant_emit(p, nc, None, xn.a(), xq.a(), DIM // 16, qscr,
                        tag=f"qx{ti}_")
            for fc in range(NFC):
                p.T(lambda e, fc=fc, xq=xq: e.transpose(
                        psTb.ap()[:, fc * 128:(fc + 1) * 128],
                        xq.a((fc * 128, 128)), identb.a()),
                    reads=[xq.res, identb.res], writes=[rTb],
                    tag=f"trq_{ti}_{fc}")
            xs = xqst[ti % 2]
            p.S(lambda e, xs=xs: e.copy(
                    xs.a(), psTb.ap().rearrange("p (fc t) -> p fc t", t=128)),
                reads=[rTb], writes=[xs.res], tag=f"evq_{ti}")
            p.store(sxq[ti], xs.a(), reads=[xs.res], writes=[rxq[ti]],
                    tag=f"sxq{ti}")
            for kc in range(NFC):
                p.T(lambda e, kc=kc, j=j, xnfm=xnfm: e.matmul(
                        psA.ap()[:, 0:H], xnfm.a(kc, (j * 128, 128)),
                        wg.a(kc), start=(kc == 0), stop=False),
                    reads=[xnfm.res, wg.res], writes=[rA], tag=f"zg_{ti}_{kc}")
            p.T(lambda e: e.matmul(psA.ap()[:, 0:H],
                                   onesb.a((0, 128), p0=0, pn=1),
                                   bias_b["bg"].a(p0=0, pn=1),
                                   start=False, stop=True),
                reads=[onesb.res, bias_b["bg"].res], writes=[rA],
                tag=f"zg_b_{ti}")
            p.S(lambda e: e.activation(ezg.a(), psA.ap()[:, 0:H], AF.Exp),
                reads=[rA], writes=[ezg.res], tag=f"zg_e_{ti}")
            p.S(lambda e, ti=ti: e.activation(logg.a(ti), ezg.a(), AF.Ln,
                                              bias=1.0),
                reads=[ezg.res], writes=[logg.res], tag=f"zg_l_{ti}")
        c0 = (tt % 4) * 4
        for prn, wn, bn, sres, sdst in (("sq", "wsq", "bsq", rsq, ssq),
                                        ("sk", "wsk", "bsk", rsk, ssk)):
            st = stg[prn][0]
            for ncx in range(NFC):
                bank = ncx % 2
                for kc in range(NFC):
                    p.T(lambda e, kc=kc, ncx=ncx, bank=bank, wn=wn, xnfm=xnfm: e.matmul(
                            bigap(bank), wA[wn].a(kc, (ncx * 128, 128)),
                            xnfm.a(kc), start=(kc == 0), stop=False),
                        reads=[wA[wn].res, xnfm.res], writes=[rBig[bank]],
                        tag=f"{prn}_{tt}_{ncx}_{kc}")
                p.T(lambda e, ncx=ncx, bank=bank, bn=bn: e.matmul(
                        bigap(bank), bias_b[bn].a((ncx * 128, 128), p0=0, pn=1),
                        onesb.a(p0=0, pn=1), start=False, stop=True),
                    reads=[bias_b[bn].res, onesb.res], writes=[rBig[bank]],
                    tag=f"{prn}b_{tt}_{ncx}")
                p.S(lambda e, ncx=ncx, bank=bank, st=st: e.copy(
                        st.a(ncx), bigap(bank)),
                    reads=[rBig[bank]], writes=[st.res], tag=f"{prn}e_{tt}_{ncx}")
            for q in range(4):
                cix = (tt % 4) * 4 + q
                p.store(sdst[tt // 4, cix],
                        st.a(None, (q * 128, 128)),
                        reads=[st.res], writes=[sres[tt // 4][cix]],
                        tag=f"{prn}s_{tt}_{q}")
        for j in range(4):
            ti = tt * 4 + j
            cix = ti % 16
            for prn, wn, bn, stb in (("sv", "wsv", "bsv", svst),
                                     ("V", "wv", "bv", Vst)):
                for kc in range(NFC):
                    for nh in range(2):
                        p.T(lambda e, kc=kc, nh=nh, j=j, wn=wn, xnfm=xnfm: e.matmul(
                                bigap(2 + nh), xnfm.a(kc, (j * 128, 128)),
                                wA[wn].a(kc, (nh * 512, 512)),
                                start=(kc == 0), stop=False),
                            reads=[xnfm.res, wA[wn].res], writes=[rBig[2 + nh]],
                            tag=f"{prn}_{ti}_{kc}_{nh}")
                for nh in range(2):
                    p.T(lambda e, nh=nh, bn=bn: e.matmul(
                            bigap(2 + nh), onesb.a((0, 128), p0=0, pn=1),
                            bias_b[bn].a((nh * 512, 512), p0=0, pn=1),
                            start=False, stop=True),
                        reads=[onesb.res, bias_b[bn].res], writes=[rBig[2 + nh]],
                        tag=f"{prn}b_{ti}_{nh}")
                sb_ = stb[ti % 2]
                p.V(lambda e, sb_=sb_: e.tensor_copy(
                        sb_.a(), psX.ap()),
                    reads=[rBig[2], rBig[3]], writes=[sb_.res],
                    tag=f"{prn}e_{ti}")
                if prn == "sv":
                    p.store(ssv[b, cix], sb_.a(), reads=[sb_.res],
                            writes=[rsv[b][cix]], tag=f"svs_{ti}")
                else:
                    p.store(sVt[b, cix], sb_.a().rearrange(
                                "p (h e) -> p h e", e=HD),
                            reads=[sb_.res], writes=[rV[b][cix]],
                            tag=f"Vs_{ti}")
    AF32.free(xin[0], xin[1], x2s, ftmp[0], ftmp[1], ftg, bf_tmp,
              ezg, xnf[0], xnf[1], tmp16, sm)
    AB.free(wA["wsq"], wA["wsk"], wA["wsv"], wA["wv"], wg,
            xnbf[0], xnbf[1], xnfmr[0], xnfmr[1], xqtm[0], xqtm[1],
            xqst[0], xqst[1],
            stg["sq"][0], stg["sk"][0], svst[0], svst[1], Vst[0], Vst[1],
            bias_b["bsq"], bias_b["bsk"], bias_b["bsv"], bias_b["bv"],
            bias_b["bg"])
    free_qscr(qscr2[0])
    if upto == "a1":
        p.lower(nc, ec)
        return nc, stack
    _phases_bcd(nc, d, out_t, p, AF32, AB, PS, SC, locals(), upto)
    p.lower(nc, ec)
    return nc, stack


def _phases_bcd(nc, d, out_t, p, AF32, AB, PS, SC, env, upto):
    psW, psX, psA, psB = PS["psW"], PS["psX"], PS["psA"], PS["psB"]
    psTa, psTb = PS["psTa"], PS["psTb"]
    rBig, rA, rB, rTa, rTb = PS["rBig"], PS["rA"], PS["rB"], PS["rTa"], PS["rTb"]
    bigap = PS["bigap"]
    ssq, ssk, ssv, sVt = SC["ssq"], SC["ssk"], SC["ssv"], SC["sV"]
    sxq, shq, soutp = SC["sxq"], SC["shq"], SC["soutp"]
    rsq, rsk, rsv, rV = SC["rsq"], SC["rsk"], SC["rsv"], SC["rV"]
    rxq, rhq, routp = SC["rxq"], SC["rhq"], SC["routp"]
    logg = env["logg"]
    identb, identf = env["identb"], env["identf"]
    triLneg, triUneg, maskLN = env["triLneg"], env["triUneg"], env["maskLN"]
    pfull, pfullT = env["pfull"], env["pfullT"]
    onesb, onesf = env["onesb"], env["onesf"]
    bias_b, sp16 = env["bias_b"], env["sp16"]

    ftc = [AF32.alloc(f"ftc{i}", [1024]) for i in range(2)]

    def load_w(nm, dn, tagp):
        reg = AB.alloc(nm, [NFC, 1024])
        for kc in range(NFC):
            ft = ftc[kc % 2]
            p.load(ft.a(), d[dn][kc * 128:(kc + 1) * 128, :], writes=[ft.res],
                   tag=f"{tagp}{nm}l{kc}")
            p.V(lambda e, r=reg, kc=kc, ft=ft: e.tensor_copy(r.a(kc), ft.a()),
                reads=[ft.res], writes=[reg.res], tag=f"{tagp}{nm}c{kc}")
        return reg

    Mf = AF32.alloc("Mf", [H, HD])
    lld = AF32.alloc("lld", [32])
    nl = AF32.alloc("nl", [16])
    ek2 = AF32.alloc("ek2", [16])
    eend = AF32.alloc("eend", [NFC])
    lTr = AF32.alloc("lTr", [H, 128])
    lTsb = AF32.alloc("lTsb", [128])

    for b in range(NB):
        Hseq = AB.alloc("Hseq", [NFC, T])
        _DBG["Hseq"] = Hseq
        sqg = [AB.alloc(f"sqg{i}", [NFC, 128]) for i in range(2)]
        skg = [AB.alloc(f"skg{i}", [NFC, 128]) for i in range(2)]
        svc = [AB.alloc(f"svc{i}", [DIM]) for i in range(2)]
        Mbf = AB.alloc("Mbf", [H, HD])
        qp = AB.alloc("qp", [NFC, 128])
        Dm = [AB.alloc(f"Dm{i}", [128]) for i in range(16)]
        SDm = [AB.alloc(f"SDm{i}", [128]) for i in range(2)]
        elT = AB.alloc("elT", [128])
        ktil = AB.alloc("ktil", [H, HD])
        p.V(lambda e: e.memset(Mf.a(), 0.0), writes=[Mf.res], tag=f"M0_{b}")
        for c in range(NCH):
            sq_, sk_ = sqg[c % 2], skg[c % 2]
            p.load(sq_.a(), ssq[b, c], reads=[rsq[b][c]],
                   writes=[sq_.res], tag=f"Bq{b}_{c}")
            p.load(sk_.a(), ssk[b, c], reads=[rsk[b][c]],
                   writes=[sk_.res], tag=f"Bk{b}_{c}")
            sv_ = svc[c % 2]
            p.load(sv_.a(), ssv[b, c], reads=[rsv[b][c]], writes=[sv_.res],
                   tag=f"Bv{b}_{c}")
            ti = b * NCH + c
            p.T(lambda e, ti=ti: e.matmul(psA.ap()[:, 0:H], triLneg.a(),
                                          logg.a(ti), start=True, stop=True),
                reads=[triLneg.res, logg.res], writes=[rA], tag=f"cums{ti}")
            p.T(lambda e, ti=ti: e.matmul(psA.ap()[:, H:2 * H], triUneg.a(),
                                          logg.a(ti), start=True, stop=True),
                reads=[triUneg.res, logg.res], writes=[rA], tag=f"ldif{ti}")
            p.V(lambda e: e.tensor_copy(lld.a((0, 32)), psA.ap()[:, 0:32]),
                reads=[rA], writes=[lld.res], tag=f"lev{ti}")
            p.V(lambda e: e.tensor_scalar(nl.a(), lld.a((0, 16)), -1.0, None,
                                          ALU.mult),
                reads=[lld.res], writes=[nl.res], tag=f"nl{ti}")
            p.S(lambda e: e.activation(ek2.a(), lld.a((16, 16)), AF.Exp),
                reads=[lld.res], writes=[ek2.res], tag=f"ek2{ti}")
            p.T(lambda e: e.transpose(psB.ap()[0:16, 0:128], lld.a((0, 16)),
                                      identf.a()),
                reads=[lld.res, identf.res], writes=[rB], tag=f"lT{ti}")
            p.S(lambda e: e.activation(elT.a(p0=0, pn=16),
                                       psB.ap()[0:16, 0:128], AF.Exp),
                reads=[rB], writes=[elT.res], tag=f"elT{ti}")
            for h in range(H):
                p.T(lambda e, h=h: e.transpose(psB.ap()[0:1, 128:256],
                                               lld.a((h, 1)), identf.a()),
                    reads=[lld.res, identf.res], writes=[rB], tag=f"lr{ti}_{h}")
                p.V(lambda e, h=h: e.tensor_copy(lTr.a(h, p0=0, pn=1),
                                                 psB.ap()[0:1, 128:256]),
                    reads=[rB], writes=[lTr.res], tag=f"lrc{ti}_{h}")
            for fc in range(NFC):
                p.T(lambda e, fc=fc: e.matmul(
                        bigap(fc // 4)[:, (fc % 4) * 128:(fc % 4) * 128 + 128],
                        pfull.a((fc * 128, 128), p0=0, pn=16),
                        elT.a(p0=0, pn=16), start=True, stop=True),
                    reads=[pfull.res, elT.res], writes=[rBig[fc // 4]],
                    tag=f"eg{ti}_{fc}")
            for fc in range(NFC):
                p.V(lambda e, fc=fc, sq_=sq_: e.tensor_tensor(
                        qp.a(fc), sq_.a(fc),
                        bigap(fc // 4)[:, (fc % 4) * 128:(fc % 4) * 128 + 128],
                        ALU.mult),
                    reads=[sq_.res, rBig[fc // 4]], writes=[qp.res],
                    tag=f"qp{ti}_{fc}")
            p.V(lambda e: e.tensor_copy(
                    eend.a(), bass.AP(psW, 127, [[1024, 128], [128, 8]])),
                reads=[rBig[0], rBig[1]], writes=[eend.res], tag=f"ee{ti}")
            p.V(lambda e: e.tensor_copy(Mbf.a(), Mf.a()),
                reads=[Mf.res], writes=[Mbf.res], tag=f"mc{ti}")
            for h in range(H):
                pban = psA if h % 2 == 0 else psB
                rban = rA if h % 2 == 0 else rB
                p.T(lambda e, h=h, pban=pban: e.matmul(
                        pban.ap()[:, 256:384], onesf.a((0, 128), p0=0, pn=1),
                        lTr.a(h, p0=0, pn=1), start=True, stop=False),
                    reads=[onesf.res, lTr.res], writes=[rban],
                    tag=f"lbc{ti}_{h}")
                p.T(lambda e, pban=pban: e.matmul(
                        pban.ap()[:, 256:384], identb.a(), maskLN.a(),
                        start=False, stop=True),
                    reads=[identb.res, maskLN.res], writes=[rban],
                    tag=f"lbm{ti}_{h}")
                p.S(lambda e, h=h, pban=pban: e.activation(
                        Dm[h].a(), pban.ap()[:, 256:384], AF.Exp,
                        bias=nl.a((h, 1))),
                    reads=[rban, nl.res], writes=[Dm[h].res],
                    tag=f"dex{ti}_{h}")
            for fc in range(NFC):
                p.T(lambda e, fc=fc, sk_=sk_: e.transpose(
                        psTa.ap()[:, fc * 128:(fc + 1) * 128],
                        sk_.a(fc), identb.a()),
                    reads=[sk_.res, identb.res], writes=[rTa],
                    tag=f"ktr{ti}_{fc}")
            for h in range(H):
                fc, off = h // 2, (h % 2) * 64
                p.V(lambda e, h=h, fc=fc, off=off: e.tensor_scalar(
                        ktil.a(h),
                        psTa.ap()[:, fc * 128 + off:fc * 128 + off + 64],
                        ek2.a((h, 1)), None, ALU.mult),
                    reads=[rTa, ek2.res], writes=[ktil.res],
                    tag=f"kts{ti}_{h}")
            for h in range(H):
                fc, off = h // 2, (h % 2) * 64
                sb_ = 2 + h % 2
                hb = h % 2
                if h % 2 == 0:
                    def qkfn(e, fc=fc, sk_=sk_, sq_=sq_):
                        e.matmul(bigap(2)[:, 0:128],
                                 sk_.a(fc, p0=0, pn=64),
                                 sq_.a(fc, p0=0, pn=64),
                                 start=True, stop=True)
                        return e.matmul(bigap(3)[:, 0:128],
                                        sk_.a(fc, p0=64, pn=64),
                                        sq_.a(fc, p0=64, pn=64),
                                        start=True, stop=True)
                    p.T(qkfn, reads=[sk_.res, sq_.res],
                        writes=[rBig[2], rBig[3]], tag=f"qk{ti}_{h}")
                p.V(lambda e, h=h, sb_=sb_: e.scalar_tensor_tensor(
                        SDm[h % 2].a(), bigap(sb_)[:, 0:128], 1.0,
                        Dm[h].a(), ALU.mult, ALU.mult),
                    reads=[rBig[sb_], Dm[h].res], writes=[SDm[h % 2].res],
                    tag=f"sdm{ti}_{h}")
                p.T(lambda e, h=h, hb=hb: e.matmul(
                        bigap(hb)[0:64, 128:256], Mbf.a(h), qp.a(h // 2),
                        start=True, stop=False),
                    reads=[Mbf.res, qp.res], writes=[rBig[hb]],
                    tag=f"hin{ti}_{h}")
                p.T(lambda e, h=h, hb=hb, sv_=sv_: e.matmul(
                        bigap(hb)[0:64, 128:256], sv_.a((h * HD, HD)),
                        SDm[h % 2].a(), start=False, stop=True),
                    reads=[sv_.res, SDm[h % 2].res], writes=[rBig[hb]],
                    tag=f"hsv{ti}_{h}")
                p.V(lambda e, h=h, hb=hb, fc=fc, off=off, c=c: e.tensor_copy(
                        Hseq.a(fc, (c * 128, 128), p0=off, pn=64),
                        bigap(hb)[0:64, 128:256]),
                    reads=[rBig[hb]], writes=[Hseq.res], tag=f"hev{ti}_{h}")
                db = psA if h % 2 == 0 else psB
                rdb = rA if h % 2 == 0 else rB
                p.T(lambda e, h=h, db=db, off=off, sv_=sv_: e.matmul(
                        db.ap()[off:off + 64, 384:448], ktil.a(h),
                        sv_.a((h * HD, HD)), start=True, stop=True,
                        tile_position=(0, off)),
                    reads=[ktil.res, sv_.res], writes=[rdb], tag=f"dlt{ti}_{h}")
                p.V(lambda e, h=h, db=db, off=off, fc=fc:
                    e.scalar_tensor_tensor(
                        Mf.a(h, p0=off, pn=64), Mf.a(h, p0=off, pn=64),
                        eend.a((fc, 1), p0=off, pn=64),
                        db.ap()[off:off + 64, 384:448], ALU.mult, ALU.add),
                    reads=[Mf.res, eend.res, rdb], writes=[Mf.res],
                    tag=f"mup{ti}_{h}")
        AB.free(sqg[0], sqg[1], skg[0], skg[1], svc[0], svc[1], Mbf, qp,
                *Dm, SDm[0], SDm[1], elT, ktil)
        if upto == "b":
            if b == 0:
                AB.free(Hseq)
            continue
        Qhat = AB.alloc("Qhat", [NFC, T])
        Khat = AB.alloc("Khat", [NFC, T])
        _DBG["Qhat"], _DBG["Khat"] = Qhat, Khat
        Rraw = AF32.alloc("Rraw", [NFC, 512])
        sqb = AB.alloc("sqb", [NFC, 512])
        rq = AF32.alloc("rq", [512])
        rqb = AB.alloc("rqb", [512])
        for prn, wn, bn, dst in (("q", "wqT", "bq", Qhat),
                                 ("k", "wkT", "bk", Khat)):
            wqk = load_w("wqk", wn, f"c{b}{prn}")
            for qg in range(4):
                for ncx in range(NFC):
                    bank = ncx % 2
                    for kc in range(NFC):
                        p.T(lambda e, kc=kc, ncx=ncx, bank=bank, qg=qg,
                                   wqk=wqk: e.matmul(
                                bigap(bank), wqk.a(kc, (ncx * 128, 128)),
                                Hseq.a(kc, (qg * 512, 512)),
                                start=(kc == 0), stop=False),
                            reads=[wqk.res, Hseq.res], writes=[rBig[bank]],
                            tag=f"{prn}p{b}_{qg}_{ncx}_{kc}")
                    p.T(lambda e, ncx=ncx, bank=bank, bn=bn: e.matmul(
                            bigap(bank),
                            bias_b[bn].a((ncx * 128, 128), p0=0, pn=1),
                            onesb.a(p0=0, pn=1), start=False, stop=True),
                        reads=[bias_b[bn].res, onesb.res], writes=[rBig[bank]],
                        tag=f"{prn}pb{b}_{qg}_{ncx}")
                    p.V(lambda e, ncx=ncx, bank=bank: e.tensor_copy(
                            Rraw.a(ncx), bigap(bank)),
                        reads=[rBig[bank]], writes=[Rraw.res],
                        tag=f"{prn}ev{b}_{qg}_{ncx}")
                    p.S(lambda e, ncx=ncx: e.activation(
                            sqb.a(ncx), Rraw.a(ncx), AF.Square),
                        reads=[Rraw.res], writes=[sqb.res],
                        tag=f"{prn}sq{b}_{qg}_{ncx}")
                for fc in range(NFC):
                    p.T(lambda e, fc=fc: e.matmul(
                            psA.ap()[0:16, 0:512], pfullT.a(fc), sqb.a(fc),
                            start=(fc == 0), stop=(fc == NFC - 1)),
                        reads=[pfullT.res, sqb.res], writes=[rA],
                        tag=f"{prn}ss{b}_{qg}_{fc}")
                p.V(lambda e: e.tensor_scalar(rq.a(p0=0, pn=16),
                                              psA.ap()[0:16, 0:512],
                                              1e-24, None, ALU.max),
                    reads=[rA], writes=[rq.res], tag=f"{prn}mx{b}_{qg}")
                p.S(lambda e: e.activation(rq.a(p0=0, pn=16),
                                           rq.a(p0=0, pn=16), AF.Ln),
                    reads=[rq.res], writes=[rq.res], tag=f"{prn}ln{b}_{qg}")
                p.S(lambda e: e.activation(rq.a(p0=0, pn=16),
                                           rq.a(p0=0, pn=16), AF.Exp,
                                           scale=-0.5),
                    reads=[rq.res], writes=[rq.res], tag=f"{prn}ex{b}_{qg}")
                if prn == "k":
                    p.V(lambda e: e.tensor_scalar(rq.a(p0=0, pn=16),
                                                  rq.a(p0=0, pn=16),
                                                  sp16.a(p0=0, pn=16), None,
                                                  ALU.mult),
                        reads=[rq.res, sp16.res], writes=[rq.res],
                        tag=f"ksp{b}_{qg}")
                p.V(lambda e: e.tensor_copy(rqb.a(p0=0, pn=16),
                                            rq.a(p0=0, pn=16)),
                    reads=[rq.res], writes=[rqb.res], tag=f"{prn}cb{b}_{qg}")
                for fc in range(NFC):
                    pban = psA if fc % 2 == 0 else psB
                    rban = rA if fc % 2 == 0 else rB
                    p.T(lambda e, fc=fc, pban=pban: e.matmul(
                            pban.ap()[:, 0:512],
                            pfull.a((fc * 128, 128), p0=0, pn=16),
                            rqb.a(p0=0, pn=16), start=True, stop=True),
                        reads=[pfull.res, rqb.res], writes=[rban],
                        tag=f"{prn}rg{b}_{qg}_{fc}")
                    p.V(lambda e, fc=fc, pban=pban, rban=rban, dst=dst, qg=qg:
                        e.tensor_tensor(dst.a(fc, (qg * 512, 512)),
                                        Rraw.a(fc), pban.ap()[:, 0:512],
                                        ALU.mult),
                        reads=[Rraw.res, rban], writes=[dst.res],
                        tag=f"{prn}hm{b}_{qg}_{fc}")
            AB.free(wqk)
        AB.free(sqb, rqb)
        AF32.free(Rraw, rq)
        AB.free(Hseq)
        if upto == "c":
            if b == 0:
                AB.free(Qhat, Khat)
            continue
        wo = load_w("wo", "woT", f"d{b}")
        Vaug = AB.alloc("Vaug", [H, NCH, 65])
        Esb = [AB.alloc(f"Esb{i}", [2048]) for i in range(2)]
        attnfm = AB.alloc("attnfm", [NFC, 512])
        r64 = [AB.alloc(f"r64{i}", [512]) for i in range(2)]
        rden = AF32.alloc("rden", [512])
        rden2 = AF32.alloc("rden2", [512])
        r64b = [AB.alloc(f"r64b{i}", [512]) for i in range(2)]
        xres = [AF32.alloc(f"xres{i}", [1024]) for i in range(2)]
        ofin = [AF32.alloc(f"ofin{i}", [1024]) for i in range(2)]
        p.V(lambda e: e.memset(Vaug.a(None, None, (64, 1)), 1.0),
            writes=[Vaug.res], tag=f"vaug1_{b}")
        for c in range(NCH):
            p.load(Vaug.a(None, c, (0, 64)), sVt[b, c],
                   reads=[rV[b][c]], writes=[Vaug.res], tag=f"VL{b}_{c}")
        for qg in range(4):
            for hp in range(NFC):
                h0, h1 = 2 * hp, 2 * hp + 1
                cfg = [(h0, 0, psW, (rBig[0], rBig[1]), psA, rA, Esb[0]),
                       (h1, 64, psX, (rBig[2], rBig[3]), psB, rB, Esb[1])]

                def sc_pair(g):
                    for j in range(2):
                        kc = g * 2 + j
                        def fn(e, kc=kc, j=j, qg=qg, hp=hp):
                            e.matmul(
                                psW.ap()[:, j * 512:(j + 1) * 512],
                                Khat.a(hp, (kc * 128, 128), p0=0, pn=64),
                                Qhat.a(hp, (qg * 512, 512), p0=0, pn=64),
                                start=True, stop=True)
                            return e.matmul(
                                psX.ap()[:, j * 512:(j + 1) * 512],
                                Khat.a(hp, (kc * 128, 128), p0=64, pn=64),
                                Qhat.a(hp, (qg * 512, 512), p0=64, pn=64),
                                start=True, stop=True)
                        p.T(fn, reads=[Khat.res, Qhat.res],
                            writes=[rBig[j], rBig[2 + j]],
                            tag=f"sc{b}_{qg}_{hp}_{g}_{j}")

                def sc_group(ci, g):
                    if ci == 0:
                        sc_pair(g)

                def exp_group(ci, g):
                    h, off, pt, prs, _, _, eb = cfg[ci]
                    p.S(lambda e, eb=eb, pt=pt: e.activation(
                            eb.a((0, 1024)), pt.ap(), AF.Exp),
                        reads=[prs[0], prs[1]], writes=[eb.res],
                        tag=f"ex{b}_{qg}_{h}_{g}")

                def ev_group(ci, g):
                    h, off, pt, prs, pacc, racc, eb = cfg[ci]
                    for j in range(2):
                        kc = g * 2 + j
                        p.T(lambda e, h=h, kc=kc, j=j, eb=eb, pacc=pacc:
                            e.matmul(
                                pacc.ap()[0:65, 0:512], Vaug.a(h, kc),
                                eb.a((j * 512, 512)),
                                start=(kc == 0), stop=(kc == 15)),
                            reads=[Vaug.res, eb.res], writes=[racc],
                            tag=f"ev{b}_{qg}_{h}_{kc}")

                for g in range(8):
                    for ci in (0, 1):
                        sc_group(ci, g)
                        exp_group(ci, g)
                        ev_group(ci, g)
                for ci in (0, 1):
                    h, off, pt, prs, pacc, racc, eb = cfg[ci]
                    rd = rden if ci == 0 else rden2
                    rr0 = r64[0] if ci == 0 else r64b[0]
                    rr1 = r64[1] if ci == 0 else r64b[1]
                    p.V(lambda e, pacc=pacc, rd=rd: e.tensor_copy(
                            rd.a((0, 512), p0=0, pn=1),
                            pacc.ap()[64:65, 0:512]),
                        reads=[racc], writes=[rd.res], tag=f"rcp{b}_{qg}_{h}")
                    p.S(lambda e, rd=rd: e.activation(
                            rd.a((0, 512), p0=0, pn=1),
                            rd.a((0, 512), p0=0, pn=1), AF.Ln),
                        reads=[rd.res], writes=[rd.res], tag=f"rc{b}_{qg}_{h}")
                    p.S(lambda e, rd=rd, rr0=rr0: e.activation(
                            rr0.a(p0=0, pn=1), rd.a((0, 512), p0=0, pn=1),
                            AF.Exp, scale=-1.0),
                        reads=[rd.res], writes=[rr0.res], tag=f"rcb{b}_{qg}_{h}")
                    p.T(lambda e, pt=pt, rr0=rr0: e.matmul(
                            pt.ap()[0:64, 0:512], onesb.a((0, 64), p0=0, pn=1),
                            rr0.a(p0=0, pn=1), start=True, stop=True),
                        reads=[onesb.res, rr0.res], writes=[prs[0]],
                        tag=f"rbc{b}_{qg}_{h}")
                    p.S(lambda e, pt=pt, rr1=rr1: e.activation(
                            rr1.a(p0=0, pn=64), pt.ap()[0:64, 0:512], AF.Copy),
                        reads=[prs[0]], writes=[rr1.res], tag=f"rev{b}_{qg}_{h}")
                    p.V(lambda e, h=h, off=off, pacc=pacc, rr1=rr1, hp=hp:
                        e.tensor_tensor(
                            attnfm.a(hp, p0=off, pn=64),
                            pacc.ap()[0:64, 0:512], rr1.a(p0=0, pn=64),
                            ALU.mult),
                        reads=[racc, rr1.res], writes=[attnfm.res],
                        tag=f"at{b}_{qg}_{h}")
            for t4 in range(4):
                ti = b * NCH + qg * 4 + t4
                trow = (qg * 4 + t4) * 128
                xr = xres[t4 % 2]
                p.load(xr.a(), d["x"][b, trow:trow + 128, :], writes=[xr.res],
                       tag=f"xl{b}_{qg}_{t4}")
                for nh in range(2):
                    for kc in range(NFC):
                        p.T(lambda e, kc=kc, nh=nh, t4=t4: e.matmul(
                                psX.ap()[:, nh * 512:(nh + 1) * 512],
                                attnfm.a(kc, (t4 * 128, 128)),
                                wo.a(kc, (nh * 512, 512)),
                                start=(kc == 0), stop=False),
                            reads=[attnfm.res, wo.res],
                            writes=[rBig[2 + nh]],
                            tag=f"wo{b}_{qg}_{t4}_{kc}_{nh}")
                    p.T(lambda e, nh=nh: e.matmul(
                            psX.ap()[:, nh * 512:(nh + 1) * 512],
                            onesb.a((0, 128), p0=0, pn=1),
                            bias_b["bo"].a((nh * 512, 512), p0=0, pn=1),
                            start=False, stop=True),
                        reads=[onesb.res, bias_b["bo"].res],
                        writes=[rBig[2 + nh]], tag=f"wob{b}_{qg}_{t4}_{nh}")
                of = ofin[t4 % 2]
                p.V(lambda e, xr=xr, of=of: e.tensor_tensor(
                        of.a(), psX.ap(), xr.a(), ALU.add),
                    reads=[rBig[2], rBig[3], xr.res], writes=[of.res],
                    tag=f"oadd{b}_{qg}_{t4}")
                p.store(soutp[ti], of.a(), reads=[of.res],
                        writes=[routp[ti]], tag=f"ost{b}_{qg}_{t4}")
        AB.free(Vaug, Esb[0], Esb[1], attnfm, r64[0], r64[1], r64b[0],
                r64b[1], Qhat, Khat, wo)
        AF32.free(rden, rden2, xres[0], xres[1], ofin[0], ofin[1])
    AF32.free(Mf, lld, nl, ek2, eend, lTr, lTsb, ftc[0], ftc[1])
    if upto in ("b", "c", "d"):
        return
    _phase_a2(nc, d, out_t, p, AF32, AB, PS, SC, env)


def _phase_a2(nc, d, out_t, p, AF32, AB, PS, SC, env):
    psW, psX, psA, psB = PS["psW"], PS["psX"], PS["psA"], PS["psB"]
    psTa, psTb = PS["psTa"], PS["psTb"]
    rBig, rA, rB, rTa, rTb = PS["rBig"], PS["rA"], PS["rB"], PS["rTa"], PS["rTb"]
    bigap = PS["bigap"]
    sxq, shq, soutp = SC["sxq"], SC["shq"], SC["soutp"]
    rxq, rhq, routp = SC["rxq"], SC["rhq"], SC["routp"]
    identb = env["identb"]
    mk_qscr, free_qscr = env["mk_qscr"], env["free_qscr"]

    IG = [(0, 512), (512, 512), (1024, 512), (1536, 512), (2048, 512),
          (2560, 256)]
    AF32.free(env["logg"])

    wqg = AB.alloc("wqg", [NFC, FFI])
    wqu = AB.alloc("wqu", [NFC, FFI])
    xq = [AB.alloc(f"a2xq{i}", [NFC, 128]) for i in range(2)]
    hqr = [AB.alloc(f"hq{i}", [FFI]) for i in range(2)]
    qtile = hqr[0]
    hqs = [AB.alloc(f"hqs{i}", [4, 128]) for i in range(2)]
    qs2 = [mk_qscr(1408, "qw_"), mk_qscr(1408, "qx_")]
    qs = qs2[0]
    ftw = [AF32.alloc(f"ftw{i}", [1024]) for i in range(2)]

    def prep_w_rows(dsrc, wdst, nrt, rowlen, tag):
        for rt in range(nrt):
            ft = ftw[rt % 2]
            p.load(ft.a((0, rowlen)), dsrc[rt * 128:(rt + 1) * 128, :],
                   writes=[ft.res], tag=f"{tag}l{rt}")
            for qi in range((rowlen + 1407) // 1408):
                off = qi * 1408
                ln = min(1408, rowlen - off)
                qv = qs2[rt % 2]
                qv["xres"] = ft.res
                qv["outres"] = qtile.res
                _quant_emit(p, nc, None, ft.a((off, ln)),
                            qtile.a((off, ln)), ln // 16, qv,
                            tag=f"{tag}q{rt}_{qi}_")
            nch = rowlen // 128
            for ch in range(nch):
                pt = psTa if ch % 2 == 0 else psTb
                rt_ = rTa if ch % 2 == 0 else rTb
                p.T(lambda e, ch=ch, pt=pt: e.transpose(
                        pt.ap()[:, 0:128], qtile.a((ch * 128, 128)), identb.a()),
                    reads=[qtile.res, identb.res], writes=[rt_],
                    tag=f"{tag}t{rt}_{ch}")
                p.S(lambda e, ch=ch, rt=rt, pt=pt, wdst=wdst: e.copy(
                        wdst.a(ch, (rt * 128, 128)), pt.ap()[:, 0:128]),
                    reads=[rt_], writes=[wdst.res], tag=f"{tag}e{rt}_{ch}")
    prep_w_rows(d["wgate"], wqg, NIC, 1024, "wg_")
    prep_w_rows(d["wup"], wqu, NIC, 1024, "wu_")
    AF32.free(ftw[0], ftw[1])

    hbr = [AF32.alloc(f"hbuf{i}", [FFI]) for i in range(2)]
    sgr = [AF32.alloc("sg0", [512])]
    qh = qs
    for ti in range(NT128):
        xt = xq[ti % 2]
        hb, sg, hq = hbr[ti % 2], sgr[0], hqr[ti % 2]
        p.load(xt.a(), sxq[ti], reads=[rxq[ti]], writes=[xt.res],
               tag=f"a2x{ti}")
        for gi, (i0, ilen) in enumerate(IG):
            for kc in range(NFC):
                p.T(lambda e, kc=kc, i0=i0, ilen=ilen, xt=xt: e.matmul(
                        bigap(0)[:, 0:ilen], xt.a(kc), wqg.a(kc, (i0, ilen)),
                        start=(kc == 0), stop=(kc == NFC - 1)),
                    reads=[xt.res, wqg.res], writes=[rBig[0]],
                    tag=f"gt{ti}_{gi}_{kc}")
                p.T(lambda e, kc=kc, i0=i0, ilen=ilen, xt=xt: e.matmul(
                        bigap(1)[:, 0:ilen], xt.a(kc), wqu.a(kc, (i0, ilen)),
                        start=(kc == 0), stop=(kc == NFC - 1)),
                    reads=[xt.res, wqu.res], writes=[rBig[1]],
                    tag=f"up{ti}_{gi}_{kc}")
            p.S(lambda e, ilen=ilen, sg=sg: e.activation(sg.a((0, ilen)),
                                                  bigap(0)[:, 0:ilen], AF.Silu),
                reads=[rBig[0]], writes=[sg.res], tag=f"si{ti}_{gi}")
            p.V(lambda e, i0=i0, ilen=ilen, hb=hb, sg=sg: e.tensor_tensor(
                    hb.a((i0, ilen)), sg.a((0, ilen)), bigap(1)[:, 0:ilen],
                    ALU.mult),
                reads=[sg.res, rBig[1]], writes=[hb.res], tag=f"hm{ti}_{gi}")
        for qi in range(2):
            off, ln = qi * 1408, 1408
            qv = qs2[qi]
            qv["xres"] = hb.res
            qv["outres"] = hq.res
            _quant_emit(p, nc, None, hb.a((off, ln)), hq.a((off, ln)),
                        ln // 16, qv, tag=f"qh{ti}_{qi}_")
        ch = 0
        grp = 0
        while ch < NIC:
            n4 = min(4, NIC - ch)
            hs = hqs[grp % 2]
            for q2 in range(n4):
                pt = psTa if q2 % 2 == 0 else psTb
                rt_ = rTa if q2 % 2 == 0 else rTb
                p.T(lambda e, ch=ch, q2=q2, pt=pt, hq=hq: e.transpose(
                        pt.ap()[:, 0:128], hq.a(((ch + q2) * 128, 128)),
                        identb.a()),
                    reads=[hq.res, identb.res], writes=[rt_],
                    tag=f"ht{ti}_{ch + q2}")
                p.S(lambda e, q2=q2, pt=pt, hs=hs: e.copy(
                        hs.a(q2), pt.ap()[:, 0:128]),
                    reads=[rt_], writes=[hs.res], tag=f"he{ti}_{ch + q2}")
            p.store(shq[ti][:, ch:ch + n4, :], hs.a((0, n4)), reads=[hs.res],
                    writes=[rhq[ti]], tag=f"hs{ti}_{grp}")
            ch += n4
            grp += 1
    AB.free(wqg, wqu, xq[0], xq[1], hqr[0], hqr[1], hqs[0], hqs[1])
    AF32.free(hbr[0], hbr[1], sgr[0])

    wqd = AB.alloc("wqd", [NIC, 1024])
    ftd = AF32.alloc("ftd", [FFI])
    for rt in range(NFC):
        p.load(ftd.a(), d["wdown"][rt * 128:(rt + 1) * 128, :],
               writes=[ftd.res], tag=f"wd_l{rt}")
        qdt2 = AB.alloc(f"qdt2_{rt}", [FFI]) if rt == 0 else env["qdt2"]
        env["qdt2"] = qdt2
        for qi in range(2):
            off, ln = qi * 1408, 1408
            qv = qs2[qi]
            qv["xres"] = ftd.res
            qv["outres"] = qdt2.res
            _quant_emit(p, nc, None, ftd.a((off, ln)), qdt2.a((off, ln)),
                        ln // 16, qv, tag=f"wd_q{rt}_{qi}_")
        for ch in range(NIC):
            pt = psTa if ch % 2 == 0 else psTb
            rt_ = rTa if ch % 2 == 0 else rTb
            p.T(lambda e, ch=ch, pt=pt, qdt2=qdt2: e.transpose(
                    pt.ap()[:, 0:128], qdt2.a((ch * 128, 128)), identb.a()),
                reads=[qdt2.res, identb.res], writes=[rt_],
                tag=f"wd_t{rt}_{ch}")
            p.S(lambda e, ch=ch, rt=rt, pt=pt: e.copy(
                    wqd.a(ch, (rt * 128, 128)), pt.ap()[:, 0:128]),
                reads=[rt_], writes=[wqd.res], tag=f"wd_e{rt}_{ch}")
    free_qscr(qs2[0])
    free_qscr(qs2[1])
    AF32.free(ftd)

    hqt = [AB.alloc(f"hqt{i}", [NIC, 128]) for i in range(2)]
    opt = [AF32.alloc(f"opt{i}", [1024]) for i in range(2)]
    fin = [AF32.alloc(f"fin{i}", [1024]) for i in range(2)]
    for ti in range(NT128):
        b, trow = ti // NCH, (ti % NCH) * 128
        ht = hqt[ti % 2]
        ot = opt[ti % 2]
        p.load(ht.a(), shq[ti], reads=[rhq[ti]], writes=[ht.res],
               tag=f"p2h{ti}")
        p.load(ot.a(), soutp[ti], reads=[routp[ti]], writes=[ot.res],
               tag=f"p2o{ti}")
        for nh in range(2):
            for ic in range(NIC):
                p.T(lambda e, ic=ic, nh=nh, ht=ht: e.matmul(
                        bigap(nh), ht.a(ic), wqd.a(ic, (nh * 512, 512)),
                        start=(ic == 0), stop=(ic == NIC - 1)),
                    reads=[ht.res, wqd.res], writes=[rBig[nh]],
                    tag=f"dn{ti}_{ic}_{nh}")
        fi = fin[ti % 2]
        p.V(lambda e, ot=ot, fi=fi: e.tensor_tensor(
                fi.a(), psW.ap(), ot.a(), ALU.add),
            reads=[rBig[0], rBig[1], ot.res], writes=[fi.res],
            tag=f"fadd{ti}")
        p.store(out_t[b, trow:trow + 128, :], fi.a(), reads=[fi.res],
                tag=f"out{ti}")


_NC_CACHE = {}


def _get_nc():
    if "nc" not in _NC_CACHE:
        nc, stack = build_program(upto="all")
        _NC_CACHE["nc"] = nc
        _NC_CACHE["stack"] = stack
    return _NC_CACHE["nc"]


def _consts():
    i128 = np.eye(128, dtype=np.float32)
    k = np.arange(128)
    import ml_dtypes
    BF = ml_dtypes.bfloat16
    pf = np.zeros((16, 1024), np.float32)
    for h in range(16):
        pf[h, h * 64:(h + 1) * 64] = 1.0
    pfT = np.zeros((128, NFC * 16), np.float32)
    for fc in range(8):
        for h2 in range(2):
            pfT[h2 * 64:(h2 + 1) * 64, fc * 16 + 2 * fc + h2] = 1.0
    return {
        "identb": i128.astype(BF), "identf": i128,
        "triLneg": -((k[:, None] <= k[None, :]).astype(np.float32)),
        "triUneg": -((k[:, None] > k[None, :]).astype(np.float32)),
        "maskLowNeg": ((k[:, None] > k[None, :]).astype(np.float32)
                       * np.float32(-1e30)).astype(BF),
        "pfull": pf.astype(BF), "pfullT": pfT.astype(BF),
        "onesb": np.ones((1, 512), BF), "onesf": np.ones((1, 128), np.float32),
    }


def kernel(**inputs):
    nc = _get_nc()
    consts = _consts()
    xfull = np.ascontiguousarray(np.asarray(inputs["L0_attn_x"], np.float32))
    shared = {
        "wsqT": np.ascontiguousarray(inputs["L0_attn_Wsq"].T),
        "wskT": np.ascontiguousarray(inputs["L0_attn_Wsk"].T),
        "wsvT": np.ascontiguousarray(inputs["L0_attn_Wsv"].T),
        "wvT": np.ascontiguousarray(inputs["L0_attn_Wv"].T),
        "wqT": np.ascontiguousarray(inputs["L0_attn_Wq"].T),
        "wkT": np.ascontiguousarray(inputs["L0_attn_Wk"].T),
        "woT": np.ascontiguousarray(inputs["L0_attn_Wo"].T),
        "wgT": np.ascontiguousarray(inputs["L0_attn_Wg"].T),
        "bsq": np.ascontiguousarray(inputs["L0_attn_bsq"]).reshape(1, -1),
        "bsk": np.ascontiguousarray(inputs["L0_attn_bsk"]).reshape(1, -1),
        "bsv": np.ascontiguousarray(inputs["L0_attn_bsv"]).reshape(1, -1),
        "bq": np.ascontiguousarray(inputs["L0_attn_bq"]).reshape(1, -1),
        "bk": np.ascontiguousarray(inputs["L0_attn_bk"]).reshape(1, -1),
        "bv": np.ascontiguousarray(inputs["L0_attn_bv"]).reshape(1, -1),
        "bo": np.ascontiguousarray(inputs["L0_attn_bo"]).reshape(1, -1),
        "bg": np.ascontiguousarray(inputs["L0_attn_bg"]).reshape(1, -1),
        "temp16": np.ascontiguousarray(
            np.asarray(inputs["L0_attn_temp"], np.float32).reshape(16, 1)),
        "wgate": np.ascontiguousarray(inputs["L0_ffn_w_gate"]),
        "wup": np.ascontiguousarray(inputs["L0_ffn_w_up"]),
        "wdown": np.ascontiguousarray(inputs["L0_ffn_w_down"]),
    }
    shared = {k: np.ascontiguousarray(np.asarray(v, np.float32))
              for k, v in shared.items()}
    shared.update(consts)
    in_maps = []
    for core in range(NCORES):
        m = dict(shared)
        m["x"] = np.ascontiguousarray(xfull[core * NB:(core + 1) * NB])
        in_maps.append(m)
    res = run_bass_kernel_spmd(nc, in_maps, core_ids=list(range(NCORES)))
    out = np.concatenate([r["out"] for r in res.results], axis=0)
    return out.astype(np.float32)
